# version 06
# baseline (speedup 1.0000x reference)
import contextlib
import math

import numpy as np
import ml_dtypes

import concourse.bass as bass
import concourse.mybir as mybir
from concourse.bass_utils import run_bass_kernel_spmd

F32 = mybir.dt.float32
_DBG_VV = False
BF16 = mybir.dt.bfloat16
I32 = mybir.dt.int32
BF = ml_dtypes.bfloat16
AF = mybir.ActivationFunctionType
ALU = mybir.AluOpType

NCORES = 8
DM = 1024
H = 128
NCH = DM // 128
TC = 2048
TB = 512
NB = TC // TB
SEQ = NCORES * TC
NTILE = SEQ // 128
SW = 1024
NSW = TC // SW
KG = 512

_TPI = 2 * math.pi
TWO_PI_HI = float(np.float32(_TPI))
TWO_PI_MID = float(np.float32(_TPI - TWO_PI_HI))
TWO_PI_LO = float(_TPI - TWO_PI_HI - TWO_PI_MID)
PI = float(np.float32(math.pi))
HALF_PI = float(np.float32(math.pi / 2))

AX_WAV, AX_PHASE, AX_BQ, AX_BK = 0, 1, 2, 3
AX_BF, AX_BI, AX_BC, AX_BO_F, AX_INC, AX_BO_A = 4, 12, 20, 28, 36, 44
AX_ISQ, AX_EPS, AX_ZERO = 52, 53, 54
NAUXF = 56

SM_GBF, SM_GBI, SM_GBC, SM_INVW = 0, 8, 16, 24
NSMALL = 25
SM_QB, SM_KB = 25, 26


class Prog:

    def __init__(self):
        self.p = {k: [] for k in ("sync", "gpsimd", "vector", "scalar", "tensor")}
        self.ctr = {}

    def emit(self, eng, fn):
        self.p[eng].append(fn)


def _mk(prog, sems):
    def w_emit(e, wait):
        if wait:
            for (cn, wv) in wait:
                e.wait_ge(sems[cn], wv)

    def dma(queue, cn, out_ap, in_ap, wait=None, qn=None):
        prog.ctr[cn] = prog.ctr.get(cn, 0) + 16
        val = prog.ctr[cn]

        def f(e, out_ap=out_ap, in_ap=in_ap, wait=wait):
            w_emit(e, wait)
            e.dma_start(out=out_ap, in_=in_ap).then_inc(sems[cn], 16)

        prog.emit(queue, f)
        return val

    def op(eng, fn, tick=None, wait=None):
        val = None
        if tick is not None:
            prog.ctr[tick] = prog.ctr.get(tick, 0) + 1
            val = prog.ctr[tick]

        def f(e, fn=fn, wait=wait, tick=tick):
            w_emit(e, wait)
            ins = fn(e)
            if tick is not None:
                ins.then_inc(sems[tick], 1)

        prog.emit(eng, f)
        return val

    return dma, op


def _ln_emit(op, E, xsrc, wait_first_cast):
    xb, x2b, yb = E["xb"], E["x2b"], E["yb"]
    stats, bc = E["stats_ps"], E["bc_ps"]
    srow, mrow, zrow = E["srow"], E["mrow"], E["zrow"]
    m_bc, z_bc = E["m_bc"], E["z_bc"]
    onesb, onesrow, auxf = E["onesb"], E["onesrow"], E["auxf"]

    cast_t = []
    for ch in range(NCH):
        w = wait_first_cast if ch == 0 else None
        op("vector", lambda e, ch=ch: e.tensor_copy(out=xb[ch][:, :], in_=xsrc[ch]),
           wait=w)
        t = op("vector", lambda e, ch=ch: e.tensor_mul(
            out=x2b[ch][:, :], in0=xb[ch][:, :], in1=xb[ch][:, :]), tick="v")
        cast_t.append(t)
    for ch in range(NCH):
        op("tensor", lambda e, ch=ch: e.matmul(
            out=stats[0:1, 0:TB], lhsT=onesb[:, 0:1], rhs=xb[ch][:, :],
            start=(ch == 0), stop=(ch == NCH - 1)),
           wait=[("v", cast_t[ch])])
        last = (ch == NCH - 1)
        t_stats = op("tensor", lambda e, ch=ch, last=last: e.matmul(
            out=stats[0:1, TB:2 * TB], lhsT=onesb[:, 0:1], rhs=x2b[ch][:, :],
            start=(ch == 0), stop=last),
            tick="t" if last else None)
    op("vector", lambda e: e.tensor_scalar(
        out=mrow[0:1, :], in0=stats[0:1, 0:TB], scalar1=1.0 / DM, scalar2=None,
        op0=ALU.mult), wait=[("t", t_stats)])
    op("vector", lambda e: e.tensor_scalar(
        out=srow[0:1, 0:TB], in0=stats[0:1, TB:2 * TB], scalar1=1.0 / DM,
        scalar2=None, op0=ALU.mult))
    op("vector", lambda e: e.tensor_mul(
        out=srow[0:1, TB:2 * TB], in0=mrow[0:1, :], in1=mrow[0:1, :]))
    tv = op("vector", lambda e: e.tensor_sub(
        out=srow[0:1, 0:TB], in0=srow[0:1, 0:TB], in1=srow[0:1, TB:2 * TB]),
        tick="v")
    ts0 = op("scalar", lambda e: e.activation(
        out=srow[0:1, TB:2 * TB], in_=srow[0:1, 0:TB], func=AF.Sqrt,
        bias=auxf[0:1, AX_EPS:AX_EPS + 1]), tick="s", wait=[("v", tv)])
    op("vector", lambda e: e.tensor_scalar(
        out=srow[0:1, 2 * TB:3 * TB], in0=srow[0:1, 0:TB], scalar1=1e-5,
        scalar2=None, op0=ALU.add), wait=[("s", ts0)])
    op("vector", lambda e: e.reciprocal(out=zrow[0:1, :], in_=srow[0:1, TB:2 * TB]))
    op("vector", lambda e: e.tensor_copy(out=m_bc[0:1, :], in_=mrow[0:1, :]))
    op("vector", lambda e: e.tensor_copy(out=z_bc[0:1, :], in_=mrow[0:1, :]))
    op("vector", lambda e: e.tensor_mul(
        out=srow[0:1, 0:TB], in0=zrow[0:1, :], in1=zrow[0:1, :]))
    op("vector", lambda e: e.tensor_mul(
        out=srow[0:1, 0:TB], in0=srow[0:1, 0:TB], in1=srow[0:1, 2 * TB:3 * TB]))
    op("vector", lambda e: e.tensor_scalar(
        out=srow[0:1, 0:TB], in0=srow[0:1, 0:TB], scalar1=-0.5, scalar2=1.5,
        op0=ALU.mult, op1=ALU.add))
    tz = op("vector", lambda e: e.tensor_mul(
        out=zrow[0:1, :], in0=zrow[0:1, :], in1=srow[0:1, 0:TB]), tick="v")
    op("tensor", lambda e: e.matmul(
        out=bc[:, 0:TB], lhsT=onesrow[0:1, :], rhs=mrow[0:1, :],
        start=True, stop=True), wait=[("v", tz)])
    tb = op("tensor", lambda e: e.matmul(
        out=bc[:, TB:2 * TB], lhsT=onesrow[0:1, :], rhs=zrow[0:1, :],
        start=True, stop=True), tick="t")
    op("vector", lambda e: e.tensor_copy(out=m_bc[:, :], in_=bc[:, 0:TB]),
       wait=[("t", tb)])
    op("vector", lambda e: e.tensor_copy(out=z_bc[:, :], in_=bc[:, TB:2 * TB]))
    y_t = []
    for ch in range(NCH):
        op("vector", lambda e, ch=ch: e.tensor_sub(
            out=yb[ch][:, :], in0=xb[ch][:, :], in1=m_bc[:, :]))
        t = op("vector", lambda e, ch=ch: e.tensor_mul(
            out=yb[ch][:, :], in0=yb[ch][:, :], in1=z_bc[:, :]), tick="v")
        y_t.append(t)
    return y_t, t_stats


def _ffn_emit(op, E, y_ticks, xdst, extra_gate_wait):
    yb, cb = E["yb"], E["cb"]
    ftile, itile, gtile, gtmp = E["ftile"], E["itile"], E["gtile"], E["gtmp"]
    th32 = E["th32"]
    wf, wi, wc, wo, uo = E["wfT"], E["wiT"], E["wcT"], E["woT"], E["uoT"]
    auxf, smalls = E["auxf"], E["smalls"]
    gsets, hsets = E["gsets"], E["hsets"]

    act_t = {}
    c_t = {}
    for ob in range(8):
        pf, pi_, pg = gsets[ob % 2]
        if ob >= 2:
            op("tensor", lambda e: e.nop(), wait=[("s", act_t[ob - 2])])
        for ch in range(NCH):
            w = [("v", y_ticks[ch])]
            if ob == 0 and ch == 0 and extra_gate_wait:
                w = w + extra_gate_wait
            op("tensor", lambda e, ch=ch, ob=ob, pf=pf: e.matmul(
                out=pf, lhsT=wf[:, ch * DM + ob * 128:ch * DM + (ob + 1) * 128],
                rhs=yb[ch][:, :], start=(ch == 0), stop=(ch == NCH - 1)),
               wait=w if ob == 0 else (w if ch == 0 and ob == 0 else None))
        for ch in range(NCH):
            op("tensor", lambda e, ch=ch, ob=ob, pi_=pi_: e.matmul(
                out=pi_, lhsT=wi[:, ch * DM + ob * 128:ch * DM + (ob + 1) * 128],
                rhs=yb[ch][:, :], start=(ch == 0), stop=(ch == NCH - 1)))
        for ch in range(NCH):
            last = ch == NCH - 1
            tmm = op("tensor", lambda e, ch=ch, ob=ob, pg=pg, last=last: e.matmul(
                out=pg, lhsT=wc[:, ch * DM + ob * 128:ch * DM + (ob + 1) * 128],
                rhs=yb[ch][:, :], start=(ch == 0), stop=last),
                tick="t" if last else None)
        aw = [("t", tmm)]
        if ob >= 1:
            aw.append(("v", c_t[ob - 1]))
        if ob == 0 and E.get("act_extra_wait"):
            aw = aw + E.pop("act_extra_wait")
        op("scalar", lambda e, ob=ob, pf=pf: e.activation(
            out=ftile[:, :], in_=pf, func=AF.Sigmoid,
            bias=smalls[:, SM_GBF + ob:SM_GBF + ob + 1]), wait=aw)
        op("scalar", lambda e, ob=ob, pi_=pi_: e.activation(
            out=itile[:, :], in_=pi_, func=AF.Sigmoid,
            bias=smalls[:, SM_GBI + ob:SM_GBI + ob + 1]))
        ta = op("scalar", lambda e, ob=ob, pg=pg: e.activation(
            out=gtile[:, :], in_=pg, func=AF.Tanh,
            bias=smalls[:, SM_GBC + ob:SM_GBC + ob + 1]), tick="s")
        act_t[ob] = ta
        op("vector", lambda e, ob=ob: e.tensor_scalar(
            out=cb[ob][:, :], in0=ftile[:, :],
            scalar1=auxf[:, AX_INC + ob:AX_INC + ob + 1], scalar2=None,
            op0=ALU.mult), wait=[("s", ta)])
        op("vector", lambda e: e.tensor_mul(
            out=gtmp[:, :], in0=itile[:, :], in1=gtile[:, :]))
        c_t[ob] = op("vector", lambda e, ob=ob: e.tensor_add(
            out=cb[ob][:, :], in0=cb[ob][:, :], in1=gtmp[:, :]), tick="v")
        if ob == 1 and E.get("ffn_mid_hook"):
            E.pop("ffn_mid_hook")()

    h_t = {}
    x_t = {}
    for ob in range(8):
        ph = hsets[ob % 2]
        if ob >= 2:
            op("tensor", lambda e: e.nop(), wait=[("s", h_t[ob - 2])])
        for ch in range(NCH):
            w = None
            if ob == 0 and ch == 0:
                w = [("v", c_t[7])] + (E.get("h_weight_wait") or [])
            op("tensor", lambda e, ch=ch, ob=ob, ph=ph: e.matmul(
                out=ph, lhsT=wo[:, ch * DM + ob * 128:ch * DM + (ob + 1) * 128],
                rhs=yb[ch][:, :], start=(ch == 0), stop=False), wait=w)
        for ch in range(NCH):
            last = ch == NCH - 1
            tmm = op("tensor", lambda e, ch=ch, ob=ob, ph=ph, last=last: e.matmul(
                out=ph, lhsT=uo[:, ch * DM + ob * 128:ch * DM + (ob + 1) * 128],
                rhs=cb[ch][:, :], start=False, stop=last),
                tick="t" if last else None)
        hw = [("t", tmm)]
        if ob >= 2:
            hw.append(("v", x_t[ob - 2]))
        tth = op("scalar", lambda e, ob=ob, ph=ph: e.activation(
            out=th32[ob % 2][:, :], in_=ph, func=AF.Tanh,
            bias=auxf[:, AX_BO_F + ob:AX_BO_F + ob + 1]), tick="s", wait=hw)
        h_t[ob] = tth
        x_t[ob] = op("vector", lambda e, ob=ob: e.scalar_tensor_tensor(
            out=xdst[ob], in0=th32[ob % 2][:, :], scalar=0.5, in1=xdst[ob],
            op0=ALU.mult, op1=ALU.add), tick="v", wait=[("s", tth)])
    return x_t


def _trig_emit(op, E, src_ps_ap, bias_col, cos_out, sin_out, wait_src):
    th, red, kf, ki = E["trig_th"], E["trig_red"], E["trig_kf"], E["trig_ki"]
    smalls, auxf = E["smalls"], E["auxf"]
    w0 = list(wait_src)
    if E.get("trig_last_s"):
        w0.append(("s", E["trig_last_s"]))
    op("vector", lambda e: e.tensor_copy(out=th[:, :], in_=src_ps_ap), wait=w0)
    op("vector", lambda e: e.tensor_scalar(
        out=red[:, :], in0=th[:, :], scalar1=auxf[:, bias_col:bias_col + 1],
        scalar2=None, op0=ALU.add))
    op("vector", lambda e: e.tensor_scalar(
        out=th[:, :], in0=red[:, :], scalar1=smalls[:, SM_INVW:SM_INVW + 1],
        scalar2=auxf[:, AX_PHASE:AX_PHASE + 1], op0=ALU.mult, op1=ALU.add))
    op("vector", lambda e: e.tensor_scalar(
        out=red[:, :], in0=th[:, :], scalar1=-1.0 / _TPI, scalar2=None,
        op0=ALU.mult))
    op("vector", lambda e: e.tensor_copy(out=ki[:, :], in_=red[:, :]))
    op("vector", lambda e: e.tensor_copy(out=kf[:, :], in_=ki[:, :]))
    op("vector", lambda e: e.scalar_tensor_tensor(
        out=red[:, :], in0=kf[:, :], scalar=TWO_PI_HI, in1=th[:, :],
        op0=ALU.mult, op1=ALU.add))
    op("vector", lambda e: e.scalar_tensor_tensor(
        out=red[:, :], in0=kf[:, :], scalar=TWO_PI_MID, in1=red[:, :],
        op0=ALU.mult, op1=ALU.add))
    op("vector", lambda e: e.scalar_tensor_tensor(
        out=red[:, :], in0=kf[:, :], scalar=TWO_PI_LO, in1=red[:, :],
        op0=ALU.mult, op1=ALU.add))
    tsin = op("vector", lambda e: e.tensor_scalar(
        out=th[:, :], in0=red[:, :], scalar1=PI, scalar2=-PI,
        op0=ALU.min, op1=ALU.max), tick="v")
    tss = op("scalar", lambda e: e.activation(
        out=sin_out, in_=th[:, :], func=AF.Sin,
        bias=E["auxf"][:, AX_ZERO:AX_ZERO + 1]), tick="s", wait=[("v", tsin)])
    op("vector", lambda e: e.tensor_scalar(
        out=kf[:, :], in0=th[:, :], scalar1=HALF_PI, scalar2=None,
        op0=ALU.is_gt), wait=[("s", tss)])
    op("vector", lambda e: e.scalar_tensor_tensor(
        out=red[:, :], in0=kf[:, :], scalar=-TWO_PI_HI, in1=th[:, :],
        op0=ALU.mult, op1=ALU.add))
    op("vector", lambda e: e.tensor_scalar(
        out=kf[:, :], in0=red[:, :], scalar1=HALF_PI, scalar2=PI,
        op0=ALU.add, op1=ALU.min))
    tcos = op("vector", lambda e: e.tensor_scalar(
        out=red[:, :], in0=kf[:, :], scalar1=-PI, scalar2=None, op0=ALU.max),
        tick="v")
    tcs = op("scalar", lambda e: e.activation(
        out=cos_out, in_=red[:, :], func=AF.Sin,
        bias=E["auxf"][:, AX_ZERO:AX_ZERO + 1]), tick="s", wait=[("v", tcos)])
    E["trig_last_s"] = tcs
    return tcs


def _replay(nc, es, prog):
    block = es.enter_context(nc.Block())

    @block.sync
    def _(e):
        for f in prog.p["sync"]:
            f(e)

    @block.gpsimd
    def _(e):
        for f in prog.p["gpsimd"]:
            f(e)

    @block.vector
    def _(e):
        for f in prog.p["vector"]:
            f(e)

    @block.scalar
    def _(e):
        for f in prog.p["scalar"]:
            f(e)

    @block.tensor
    def _(e):
        for f in prog.p["tensor"]:
            f(e)


def _common_sbuf(nc, es):
    sb = lambda n, sh, dt: es.enter_context(nc.sbuf_tensor(n, sh, dt))
    E = {}
    for nm in ("wfT", "wiT", "wcT", "woT", "uoT"):
        E[nm] = sb(nm + "_s", [128, NCH * DM], BF16)
    E["xb"] = [sb(f"xb{c}", [128, TB], BF16) for c in range(NCH)]
    E["yb"] = [sb(f"y{c}", [128, TB], BF16) for c in range(NCH)]
    E["cb"] = [sb(f"c{c}", [128, TB], BF16) for c in range(NCH)]
    E["x2b"] = E["cb"]
    E["ftile"] = sb("ftile", [128, TB], BF16)
    E["itile"] = sb("itile", [128, TB], BF16)
    E["gtile"] = sb("gtile", [128, TB], BF16)
    E["gtmp"] = sb("gtmp", [128, TB], BF16)
    E["th32"] = [sb(f"th32_{i}", [128, TB], F32) for i in range(2)]
    E["m_bc"] = sb("m_bc", [128, TB], BF16)
    E["z_bc"] = sb("z_bc", [128, TB], BF16)
    E["srow"] = sb("srow", [1, 3 * TB], F32)
    E["mrow"] = sb("mrow", [1, TB], F32)
    E["zrow"] = sb("zrow", [1, TB], F32)
    E["auxf"] = sb("auxf_s", [128, NAUXF], F32)
    E["onesb"] = sb("auxb_s", [128, 2], BF16)
    E["onesrow"] = sb("onesrow_s", [1, 128], F32)
    return E, sb


def build_launch_a(n_blocks=NB, do_ffn=True, do_ln2=True, do_qkv=True, do_v=True):
    nc = bass.Bass("TRN2", target_bir_lowering=False,
                   detect_race_conditions=False)
    x_h = nc.dram_tensor("x", [DM, TC], F32, kind="ExternalInput").ap()
    w_h = {nm: nc.dram_tensor(nm, [DM, DM], F32, kind="ExternalInput").ap()
           for nm in ("wfT", "wiT", "wcT", "woT", "uoT", "ufT", "uiT", "ucT")}
    wq_h = {nm: nc.dram_tensor(nm, [DM, H], F32, kind="ExternalInput").ap()
            for nm in ("wqT", "wkT", "wvT")}
    auxf_h = nc.dram_tensor("auxf", [128, NAUXF], F32, kind="ExternalInput").ap()
    auxb_h = nc.dram_tensor("auxb", [128, 2], BF16, kind="ExternalInput").ap()
    onesrow_h = nc.dram_tensor("onesrow", [1, 128], F32, kind="ExternalInput").ap()
    bvbc_h = nc.dram_tensor("bvbc", [128, 128], F32, kind="ExternalInput").ap()
    x1_h = nc.dram_tensor("x1", [DM, TC], F32, kind="ExternalOutput").ap()
    cosq_h = nc.dram_tensor("cosq", [H, TC], BF16, kind="ExternalOutput").ap()
    sinq_h = nc.dram_tensor("sinq", [H, TC], BF16, kind="ExternalOutput").ap()
    cosk_h = nc.dram_tensor("cosk", [H, TC], BF16, kind="ExternalOutput").ap()
    sink_h = nc.dram_tensor("sink", [H, TC], BF16, kind="ExternalOutput").ap()
    vperm_h = nc.dram_tensor("vperm", [128, TC], BF16, kind="ExternalOutput").ap()
    smalls_h = nc.dram_tensor("smalls", [128, NSMALL], F32, kind="ExternalOutput").ap()
    if _DBG_VV:
        vvdump_h = nc.dram_tensor("vvdump", [128, 544], F32,
                                  kind="ExternalOutput").ap()

    es = contextlib.ExitStack()
    with es:
        E, sb = _common_sbuf(nc, es)
        ps_ = lambda n, sh: es.enter_context(nc.psum_tensor(n, sh, F32))
        E["smalls"] = sb("smalls_s", [128, 28], F32)
        wq_s = {nm: [sb(f"{nm}{c}", [128, H], BF16) for c in range(NCH)]
                for nm in ("wqT", "wkT", "wvT")}
        u_stage = [sb(f"ust{i}", [128, DM], F32) for i in range(2)]
        xblk = [[sb(f"x{i}_{c}", [128, TB], F32) for c in range(NCH)]
                for i in range(2)]
        bvbc = sb("bvbc_s", [128, 128], F32)

        E["trig_th"] = sb("trig_th", [128, TB], F32)
        E["trig_red"] = sb("trig_red", [128, TB], F32)
        E["trig_kf"] = sb("trig_kf", [128, TB], F32)
        E["trig_ki"] = sb("trig_ki", [128, TB], I32)
        cosq_s = sb("cosq_sb", [H, TC], BF16)
        sinq_s = sb("sinq_sb", [H, TC], BF16)
        cosk_s = sb("cosk_sb", [H, TC], BF16)
        sink_s = sb("sink_sb", [H, TC], BF16)
        vperm_s = sb("vperm_sb", [128, TC], BF16)

        stats_ps = ps_("stats", [128, 2 * TB])
        bc_ps = ps_("bcst", [128, 2 * TB])
        gA0 = ps_("gA0", [128, TB])
        gB0 = ps_("gB0", [128, TB])
        vv_ps = ps_("vv", [128, TB])
        E["stats_ps"] = stats_ps
        E["bc_ps"] = bc_ps
        E["gsets"] = ((gA0[:, :], stats_ps[:, 0:TB], stats_ps[:, TB:2 * TB]),
                      (gB0[:, :], bc_ps[:, 0:TB], bc_ps[:, TB:2 * TB]))
        E["hsets"] = (gA0[:, :], gB0[:, :])
        qk_ps = bc_ps
        v_ps = stats_ps

        sems = {k: es.enter_context(nc.semaphore(f"sem_{k}"))
                for k in ("lda", "lx0", "lx1", "lu0", "lu1", "lw1", "lw2",
                          "lw3", "st0", "st1", "stf", "v", "s", "t")}
        prog = Prog()
        dma, op = _mk(prog, sems)

        dma("sync", "lda", E["auxf"][:, :], auxf_h)
        dma("sync", "lda", E["onesb"][:, :], auxb_h)
        dma("sync", "lda", E["onesrow"][:, :], onesrow_h)
        dma("sync", "lda", bvbc[:, :], bvbc_h)
        AUX = [("lda", 64)]
        xld = {}
        for ch in range(NCH):
            xld[(0, ch)] = dma("sync", "lx0", xblk[0][ch][:, :],
                               x_h[ch * 128:(ch + 1) * 128, 0:TB])
        u_t = {}

        def emit_uchain():
            tick_last = {}
            for idx in range(24):
                mi, ic = divmod(idx, NCH)
                buf = u_stage[idx % 2]
                lus = f"lu{idx % 2}"
                w = [("t", tick_last[idx - 2])] if idx >= 2 else None
                lv = dma("sync", lus, buf[:, :],
                         w_h[("ufT", "uiT", "ucT")[mi]][ic * 128:(ic + 1) * 128, :],
                         wait=w)
                for oc in range(NCH):
                    last = oc == NCH - 1
                    col = ((mi * 8 + oc) * 8) + ic
                    tick_last[idx] = op(
                        "tensor", lambda e, buf=buf, oc=oc, ic=ic, col=col: e.matmul(
                            out=vv_ps[:, col:col + 1],
                            lhsT=buf[:, oc * 128:(oc + 1) * 128],
                            rhs=E["auxf"][:, AX_INC + ic:AX_INC + ic + 1],
                            start=True, stop=True, skip_group_check=True),
                        tick="t" if last else None,
                        wait=([(lus, lv)] + AUX) if oc == 0 else None)
            tt = E["trig_th"]
            op("vector", lambda e: e.tensor_copy(
                out=tt[:, 0:192], in_=vv_ps[:, 0:192]),
                wait=[("t", tick_last[23])])
            t192 = tt[:, 0:192].rearrange("p (c two) -> p c two", two=2)
            op("vector", lambda e: e.tensor_add(
                out=tt[:, 192:288], in0=t192[:, :, 0], in1=t192[:, :, 1]))
            t96 = tt[:, 192:288].rearrange("p (c two) -> p c two", two=2)
            op("vector", lambda e: e.tensor_copy(
                out=E["trig_red"][:, 64:120], in_=E["auxf"][:, 0:56]))
            op("vector", lambda e: e.tensor_add(
                out=tt[:, 288:336], in0=t96[:, :, 0], in1=t96[:, :, 1]))
            t48 = tt[:, 288:336].rearrange("p (c two) -> p c two", two=2)
            op("vector", lambda e: e.tensor_copy(
                out=E["trig_red"][:, 64:120], in_=E["auxf"][:, 0:56]))
            op("vector", lambda e: e.tensor_copy(
                out=E["trig_red"][:, 64:120], in_=E["auxf"][:, 0:56]))
            t_ev = op("vector", lambda e: e.tensor_add(
                out=E["smalls"][:, 0:24], in0=t48[:, :, 0], in1=t48[:, :, 1]),
                tick="v")
            op("vector", lambda e: e.tensor_copy(
                out=E["trig_red"][:, 64:120], in_=E["auxf"][:, 0:56]))
            op("vector", lambda e: e.tensor_copy(
                out=E["trig_red"][:, 64:120], in_=E["auxf"][:, 0:56]))
            for sm0, ax0 in ((SM_GBF, AX_BF), (SM_GBI, AX_BI), (SM_GBC, AX_BC)):
                op("vector", lambda e, sm0=sm0, ax0=ax0: e.tensor_add(
                    out=E["smalls"][:, sm0:sm0 + 8], in0=E["smalls"][:, sm0:sm0 + 8],
                    in1=E["auxf"][:, ax0:ax0 + 8]))
            op("vector", lambda e: e.tensor_scalar(
                out=E["trig_kf"][:, 0:8], in0=E["auxf"][:, AX_WAV:AX_WAV + 8],
                scalar1=1e-3, scalar2=1e-8, op0=ALU.max, op1=ALU.add))
            op("vector", lambda e: e.tensor_copy(
                out=E["trig_th"][:, 0:56], in_=E["auxf"][:, 0:56]))
            op("vector", lambda e: e.tensor_copy(
                out=E["trig_th"][:, 0:56], in_=E["auxf"][:, 0:56]))
            op("vector", lambda e: e.reciprocal(
                out=E["trig_red"][:, 0:8], in_=E["trig_kf"][:, 0:8]))
            op("vector", lambda e: e.tensor_copy(
                out=E["trig_th"][:, 0:56], in_=E["auxf"][:, 0:56]))
            op("vector", lambda e: e.tensor_copy(
                out=E["trig_th"][:, 0:56], in_=E["auxf"][:, 0:56]))
            t_smalls = op("vector", lambda e: e.tensor_copy(
                out=E["smalls"][:, SM_INVW:SM_INVW + 1], in_=E["trig_red"][:, 0:1]),
                tick="v")
            E["act_extra_wait"] = [("v", t_smalls)]
            E["t_smalls_v"] = t_smalls
            if _DBG_VV:
                dma("sync", "stf", vvdump_h[:, 192:528],
                    E["trig_th"][:, 0:336], wait=[("v", t_ev)])
                dma("sync", "stf", vvdump_h[:, 0:28],
                    E["smalls"][:, 0:28], wait=[("v", t_smalls)])
                dma("sync", "stf", vvdump_h[:, 28:84],
                    E["auxf"][:, 0:56], wait=[("v", t_smalls)])

        for nm in ("wfT", "wiT", "wcT"):
            dma("gpsimd", "lw1", E[nm][:, :].rearrange("p (c i) -> p c i", c=NCH),
                w_h[nm].rearrange("(c p) i -> p c i", p=128))
        for nm in ("woT", "uoT"):
            dma("gpsimd", "lw2", E[nm][:, :].rearrange("p (c i) -> p c i", c=NCH),
                w_h[nm].rearrange("(c p) i -> p c i", p=128))
        for nm in ("wqT", "wkT", "wvT"):
            for ch in range(NCH):
                dma("gpsimd", "lw3", wq_s[nm][ch][:, :],
                    wq_h[nm][ch * 128:(ch + 1) * 128, :])
        E["h_weight_wait"] = [("lw2", 2 * 16)]
        gate_w_wait = [("lw1", 3 * 16)]
        qk_w_wait = [("lw3", 24 * 16)]

        prev = {}
        for b in range(n_blocks):
            cur = xblk[b % 2]
            c0 = b * TB
            lxs = f"lx{b % 2}"
            sts = f"st{b % 2}"
            if b > 0:
                for ch in range(NCH):
                    w = None
                    if ch == 0 and b >= 2:
                        w = [("v", prev[(b - 2, "y2")]), (sts, prev[(b - 2, "xst")])]
                    xld[(b, ch)] = dma("sync", lxs, cur[ch][:, :],
                                       x_h[ch * 128:(ch + 1) * 128, c0:c0 + TB],
                                       wait=w)
            xsrc = [cur[ch][:, :] for ch in range(NCH)]
            wf = [(lxs, xld[(b, NCH - 1)])] + AUX
            if b > 0:
                wf.append(("t", prev[(b - 1, "stats2")]))
            y_t, _ = _ln_emit(op, E, xsrc, wf)
            if b == 0:
                emit_uchain()
            if not do_ffn:
                prev[(b, "stats2")] = prog.ctr.get("t", 0)
                prev[(b, "y2")] = prog.ctr.get("v", 0)
                prev[(b, "xst")] = prog.ctr.get(f"st{b % 2}", 0)
                prev[(b, "trig_v")] = prog.ctr.get("v", 0)
                prev[(b, "vev_last")] = prog.ctr.get("v", 0)
                for sub in range(TB // 128):
                    prev[(b, f"vev{sub}")] = prog.ctr.get("v", 0)
                continue
            x_t = _ffn_emit(op, E, y_t, xsrc, gate_w_wait if b == 0 else None)
            for ch in range(NCH):
                v = dma("gpsimd", sts,
                        x1_h[ch * 128:(ch + 1) * 128, c0:c0 + TB], cur[ch][:, :],
                        wait=[("v", x_t[ch])])
            prev[(b, "xst")] = v
            if not do_ln2:
                prev[(b, "stats2")] = prog.ctr.get("t", 0)
                prev[(b, "y2")] = prog.ctr.get("v", 0)
                prev[(b, "trig_v")] = prog.ctr.get("v", 0)
                prev[(b, "vev_last")] = prog.ctr.get("v", 0)
                for sub in range(TB // 128):
                    prev[(b, f"vev{sub}")] = prog.ctr.get("v", 0)
                continue
            wf2 = [("v", x_t[NCH - 1])]
            y2_t, t_stats2 = _ln_emit(op, E, xsrc, wf2)
            prev[(b, "stats2")] = t_stats2
            prev[(b, "y2")] = y2_t[NCH - 1]
            if not do_qkv:
                prev[(b, "trig_v")] = prog.ctr.get("v", 0)
                prev[(b, "vev_last")] = prog.ctr.get("v", 0)
                for sub in range(TB // 128):
                    prev[(b, f"vev{sub}")] = prog.ctr.get("v", 0)
                continue
            for (nm, pcols, cs, sn, bcol) in (
                    ("wqT", (0, TB), cosq_s, sinq_s, AX_BQ),
                    ("wkT", (TB, 2 * TB), cosk_s, sink_s, AX_BK)):
                for ch in range(NCH):
                    w = [("v", y2_t[ch])]
                    if ch == 0:
                        w += qk_w_wait
                        if b > 0:
                            w.append(("v", prev[(b - 1, "trig_v")]))
                    last = ch == NCH - 1
                    tq = op("tensor", lambda e, ch=ch, nm=nm, pcols=pcols,
                            last=last: e.matmul(
                        out=qk_ps[:, pcols[0]:pcols[1]], lhsT=wq_s[nm][ch][:, :],
                        rhs=E["yb"][ch][:, :], start=(ch == 0), stop=last),
                        tick="t" if last else None, wait=w)
                _trig_emit(op, E, qk_ps[:, pcols[0]:pcols[1]], bcol,
                           cs[:, c0:c0 + TB], sn[:, c0:c0 + TB],
                           [("t", tq), ("v", E["t_smalls_v"])])
            prev[(b, "trig_v")] = prog.ctr.get("v", 0)
            if not do_v:
                prev[(b, "vev_last")] = prog.ctr.get("v", 0)
                for sub in range(TB // 128):
                    prev[(b, f"vev{sub}")] = prog.ctr.get("v", 0)
                continue
            for sub in range(TB // 128):
                pc = (sub % 2) * 512
                tk0 = c0 + sub * 128
                for ch in range(NCH):
                    w = None
                    if ch == 0:
                        w = []
                        if sub >= 2:
                            w.append(("v", prev[(b, f"vev{sub-2}")]))
                        if sub == 0 and b > 0:
                            w.append(("v", prev[(b - 1, "vev_last")]))
                        w = w or None
                    last = ch == NCH - 1
                    tvm = op("tensor", lambda e, ch=ch, pc=pc, sub=sub,
                             last=last: e.matmul(
                        out=v_ps[:, pc:pc + 128],
                        lhsT=E["yb"][ch][:, sub * 128:(sub + 1) * 128],
                        rhs=wq_s["wvT"][ch][:, :], start=(ch == 0), stop=last),
                        tick="t" if last else None, wait=w)
                t = op("vector", lambda e, pc=pc, tk0=tk0: e.tensor_add(
                    out=vperm_s[:, tk0:tk0 + 128], in0=v_ps[:, pc:pc + 128],
                    in1=bvbc[:, :]), tick="v", wait=[("t", tvm)])
                prev[(b, f"vev{sub}")] = t
            prev[(b, "vev_last")] = t

        t_end_v = prog.ctr.get("v", 0)
        t_end_s = prog.ctr.get("s", 0)
        for hb, sbl in ((cosq_h, cosq_s), (sinq_h, sinq_s), (cosk_h, cosk_s),
                        (sink_h, sink_s), (vperm_h, vperm_s)):
            dma("gpsimd", "stf", hb, sbl[:, :],
                wait=[("v", t_end_v), ("s", t_end_s)])
        dma("gpsimd", "stf", smalls_h, E["smalls"][:, 0:NSMALL])

        def _fin(e):
            for cn in ("st0", "st1", "stf"):
                e.wait_ge(sems[cn], prog.ctr.get(cn, 0))
        prog.emit("gpsimd", _fin)

        _replay(nc, es, prog)
    return nc


def build_launch_b():
    nc = bass.Bass("TRN2", target_bir_lowering=False,
                   detect_race_conditions=False)
    x1_h = nc.dram_tensor("x1", [DM, TC], F32, kind="ExternalInput").ap()
    cosq_h = nc.dram_tensor("cosq", [H, TC], BF16, kind="ExternalInput").ap()
    sinq_h = nc.dram_tensor("sinq", [H, TC], BF16, kind="ExternalInput").ap()
    coskF_h = nc.dram_tensor("coskF", [H, SEQ], BF16, kind="ExternalInput").ap()
    sinkF_h = nc.dram_tensor("sinkF", [H, SEQ], BF16, kind="ExternalInput").ap()
    vpermF_h = nc.dram_tensor("vpermF", [128, SEQ], BF16, kind="ExternalInput").ap()
    w_h = {nm: nc.dram_tensor(nm, [DM, DM], F32, kind="ExternalInput").ap()
           for nm in ("wfT", "wiT", "wcT", "woT", "uoT")}
    waoT_h = nc.dram_tensor("waoT", [H, DM], F32, kind="ExternalInput").ap()
    auxf_h = nc.dram_tensor("auxf", [128, NAUXF], F32, kind="ExternalInput").ap()
    auxb_h = nc.dram_tensor("auxb", [128, 2], BF16, kind="ExternalInput").ap()
    onesrow_h = nc.dram_tensor("onesrow", [1, 128], F32, kind="ExternalInput").ap()
    ones512_h = nc.dram_tensor("ones512", [1, 512], F32, kind="ExternalInput").ap()
    boa_h = nc.dram_tensor("boaRow", [1, DM], F32, kind="ExternalInput").ap()
    smalls_h = nc.dram_tensor("smalls", [128, NSMALL], F32, kind="ExternalInput").ap()
    out_h = nc.dram_tensor("out", [DM, TC], F32, kind="ExternalOutput").ap()

    es = contextlib.ExitStack()
    with es:
        E, sb = _common_sbuf(nc, es)
        ps_ = lambda n, sh: es.enter_context(nc.psum_tensor(n, sh, F32))
        E["smalls"] = sb("smalls_s", [128, NSMALL], F32)
        waoT = sb("wao_s", [H, DM], BF16)
        cosq = sb("cosq_sb", [H, TC], BF16)
        sinq = sb("sinq_sb", [H, TC], BF16)
        kbuf = [dict(ck=sb(f"ck{i}", [H, KG], BF16), sk=sb(f"sk{i}", [H, KG], BF16),
                     vp=sb(f"vp{i}", [128, KG], BF16)) for i in range(2)]
        x1s = [[sb(f"x1_0_{c}", [128, SW], F32) for c in range(NCH)]]
        ptiles = [sb(f"ptile{i}", [128, SW], BF16) for i in range(3)]
        pairs = [sb(f"pair{i}", [128, SW], BF16) for i in range(2)]
        quad = sb("quad", [128, SW], BF16)
        ones512 = sb("ones512_s", [1, 512], F32)
        boa = sb("boa_s", [1, DM], F32)
        out_sb = sb("out_sb", [H, SW], F32)
        rproj = sb("rproj", [H, SW], BF16)
        drow = sb("drow", [1, SW], F32)
        drow2 = sb("drow2", [1, SW], F32)

        sA = ps_("sA", [128, SW])
        sB = ps_("sB", [128, SW])
        out_ps = ps_("out_ps", [128, SW])
        den_ps = ps_("den_ps", [128, SW])
        E["stats_ps"] = sA
        E["bc_ps"] = sB
        E["gsets"] = ((sA[:, 0:TB], sA[:, TB:2 * TB], sB[:, 0:TB]),
                      (sB[:, TB:2 * TB], out_ps[:, 0:TB], out_ps[:, TB:2 * TB]))
        E["hsets"] = (den_ps[:, 0:TB], den_ps[:, TB:2 * TB])

        sems = {k: es.enter_context(nc.semaphore(f"sem_{k}"))
                for k in ("lda", "lx0", "lk0", "lk1", "lw1", "lw2", "lw3",
                          "st0", "st1", "v", "s", "t")}
        prog = Prog()
        dma, op = _mk(prog, sems)

        dma("sync", "lda", E["auxf"][:, :], auxf_h)
        dma("sync", "lda", E["onesb"][:, :], auxb_h)
        dma("sync", "lda", E["onesrow"][:, :], onesrow_h)
        dma("sync", "lda", E["smalls"][:, :], smalls_h)
        dma("sync", "lda", cosq[:, :], cosq_h)
        dma("sync", "lda", sinq[:, :], sinq_h)
        dma("sync", "lda", ones512[:, :], ones512_h)
        dma("sync", "lda", boa[:, :], boa_h)
        AUX = [("lda", 128)]
        x1ld = {}
        for ch in range(NCH):
            x1ld[(0, ch)] = dma("sync", "lx0", x1s[0][ch][:, :],
                                x1_h[ch * 128:(ch + 1) * 128, 0:SW])
        for nm in ("wfT", "wiT", "wcT"):
            dma("gpsimd", "lw1", E[nm][:, :].rearrange("p (c i) -> p c i", c=NCH),
                w_h[nm].rearrange("(c p) i -> p c i", p=128))
        for nm in ("woT", "uoT"):
            dma("gpsimd", "lw2", E[nm][:, :].rearrange("p (c i) -> p c i", c=NCH),
                w_h[nm].rearrange("(c p) i -> p c i", p=128))
        dma("gpsimd", "lw3", waoT[:, :], waoT_h)
        E["h_weight_wait"] = [("lw2", 2 * 16)]
        gate_w_wait = [("lw1", 3 * 16)] + AUX

        NG = NTILE // 4
        kc = {}
        prev = {}
        E["trig_th"] = None
        for sw in range(NSW):
            cur = x1s[0]
            q0 = sw * SW
            if sw > 0:
                for ch in range(NCH):
                    w = None
                    if ch == 0:
                        w = [(f"st{(sw - 1) % 2}", prev[(sw - 1, "ost")])]
                    x1ld[(sw, ch)] = dma("sync", "lx0", cur[ch][:, :],
                                         x1_h[ch * 128:(ch + 1) * 128, q0:q0 + SW],
                                         wait=w)
            xp_t = {}
            for ch in range(NCH):
                xp_t[ch] = prog.ctr.get("v", 0)
            t_x1rdy = [("lx0", x1ld[(sw, NCH - 1)])] + AUX
            sweep_bar = [("v", prog.ctr.get("v", 0)), ("s", prog.ctr.get("s", 0))]
            exp_t = {}
            den_t = {}
            tile_meta = {}

            def emit_kload(g, wait):
                buf = kbuf[g % 2]
                gg = sw * NG + g
                k0 = g * KG
                lks = f"lk{gg % 2}"
                dma("sync", lks, buf["ck"][:, :], coskF_h[:, k0:k0 + KG], wait=wait)
                dma("sync", lks, buf["sk"][:, :], sinkF_h[:, k0:k0 + KG])
                lvp = dma("sync", lks, buf["vp"][:, :], vpermF_h[:, k0:k0 + KG])
                for ti in range(4):
                    tile_meta[g * 4 + ti] = (g * 4 + ti, buf, lks, lvp, ti)

            emit_kload(0, None if sw == 0 else [("t", kc[sw * NG - 2])])
            emit_kload(1, None if sw == 0 else [("t", kc[sw * NG - 1])])

            def emit_S(tile, buf, lks, lvp):
                Sp = sA if tile % 2 == 0 else sB
                c0, c1 = (tile % 4) * 128, (tile % 4 + 1) * 128
                w = [(lks, lvp)]
                if tile == 0:
                    w.append(("lw3", 16))
                    w += sweep_bar
                if tile >= 2:
                    w.append(("s", exp_t[tile - 2]))
                tS = None
                for half in range(2):
                    hc0 = half * 512
                    op("tensor", lambda e, buf=buf, c0=c0, c1=c1, hc0=hc0,
                       Sp=Sp: e.matmul(
                           out=Sp[:, hc0:hc0 + 512], lhsT=buf["ck"][:, c0:c1],
                           rhs=cosq[:, q0 + hc0:q0 + hc0 + 512],
                           start=True, stop=False),
                       wait=w if half == 0 else None)
                    last = half == 1
                    tS = op("tensor", lambda e, buf=buf, c0=c0, c1=c1, hc0=hc0,
                            Sp=Sp: e.matmul(
                                out=Sp[:, hc0:hc0 + 512], lhsT=buf["sk"][:, c0:c1],
                                rhs=sinq[:, q0 + hc0:q0 + hc0 + 512],
                                start=False, stop=True),
                            tick="t" if last else None)
                return tS, Sp

            pair_t = {}
            quad_t = {}

            def emit_exp(tile, tS, Sp):
                w = [("t", tS)]
                if tile >= 3:
                    w.append(("t", den_t[tile - 3]))
                    w.append(("v", pair_t[tile - 3]))
                pt = ptiles[tile % 3]
                exp_t[tile] = op("scalar", lambda e, Sp=Sp, pt=pt: e.activation(
                    out=pt[:, :], in_=Sp[:, :], func=AF.Exp,
                    bias=E["auxf"][:, AX_ZERO:AX_ZERO + 1],
                    scale=E["auxf"][:, AX_ISQ:AX_ISQ + 1]), tick="s", wait=w)

            def emit_pv(tile, buf):
                c0, c1 = (tile % 4) * 128, (tile % 4 + 1) * 128
                pt = ptiles[tile % 3]
                tden = None
                for half in range(2):
                    hc0 = half * 512
                    w2 = [("s", exp_t[tile])] if half == 0 else None
                    last = half == 1
                    tden = op("tensor", lambda e, buf=buf, c0=c0, c1=c1, hc0=hc0,
                              tile=tile, pt=pt: e.matmul(
                                  out=out_ps[:, hc0:hc0 + 512],
                                  lhsT=buf["vp"][:, c0:c1],
                                  rhs=pt[:, hc0:hc0 + 512], start=(tile == 0),
                                  stop=(tile == NTILE - 1)),
                              tick="t" if last else None, wait=w2)
                if tile % 2 == 1:
                    pair_t[tile] = pair_t[tile - 1] = op(
                        "vector", lambda e, tile=tile: e.tensor_add(
                            out=pairs[(tile // 2) % 2][:, :],
                            in0=ptiles[(tile - 1) % 3][:, :],
                            in1=ptiles[tile % 3][:, :]),
                        tick="v", wait=[("s", exp_t[tile])])
                else:
                    pair_t.setdefault(tile, prog.ctr.get("v", 0))
                if tile % 4 == 3:
                    tq4 = op("vector", lambda e: e.tensor_add(
                        out=quad[:, :], in0=pairs[0][:, :], in1=pairs[1][:, :]),
                        tick="v",
                        wait=[("t", quad_t[tile - 4])] if tile >= 7 else None)
                    for half in range(2):
                        hc0 = half * 512
                        last = half == 1
                        tden = op("tensor", lambda e, hc0=hc0, tile=tile: e.matmul(
                            out=den_ps[0:1, hc0:hc0 + 512],
                            lhsT=E["onesb"][:, 0:1], rhs=quad[:, hc0:hc0 + 512],
                            start=(tile == 3), stop=(tile == NTILE - 1)),
                            tick="t" if last else None,
                            wait=[("v", tq4)] if half == 0 else None)
                        quad_t[tile] = tden
                den_t[tile] = tden
                return tden

            pend = {}
            tile, buf, lks, lvp, _ = tile_meta[0]
            pend[0] = emit_S(tile, buf, lks, lvp)

            def do_pv(t):
                tden = emit_pv(t, tile_meta[t][1])
                if tile_meta[t][4] == 3:
                    g = tile_meta[t][0] // 4
                    kc[sw * NG + g] = tden
                    if g + 2 < NG:
                        emit_kload(g + 2, [("t", kc[sw * NG + g])])
                return tden

            for t in range(NTILE):
                emit_exp(t, *pend.pop(t))
                if t + 1 < NTILE:
                    tile, buf, lks, lvp, _ = tile_meta[t + 1]
                    pend[t + 1] = emit_S(tile, buf, lks, lvp)
                if t >= 1:
                    tden = do_pv(t - 1)
            tden = do_pv(NTILE - 1)
            t_att = tden
            op("vector", lambda e: e.tensor_copy(out=drow2[0:1, :], in_=den_ps[0:1, :]),
               wait=[("t", t_att)])
            op("vector", lambda e: e.tensor_copy(out=out_sb[0:1, :], in_=drow2[0:1, :]))
            op("vector", lambda e: e.tensor_copy(out=rproj[0:1, :], in_=drow2[0:1, :]))
            op("vector", lambda e: e.reciprocal(out=drow[0:1, :], in_=drow2[0:1, :]))
            op("vector", lambda e: e.tensor_copy(out=out_sb[0:1, :], in_=drow2[0:1, :]))
            op("vector", lambda e: e.tensor_copy(out=rproj[0:1, :], in_=drow2[0:1, :]))
            tdr = op("vector", lambda e: e.tensor_copy(out=out_sb[:, :],
                                                       in_=out_ps[:, :]), tick="v")
            for half in range(2):
                hc0 = half * 512
                tb = op("tensor", lambda e, hc0=hc0: e.matmul(
                    out=sA[:, hc0:hc0 + 512], lhsT=E["onesrow"][0:1, :],
                    rhs=drow[0:1, hc0:hc0 + 512], start=True, stop=True),
                    tick="t" if half == 1 else None, wait=[("v", tdr)])
            trp = op("vector", lambda e: e.tensor_mul(
                out=rproj[:, :], in0=out_sb[:, :], in1=sA[:, :]), tick="v",
                wait=[("t", tb)])
            pr_t = {}
            for ch in range(NCH):
                pb = sB if ch % 2 == 0 else den_ps
                if ch >= 2:
                    op("tensor", lambda e: e.nop(), wait=[("v", pr_t[ch - 2])])
                for half in range(2):
                    hc0 = half * 512
                    w3 = [("v", trp)] if (ch == 0 and half == 0) else None
                    op("tensor", lambda e, ch=ch, hc0=hc0, pb=pb: e.matmul(
                        out=pb[:, hc0:hc0 + 512],
                        lhsT=waoT[:, ch * 128:(ch + 1) * 128],
                        rhs=rproj[:, hc0:hc0 + 512], start=True, stop=False),
                        wait=w3)
                    last = half == 1
                    tpm = op("tensor", lambda e, ch=ch, hc0=hc0, pb=pb: e.matmul(
                        out=pb[:, hc0:hc0 + 512],
                        lhsT=boa[0:1, ch * 128:(ch + 1) * 128],
                        rhs=ones512[0:1, :], start=False, stop=True),
                        tick="t" if last else None)
                pr_t[ch] = op("vector", lambda e, ch=ch, cur=cur, pb=pb: e.tensor_add(
                    out=cur[ch][:, :], in0=cur[ch][:, :], in1=pb[:, :]), tick="v",
                    wait=[("t", tpm)] + t_x1rdy)
            stats_prev = None
            for bb in range(SW // TB):
                xsrc = [cur[ch][:, bb * TB:(bb + 1) * TB] for ch in range(NCH)]
                wfst = [("v", pr_t[NCH - 1]), ("t", prog.ctr.get("t", 0))]
                if stats_prev is not None:
                    wfst.append(("t", stats_prev))
                y_t, stats_prev = _ln_emit(op, E, xsrc, wfst)
                x_t = _ffn_emit(op, E, y_t, xsrc,
                                gate_w_wait if (sw == 0 and bb == 0) else None)
                for ch in range(NCH):
                    v = dma("gpsimd", f"st{sw % 2}",
                            out_h[ch * 128:(ch + 1) * 128,
                                  q0 + bb * TB:q0 + (bb + 1) * TB],
                            xsrc[ch], wait=[("v", x_t[ch])])
            prev[(sw, "ost")] = v

        def _fin(e):
            for cn in ("st0", "st1"):
                e.wait_ge(sems[cn], prog.ctr.get(cn, 0))
        prog.emit("gpsimd", _fin)
        _replay(nc, es, prog)
    return nc


def _prep_host(inputs):
    g = lambda k: np.ascontiguousarray(np.asarray(inputs[k], dtype=np.float32))
    auxf = np.zeros((128, NAUXF), np.float32)
    auxf[:, AX_WAV] = g("L0_attn_wavelengths")
    auxf[:, AX_PHASE] = g("L0_attn_phase_bias")
    auxf[:, AX_BQ] = g("L0_attn_bq")
    auxf[:, AX_BK] = g("L0_attn_bk")
    auxf[:, AX_BF:AX_BF + 8] = g("L0_ffn_bf").reshape(8, 128).T
    auxf[:, AX_BI:AX_BI + 8] = g("L0_ffn_bi").reshape(8, 128).T
    auxf[:, AX_BC:AX_BC + 8] = g("L0_ffn_bc").reshape(8, 128).T
    auxf[:, AX_BO_F:AX_BO_F + 8] = g("L0_ffn_bo").reshape(8, 128).T
    auxf[:, AX_INC:AX_INC + 8] = g("L0_ffn_in_c").reshape(8, 128).T
    auxf[:, AX_BO_A:AX_BO_A + 8] = g("L0_attn_bo").reshape(8, 128).T
    auxf[:, AX_ISQ] = 1.0 / math.sqrt(H)
    auxf[:, AX_EPS] = 1e-5
    auxf[:, AX_ZERO] = 0.0
    auxb = np.zeros((128, 2), BF)
    auxb[:, 0] = 1.0
    onesrow = np.ones((1, 128), np.float32)
    bvbc = np.ascontiguousarray(np.broadcast_to(g("L0_attn_bv")[None, :], (128, 128)))
    T = lambda a: np.ascontiguousarray(a.T)
    com = {"wfT": T(g("L0_ffn_wf")), "wiT": T(g("L0_ffn_wi")),
           "wcT": T(g("L0_ffn_wc")), "woT": T(g("L0_ffn_wo")),
           "uoT": T(g("L0_ffn_uo")), "auxf": auxf, "auxb": auxb,
           "onesrow": onesrow}
    a_only = {"ufT": T(g("L0_ffn_uf")), "uiT": T(g("L0_ffn_ui")),
              "ucT": T(g("L0_ffn_uc")),
              "wqT": T(g("L0_attn_Wq")), "wkT": T(g("L0_attn_Wk")),
              "wvT": T(g("L0_attn_Wv")), "bvbc": bvbc}
    b_only = {"waoT": T(g("L0_attn_Wo")),
              "ones512": np.ones((1, 512), np.float32),
              "boaRow": np.ascontiguousarray(g("L0_attn_bo")[None, :])}
    xT = np.ascontiguousarray(g("L0_attn_x").reshape(SEQ, DM).T)
    return com, a_only, b_only, xT


_NC_CACHE = {}


def kernel(**inputs):
    com, a_only, b_only, xT = _prep_host(inputs)
    if "a" not in _NC_CACHE:
        _NC_CACHE["a"] = build_launch_a()
        _NC_CACHE["b"] = build_launch_b()
    nc_a, nc_b = _NC_CACHE["a"], _NC_CACHE["b"]

    in_maps_a = []
    for c in range(NCORES):
        m = dict(com)
        m.update(a_only)
        m["x"] = np.ascontiguousarray(xT[:, c * TC:(c + 1) * TC])
        in_maps_a.append(m)
    res_a = run_bass_kernel_spmd(nc_a, in_maps_a, core_ids=list(range(NCORES)))

    coskF = np.concatenate([res_a.results[c]["cosk"] for c in range(NCORES)], axis=1)
    sinkF = np.concatenate([res_a.results[c]["sink"] for c in range(NCORES)], axis=1)
    vpermF = np.concatenate([res_a.results[c]["vperm"] for c in range(NCORES)], axis=1)

    in_maps_b = []
    for c in range(NCORES):
        m = dict(com)
        m.update(b_only)
        m["x1"] = res_a.results[c]["x1"]
        m["cosq"] = res_a.results[c]["cosq"]
        m["sinq"] = res_a.results[c]["sinq"]
        m["coskF"] = coskF
        m["sinkF"] = sinkF
        m["vpermF"] = vpermF
        m["smalls"] = res_a.results[c]["smalls"]
        in_maps_b.append(m)
    res_b = run_bass_kernel_spmd(nc_b, in_maps_b, core_ids=list(range(NCORES)))

    out = np.concatenate([res_b.results[c]["out"] for c in range(NCORES)], axis=1)
    return np.ascontiguousarray(out.T).reshape(1, SEQ, DM)


# version 08
# speedup vs baseline: 1.0794x; 1.0794x over previous
import contextlib
import math

import numpy as np
import ml_dtypes

import concourse.bass as bass
import concourse.mybir as mybir
from concourse.bass_utils import run_bass_kernel_spmd

F32 = mybir.dt.float32
_DBG_VV = False
BF16 = mybir.dt.bfloat16
I32 = mybir.dt.int32
BF = ml_dtypes.bfloat16
AF = mybir.ActivationFunctionType
ALU = mybir.AluOpType

NCORES = 8
DM = 1024
H = 128
NCH = DM // 128
TC = 2048
TB = 512
NB = TC // TB
SEQ = NCORES * TC
NTILE = SEQ // 128
SW = 1024
NSW = TC // SW
KG = 512

_TPI = 2 * math.pi
TWO_PI_HI = float(np.float32(_TPI))
TWO_PI_MID = float(np.float32(_TPI - TWO_PI_HI))
TWO_PI_LO = float(_TPI - TWO_PI_HI - TWO_PI_MID)
PI = float(np.float32(math.pi))
HALF_PI = float(np.float32(math.pi / 2))

AX_WAV, AX_PHASE, AX_BQ, AX_BK = 0, 1, 2, 3
AX_BF, AX_BI, AX_BC, AX_BO_F, AX_INC, AX_BO_A = 4, 12, 20, 28, 36, 44
AX_ISQ, AX_EPS, AX_ZERO = 52, 53, 54
NAUXF = 56

SM_GBF, SM_GBI, SM_GBC, SM_INVW = 0, 8, 16, 24
NSMALL = 25
SM_QB, SM_KB = 25, 26


class Prog:

    def __init__(self):
        self.p = {k: [] for k in ("sync", "gpsimd", "vector", "scalar", "tensor")}
        self.ctr = {}

    def emit(self, eng, fn):
        self.p[eng].append(fn)


def _mk(prog, sems):
    def w_emit(e, wait):
        if wait:
            for (cn, wv) in wait:
                e.wait_ge(sems[cn], wv)

    def dma(queue, cn, out_ap, in_ap, wait=None, qn=None):
        prog.ctr[cn] = prog.ctr.get(cn, 0) + 16
        val = prog.ctr[cn]

        def f(e, out_ap=out_ap, in_ap=in_ap, wait=wait):
            w_emit(e, wait)
            e.dma_start(out=out_ap, in_=in_ap).then_inc(sems[cn], 16)

        prog.emit(queue, f)
        return val

    def op(eng, fn, tick=None, wait=None):
        val = None
        if tick is not None:
            prog.ctr[tick] = prog.ctr.get(tick, 0) + 1
            val = prog.ctr[tick]

        def f(e, fn=fn, wait=wait, tick=tick):
            w_emit(e, wait)
            ins = fn(e)
            if tick is not None:
                ins.then_inc(sems[tick], 1)

        prog.emit(eng, f)
        return val

    return dma, op


def _ln_emit(op, E, xsrc, wait_first_cast):
    xb, x2b, yb = E["xb"], E["x2b"], E["yb"]
    stats, bc = E["stats_ps"], E["bc_ps"]
    srow, mrow, zrow = E["srow"], E["mrow"], E["zrow"]
    m_bc, z_bc = E["m_bc"], E["z_bc"]
    onesb, auxf = E["onesb"], E["auxf"]

    cast_t = []
    for ch in range(NCH):
        w = wait_first_cast if ch == 0 else None
        op("vector", lambda e, ch=ch: e.tensor_copy(out=xb[ch][:, :], in_=xsrc[ch]),
           wait=w)
        t = op("vector", lambda e, ch=ch: e.tensor_mul(
            out=x2b[ch][:, :], in0=xb[ch][:, :], in1=xb[ch][:, :]), tick="v")
        cast_t.append(t)
    for ch in range(NCH):
        op("tensor", lambda e, ch=ch: e.matmul(
            out=stats[0:1, 0:TB], lhsT=onesb[:, 0:1], rhs=xb[ch][:, :],
            start=(ch == 0), stop=(ch == NCH - 1)),
           wait=[("v", cast_t[ch])])
        last = (ch == NCH - 1)
        t_stats = op("tensor", lambda e, ch=ch, last=last: e.matmul(
            out=stats[0:1, TB:2 * TB], lhsT=onesb[:, 0:1], rhs=x2b[ch][:, :],
            start=(ch == 0), stop=last),
            tick="t" if last else None)
    op("vector", lambda e: e.tensor_scalar(
        out=mrow[0:1, :], in0=stats[0:1, 0:TB], scalar1=1.0 / DM, scalar2=None,
        op0=ALU.mult), wait=[("t", t_stats)])
    op("vector", lambda e: e.tensor_scalar(
        out=srow[0:1, 0:TB], in0=stats[0:1, TB:2 * TB], scalar1=1.0 / DM,
        scalar2=None, op0=ALU.mult))
    op("vector", lambda e: e.tensor_mul(
        out=srow[0:1, TB:2 * TB], in0=mrow[0:1, :], in1=mrow[0:1, :]))
    tv = op("vector", lambda e: e.tensor_sub(
        out=srow[0:1, 0:TB], in0=srow[0:1, 0:TB], in1=srow[0:1, TB:2 * TB]),
        tick="v")
    ts0 = op("scalar", lambda e: e.activation(
        out=srow[0:1, TB:2 * TB], in_=srow[0:1, 0:TB], func=AF.Sqrt,
        bias=auxf[0:1, AX_EPS:AX_EPS + 1]), tick="s", wait=[("v", tv)])
    op("vector", lambda e: e.tensor_scalar(
        out=srow[0:1, 2 * TB:3 * TB], in0=srow[0:1, 0:TB], scalar1=1e-5,
        scalar2=None, op0=ALU.add), wait=[("s", ts0)])
    op("vector", lambda e: e.reciprocal(out=zrow[0:1, :], in_=srow[0:1, TB:2 * TB]))
    op("vector", lambda e: e.tensor_copy(out=m_bc[0:1, :], in_=mrow[0:1, :]))
    op("vector", lambda e: e.tensor_copy(out=z_bc[0:1, :], in_=mrow[0:1, :]))
    op("vector", lambda e: e.tensor_mul(
        out=srow[0:1, 0:TB], in0=zrow[0:1, :], in1=zrow[0:1, :]))
    op("vector", lambda e: e.tensor_mul(
        out=srow[0:1, 0:TB], in0=srow[0:1, 0:TB], in1=srow[0:1, 2 * TB:3 * TB]))
    op("vector", lambda e: e.tensor_scalar(
        out=srow[0:1, 0:TB], in0=srow[0:1, 0:TB], scalar1=-0.5, scalar2=1.5,
        op0=ALU.mult, op1=ALU.add))
    tz = op("vector", lambda e: e.tensor_mul(
        out=zrow[0:1, :], in0=zrow[0:1, :], in1=srow[0:1, 0:TB]), tick="v")
    op("vector", lambda e: e.tensor_copy(
        out=E["mrow_b"][0:1, :], in_=mrow[0:1, :]))
    tzb = op("vector", lambda e: e.tensor_copy(
        out=E["zrow_b"][0:1, :], in_=zrow[0:1, :]), tick="v")
    op("tensor", lambda e: e.matmul(
        out=bc[:, 0:TB], lhsT=E["onesrow_b"][0:1, :], rhs=E["mrow_b"][0:1, :],
        start=True, stop=True), wait=[("v", tzb)])
    tb = op("tensor", lambda e: e.matmul(
        out=bc[:, TB:2 * TB], lhsT=E["onesrow_b"][0:1, :],
        rhs=E["zrow_b"][0:1, :], start=True, stop=True), tick="t")
    op("vector", lambda e: e.tensor_copy(out=m_bc[:, :], in_=bc[:, 0:TB]),
       wait=[("t", tb)])
    op("vector", lambda e: e.tensor_copy(out=z_bc[:, :], in_=bc[:, TB:2 * TB]))
    y_t = []
    for ch in range(NCH):
        op("vector", lambda e, ch=ch: e.tensor_sub(
            out=yb[ch][:, :], in0=xb[ch][:, :], in1=m_bc[:, :]))
        t = op("vector", lambda e, ch=ch: e.tensor_mul(
            out=yb[ch][:, :], in0=yb[ch][:, :], in1=z_bc[:, :]), tick="v")
        y_t.append(t)
    return y_t, t_stats


def _ffn_emit(op, E, y_ticks, xdst, extra_gate_wait):
    yb, cb = E["yb"], E["cb"]
    ftile, itile, gtile, gtmp = E["ftile"], E["itile"], E["gtile"], E["gtmp"]
    th32 = E["th32"]
    wf, wi, wc, wo, uo = E["wfT"], E["wiT"], E["wcT"], E["woT"], E["uoT"]
    auxf, smalls = E["auxf"], E["smalls"]
    gsets, hsets = E["gsets"], E["hsets"]

    act_t = {}
    c_t = {}
    for ob in range(8):
        pf, pi_, pg = gsets[ob % 2]
        if ob >= 2:
            op("tensor", lambda e: e.nop(), wait=[("s", act_t[ob - 2])])
        for ch in range(NCH):
            w = [("v", y_ticks[ch])]
            if ob == 0 and ch == 0 and extra_gate_wait:
                w = w + extra_gate_wait
            op("tensor", lambda e, ch=ch, ob=ob, pf=pf: e.matmul(
                out=pf, lhsT=wf[:, ch * DM + ob * 128:ch * DM + (ob + 1) * 128],
                rhs=yb[ch][:, :], start=(ch == 0), stop=(ch == NCH - 1)),
               wait=w if ob == 0 else (w if ch == 0 and ob == 0 else None))
        for ch in range(NCH):
            op("tensor", lambda e, ch=ch, ob=ob, pi_=pi_: e.matmul(
                out=pi_, lhsT=wi[:, ch * DM + ob * 128:ch * DM + (ob + 1) * 128],
                rhs=yb[ch][:, :], start=(ch == 0), stop=(ch == NCH - 1)))
        for ch in range(NCH):
            last = ch == NCH - 1
            tmm = op("tensor", lambda e, ch=ch, ob=ob, pg=pg, last=last: e.matmul(
                out=pg, lhsT=wc[:, ch * DM + ob * 128:ch * DM + (ob + 1) * 128],
                rhs=yb[ch][:, :], start=(ch == 0), stop=last),
                tick="t" if last else None)
        aw = [("t", tmm)]
        if ob >= 1:
            aw.append(("v", c_t[ob - 1]))
        if ob == 0 and E.get("act_extra_wait"):
            aw = aw + E.pop("act_extra_wait")
        op("scalar", lambda e, ob=ob, pf=pf: e.activation(
            out=ftile[:, :], in_=pf, func=AF.Sigmoid,
            bias=smalls[:, SM_GBF + ob:SM_GBF + ob + 1]), wait=aw)
        op("scalar", lambda e, ob=ob, pi_=pi_: e.activation(
            out=itile[:, :], in_=pi_, func=AF.Sigmoid,
            bias=smalls[:, SM_GBI + ob:SM_GBI + ob + 1]))
        ta = op("scalar", lambda e, ob=ob, pg=pg: e.activation(
            out=gtile[:, :], in_=pg, func=AF.Tanh,
            bias=smalls[:, SM_GBC + ob:SM_GBC + ob + 1]), tick="s")
        act_t[ob] = ta
        op("vector", lambda e, ob=ob: e.tensor_scalar(
            out=cb[ob][:, :], in0=ftile[:, :],
            scalar1=auxf[:, AX_INC + ob:AX_INC + ob + 1], scalar2=None,
            op0=ALU.mult), wait=[("s", ta)])
        op("vector", lambda e: e.tensor_mul(
            out=gtmp[:, :], in0=itile[:, :], in1=gtile[:, :]))
        c_t[ob] = op("vector", lambda e, ob=ob: e.tensor_add(
            out=cb[ob][:, :], in0=cb[ob][:, :], in1=gtmp[:, :]), tick="v")
        if ob == 1 and E.get("ffn_mid_hook"):
            E.pop("ffn_mid_hook")()

    h_t = {}
    x_t = {}
    for ob in range(8):
        ph = hsets[ob % 2]
        if ob >= 2:
            op("tensor", lambda e: e.nop(), wait=[("s", h_t[ob - 2])])
        for ch in range(NCH):
            w = None
            if ob == 0 and ch == 0:
                w = [("v", c_t[7])] + (E.get("h_weight_wait") or [])
            op("tensor", lambda e, ch=ch, ob=ob, ph=ph: e.matmul(
                out=ph, lhsT=wo[:, ch * DM + ob * 128:ch * DM + (ob + 1) * 128],
                rhs=yb[ch][:, :], start=(ch == 0), stop=False), wait=w)
        for ch in range(NCH):
            last = ch == NCH - 1
            tmm = op("tensor", lambda e, ch=ch, ob=ob, ph=ph, last=last: e.matmul(
                out=ph, lhsT=uo[:, ch * DM + ob * 128:ch * DM + (ob + 1) * 128],
                rhs=cb[ch][:, :], start=False, stop=last),
                tick="t" if last else None)
        hw = [("t", tmm)]
        if ob >= 2:
            hw.append(("v", x_t[ob - 2]))
        tth = op("scalar", lambda e, ob=ob, ph=ph: e.activation(
            out=th32[ob % 2][:, :], in_=ph, func=AF.Tanh,
            bias=auxf[:, AX_BO_F + ob:AX_BO_F + ob + 1]), tick="s", wait=hw)
        h_t[ob] = tth
        x_t[ob] = op("vector", lambda e, ob=ob: e.scalar_tensor_tensor(
            out=xdst[ob], in0=th32[ob % 2][:, :], scalar=0.5, in1=xdst[ob],
            op0=ALU.mult, op1=ALU.add), tick="v", wait=[("s", tth)])
    return x_t


def _trig_emit(op, E, src_ps_ap, bias_col, cos_out, sin_out, wait_src):
    th, red, kf, ki = E["trig_th"], E["trig_red"], E["trig_kf"], E["trig_ki"]
    smalls, auxf = E["smalls"], E["auxf"]
    w0 = list(wait_src)
    if E.get("trig_last_s"):
        w0.append(("s", E["trig_last_s"]))
    op("vector", lambda e: e.tensor_copy(out=th[:, :], in_=src_ps_ap), wait=w0)
    op("vector", lambda e: e.tensor_scalar(
        out=red[:, :], in0=th[:, :], scalar1=auxf[:, bias_col:bias_col + 1],
        scalar2=None, op0=ALU.add))
    op("vector", lambda e: e.tensor_scalar(
        out=th[:, :], in0=red[:, :], scalar1=smalls[:, SM_INVW:SM_INVW + 1],
        scalar2=auxf[:, AX_PHASE:AX_PHASE + 1], op0=ALU.mult, op1=ALU.add))
    op("vector", lambda e: e.tensor_scalar(
        out=red[:, :], in0=th[:, :], scalar1=-1.0 / _TPI, scalar2=None,
        op0=ALU.mult))
    op("vector", lambda e: e.tensor_copy(out=ki[:, :], in_=red[:, :]))
    op("vector", lambda e: e.tensor_copy(out=kf[:, :], in_=ki[:, :]))
    op("vector", lambda e: e.scalar_tensor_tensor(
        out=red[:, :], in0=kf[:, :], scalar=TWO_PI_HI, in1=th[:, :],
        op0=ALU.mult, op1=ALU.add))
    op("vector", lambda e: e.scalar_tensor_tensor(
        out=red[:, :], in0=kf[:, :], scalar=TWO_PI_MID, in1=red[:, :],
        op0=ALU.mult, op1=ALU.add))
    op("vector", lambda e: e.scalar_tensor_tensor(
        out=red[:, :], in0=kf[:, :], scalar=TWO_PI_LO, in1=red[:, :],
        op0=ALU.mult, op1=ALU.add))
    tsin = op("vector", lambda e: e.tensor_scalar(
        out=th[:, :], in0=red[:, :], scalar1=PI, scalar2=-PI,
        op0=ALU.min, op1=ALU.max), tick="v")
    tss = op("scalar", lambda e: e.activation(
        out=sin_out, in_=th[:, :], func=AF.Sin,
        bias=E["auxf"][:, AX_ZERO:AX_ZERO + 1]), tick="s", wait=[("v", tsin)])
    op("vector", lambda e: e.tensor_scalar(
        out=kf[:, :], in0=th[:, :], scalar1=HALF_PI, scalar2=None,
        op0=ALU.is_gt), wait=[("s", tss)])
    op("vector", lambda e: e.scalar_tensor_tensor(
        out=red[:, :], in0=kf[:, :], scalar=-TWO_PI_HI, in1=th[:, :],
        op0=ALU.mult, op1=ALU.add))
    op("vector", lambda e: e.tensor_scalar(
        out=kf[:, :], in0=red[:, :], scalar1=HALF_PI, scalar2=PI,
        op0=ALU.add, op1=ALU.min))
    tcos = op("vector", lambda e: e.tensor_scalar(
        out=red[:, :], in0=kf[:, :], scalar1=-PI, scalar2=None, op0=ALU.max),
        tick="v")
    tcs = op("scalar", lambda e: e.activation(
        out=cos_out, in_=red[:, :], func=AF.Sin,
        bias=E["auxf"][:, AX_ZERO:AX_ZERO + 1]), tick="s", wait=[("v", tcos)])
    E["trig_last_s"] = tcs
    return tcs


def _replay(nc, es, prog):
    block = es.enter_context(nc.Block())

    @block.sync
    def _(e):
        for f in prog.p["sync"]:
            f(e)

    @block.gpsimd
    def _(e):
        for f in prog.p["gpsimd"]:
            f(e)

    @block.vector
    def _(e):
        for f in prog.p["vector"]:
            f(e)

    @block.scalar
    def _(e):
        for f in prog.p["scalar"]:
            f(e)

    @block.tensor
    def _(e):
        for f in prog.p["tensor"]:
            f(e)


def _common_sbuf(nc, es):
    sb = lambda n, sh, dt: es.enter_context(nc.sbuf_tensor(n, sh, dt))
    E = {}
    for nm in ("wfT", "wiT", "wcT", "woT", "uoT"):
        E[nm] = sb(nm + "_s", [128, NCH * DM], BF16)
    E["xb"] = [sb(f"xb{c}", [128, TB], BF16) for c in range(NCH)]
    E["yb"] = [sb(f"y{c}", [128, TB], BF16) for c in range(NCH)]
    E["cb"] = [sb(f"c{c}", [128, TB], BF16) for c in range(NCH)]
    E["x2b"] = E["cb"]
    E["ftile"] = sb("ftile", [128, TB], BF16)
    E["itile"] = sb("itile", [128, TB], BF16)
    E["gtile"] = sb("gtile", [128, TB], BF16)
    E["gtmp"] = sb("gtmp", [128, TB], BF16)
    E["th32"] = [sb(f"th32_{i}", [128, TB], F32) for i in range(2)]
    E["m_bc"] = sb("m_bc", [128, TB], BF16)
    E["z_bc"] = sb("z_bc", [128, TB], BF16)
    E["srow"] = sb("srow", [1, 3 * TB], F32)
    E["mrow"] = sb("mrow", [1, TB], F32)
    E["zrow"] = sb("zrow", [1, TB], F32)
    E["auxf"] = sb("auxf_s", [128, NAUXF], F32)
    E["onesb"] = sb("auxb_s", [128, 2], BF16)
    E["onesrow_b"] = sb("onesrow_b", [1, 128], BF16)
    E["mrow_b"] = sb("mrow_b", [1, TB], BF16)
    E["zrow_b"] = sb("zrow_b", [1, TB], BF16)
    return E, sb


def build_launch_a(n_blocks=NB, do_ffn=True, do_ln2=True, do_qkv=True, do_v=True):
    nc = bass.Bass("TRN2", target_bir_lowering=False,
                   detect_race_conditions=False)
    x_h = nc.dram_tensor("x", [DM, TC], F32, kind="ExternalInput").ap()
    w_h = {nm: nc.dram_tensor(nm, [DM, DM], F32, kind="ExternalInput").ap()
           for nm in ("wfT", "wiT", "wcT", "woT", "uoT", "ufT", "uiT", "ucT")}
    wq_h = {nm: nc.dram_tensor(nm, [DM, H], F32, kind="ExternalInput").ap()
            for nm in ("wqT", "wkT", "wvT")}
    auxf_h = nc.dram_tensor("auxf", [128, NAUXF], F32, kind="ExternalInput").ap()
    auxb_h = nc.dram_tensor("auxb", [128, 2], BF16, kind="ExternalInput").ap()
    onesrow_h = nc.dram_tensor("onesrow", [1, 128], BF16,
                               kind="ExternalInput").ap()
    bvbc_h = nc.dram_tensor("bvbc", [128, 128], F32, kind="ExternalInput").ap()
    x1_h = nc.dram_tensor("x1", [DM, TC], F32, kind="ExternalOutput").ap()
    cosq_h = nc.dram_tensor("cosq", [H, TC], BF16, kind="ExternalOutput").ap()
    sinq_h = nc.dram_tensor("sinq", [H, TC], BF16, kind="ExternalOutput").ap()
    cosk_h = nc.dram_tensor("cosk", [H, TC], BF16, kind="ExternalOutput").ap()
    sink_h = nc.dram_tensor("sink", [H, TC], BF16, kind="ExternalOutput").ap()
    vperm_h = nc.dram_tensor("vperm", [128, TC], BF16, kind="ExternalOutput").ap()
    smalls_h = nc.dram_tensor("smalls", [128, NSMALL], F32, kind="ExternalOutput").ap()
    if _DBG_VV:
        vvdump_h = nc.dram_tensor("vvdump", [128, 544], F32,
                                  kind="ExternalOutput").ap()

    es = contextlib.ExitStack()
    with es:
        E, sb = _common_sbuf(nc, es)
        ps_ = lambda n, sh: es.enter_context(nc.psum_tensor(n, sh, F32))
        E["smalls"] = sb("smalls_s", [128, 28], F32)
        wq_s = {nm: [sb(f"{nm}{c}", [128, H], BF16) for c in range(NCH)]
                for nm in ("wqT", "wkT", "wvT")}
        u_stage = [sb(f"ust{i}", [128, DM], F32) for i in range(2)]
        xblk = [[sb(f"x{i}_{c}", [128, TB], F32) for c in range(NCH)]
                for i in range(2)]
        bvbc = sb("bvbc_s", [128, 128], F32)

        E["trig_th"] = sb("trig_th", [128, TB], F32)
        E["trig_red"] = sb("trig_red", [128, TB], F32)
        E["trig_kf"] = sb("trig_kf", [128, TB], F32)
        E["trig_ki"] = sb("trig_ki", [128, TB], I32)
        cosq_s = sb("cosq_sb", [H, TC], BF16)
        sinq_s = sb("sinq_sb", [H, TC], BF16)
        cosk_s = sb("cosk_sb", [H, TC], BF16)
        sink_s = sb("sink_sb", [H, TC], BF16)
        vperm_s = sb("vperm_sb", [128, TC], BF16)

        stats_ps = ps_("stats", [128, 2 * TB])
        bc_ps = ps_("bcst", [128, 2 * TB])
        gA0 = ps_("gA0", [128, TB])
        gB0 = ps_("gB0", [128, TB])
        vv_ps = ps_("vv", [128, TB])
        E["stats_ps"] = stats_ps
        E["bc_ps"] = bc_ps
        E["gsets"] = ((gA0[:, :], stats_ps[:, 0:TB], stats_ps[:, TB:2 * TB]),
                      (gB0[:, :], bc_ps[:, 0:TB], bc_ps[:, TB:2 * TB]))
        E["hsets"] = (gA0[:, :], gB0[:, :])
        qk_ps = bc_ps
        v_ps = stats_ps

        sems = {k: es.enter_context(nc.semaphore(f"sem_{k}"))
                for k in ("lda", "lx0", "lx1", "lu0", "lu1", "lw1", "lw2",
                          "lw3", "st0", "st1", "stf", "v", "s", "t")}
        prog = Prog()
        dma, op = _mk(prog, sems)

        dma("sync", "lda", E["auxf"][:, :], auxf_h)
        dma("sync", "lda", E["onesb"][:, :], auxb_h)
        dma("sync", "lda", E["onesrow_b"][:, :], onesrow_h)
        dma("sync", "lda", bvbc[:, :], bvbc_h)
        AUX = [("lda", 64)]
        xld = {}
        for ch in range(NCH):
            xld[(0, ch)] = dma("sync", "lx0", xblk[0][ch][:, :],
                               x_h[ch * 128:(ch + 1) * 128, 0:TB])
        u_t = {}

        def emit_uchain():
            tick_last = {}
            for idx in range(24):
                mi, ic = divmod(idx, NCH)
                buf = u_stage[idx % 2]
                lus = f"lu{idx % 2}"
                w = [("t", tick_last[idx - 2])] if idx >= 2 else None
                lv = dma("sync", lus, buf[:, :],
                         w_h[("ufT", "uiT", "ucT")[mi]][ic * 128:(ic + 1) * 128, :],
                         wait=w)
                for oc in range(NCH):
                    last = oc == NCH - 1
                    col = ((mi * 8 + oc) * 8) + ic
                    tick_last[idx] = op(
                        "tensor", lambda e, buf=buf, oc=oc, ic=ic, col=col: e.matmul(
                            out=vv_ps[:, col:col + 1],
                            lhsT=buf[:, oc * 128:(oc + 1) * 128],
                            rhs=E["auxf"][:, AX_INC + ic:AX_INC + ic + 1],
                            start=True, stop=True, skip_group_check=True),
                        tick="t" if last else None,
                        wait=([(lus, lv)] + AUX) if oc == 0 else None)
            tt = E["trig_th"]
            op("vector", lambda e: e.tensor_copy(
                out=tt[:, 0:192], in_=vv_ps[:, 0:192]),
                wait=[("t", tick_last[23])])
            t192 = tt[:, 0:192].rearrange("p (c two) -> p c two", two=2)
            op("vector", lambda e: e.tensor_add(
                out=tt[:, 192:288], in0=t192[:, :, 0], in1=t192[:, :, 1]))
            t96 = tt[:, 192:288].rearrange("p (c two) -> p c two", two=2)
            op("vector", lambda e: e.tensor_copy(
                out=E["trig_red"][:, 64:120], in_=E["auxf"][:, 0:56]))
            op("vector", lambda e: e.tensor_add(
                out=tt[:, 288:336], in0=t96[:, :, 0], in1=t96[:, :, 1]))
            t48 = tt[:, 288:336].rearrange("p (c two) -> p c two", two=2)
            op("vector", lambda e: e.tensor_copy(
                out=E["trig_red"][:, 64:120], in_=E["auxf"][:, 0:56]))
            op("vector", lambda e: e.tensor_copy(
                out=E["trig_red"][:, 64:120], in_=E["auxf"][:, 0:56]))
            t_ev = op("vector", lambda e: e.tensor_add(
                out=E["smalls"][:, 0:24], in0=t48[:, :, 0], in1=t48[:, :, 1]),
                tick="v")
            op("vector", lambda e: e.tensor_copy(
                out=E["trig_red"][:, 64:120], in_=E["auxf"][:, 0:56]))
            op("vector", lambda e: e.tensor_copy(
                out=E["trig_red"][:, 64:120], in_=E["auxf"][:, 0:56]))
            for sm0, ax0 in ((SM_GBF, AX_BF), (SM_GBI, AX_BI), (SM_GBC, AX_BC)):
                op("vector", lambda e, sm0=sm0, ax0=ax0: e.tensor_add(
                    out=E["smalls"][:, sm0:sm0 + 8], in0=E["smalls"][:, sm0:sm0 + 8],
                    in1=E["auxf"][:, ax0:ax0 + 8]))
            op("vector", lambda e: e.tensor_scalar(
                out=E["trig_kf"][:, 0:8], in0=E["auxf"][:, AX_WAV:AX_WAV + 8],
                scalar1=1e-3, scalar2=1e-8, op0=ALU.max, op1=ALU.add))
            op("vector", lambda e: e.tensor_copy(
                out=E["trig_th"][:, 0:56], in_=E["auxf"][:, 0:56]))
            op("vector", lambda e: e.tensor_copy(
                out=E["trig_th"][:, 0:56], in_=E["auxf"][:, 0:56]))
            op("vector", lambda e: e.reciprocal(
                out=E["trig_red"][:, 0:8], in_=E["trig_kf"][:, 0:8]))
            op("vector", lambda e: e.tensor_copy(
                out=E["trig_th"][:, 0:56], in_=E["auxf"][:, 0:56]))
            op("vector", lambda e: e.tensor_copy(
                out=E["trig_th"][:, 0:56], in_=E["auxf"][:, 0:56]))
            t_smalls = op("vector", lambda e: e.tensor_copy(
                out=E["smalls"][:, SM_INVW:SM_INVW + 1], in_=E["trig_red"][:, 0:1]),
                tick="v")
            E["act_extra_wait"] = [("v", t_smalls)]
            E["t_smalls_v"] = t_smalls
            if _DBG_VV:
                dma("sync", "stf", vvdump_h[:, 192:528],
                    E["trig_th"][:, 0:336], wait=[("v", t_ev)])
                dma("sync", "stf", vvdump_h[:, 0:28],
                    E["smalls"][:, 0:28], wait=[("v", t_smalls)])
                dma("sync", "stf", vvdump_h[:, 28:84],
                    E["auxf"][:, 0:56], wait=[("v", t_smalls)])

        for nm in ("wfT", "wiT", "wcT"):
            dma("gpsimd", "lw1", E[nm][:, :].rearrange("p (c i) -> p c i", c=NCH),
                w_h[nm].rearrange("(c p) i -> p c i", p=128))
        for nm in ("woT", "uoT"):
            dma("gpsimd", "lw2", E[nm][:, :].rearrange("p (c i) -> p c i", c=NCH),
                w_h[nm].rearrange("(c p) i -> p c i", p=128))
        for nm in ("wqT", "wkT", "wvT"):
            for ch in range(NCH):
                dma("gpsimd", "lw3", wq_s[nm][ch][:, :],
                    wq_h[nm][ch * 128:(ch + 1) * 128, :])
        E["h_weight_wait"] = [("lw2", 2 * 16)]
        gate_w_wait = [("lw1", 3 * 16)]
        qk_w_wait = [("lw3", 24 * 16)]

        prev = {}
        for b in range(n_blocks):
            cur = xblk[b % 2]
            c0 = b * TB
            lxs = f"lx{b % 2}"
            sts = f"st{b % 2}"
            if b > 0:
                for ch in range(NCH):
                    w = None
                    if ch == 0 and b >= 2:
                        w = [("v", prev[(b - 2, "y2")]), (sts, prev[(b - 2, "xst")])]
                    xld[(b, ch)] = dma("sync", lxs, cur[ch][:, :],
                                       x_h[ch * 128:(ch + 1) * 128, c0:c0 + TB],
                                       wait=w)
            xsrc = [cur[ch][:, :] for ch in range(NCH)]
            wf = [(lxs, xld[(b, NCH - 1)])] + AUX
            if b > 0:
                wf.append(("t", prev[(b - 1, "stats2")]))
            y_t, _ = _ln_emit(op, E, xsrc, wf)
            if b == 0:
                emit_uchain()
            if not do_ffn:
                prev[(b, "stats2")] = prog.ctr.get("t", 0)
                prev[(b, "y2")] = prog.ctr.get("v", 0)
                prev[(b, "xst")] = prog.ctr.get(f"st{b % 2}", 0)
                prev[(b, "trig_v")] = prog.ctr.get("v", 0)
                prev[(b, "vev_last")] = prog.ctr.get("v", 0)
                for sub in range(TB // 128):
                    prev[(b, f"vev{sub}")] = prog.ctr.get("v", 0)
                continue
            x_t = _ffn_emit(op, E, y_t, xsrc, gate_w_wait if b == 0 else None)
            for ch in range(NCH):
                v = dma("gpsimd", sts,
                        x1_h[ch * 128:(ch + 1) * 128, c0:c0 + TB], cur[ch][:, :],
                        wait=[("v", x_t[ch])])
            prev[(b, "xst")] = v
            if not do_ln2:
                prev[(b, "stats2")] = prog.ctr.get("t", 0)
                prev[(b, "y2")] = prog.ctr.get("v", 0)
                prev[(b, "trig_v")] = prog.ctr.get("v", 0)
                prev[(b, "vev_last")] = prog.ctr.get("v", 0)
                for sub in range(TB // 128):
                    prev[(b, f"vev{sub}")] = prog.ctr.get("v", 0)
                continue
            wf2 = [("v", x_t[NCH - 1])]
            y2_t, t_stats2 = _ln_emit(op, E, xsrc, wf2)
            prev[(b, "stats2")] = t_stats2
            prev[(b, "y2")] = y2_t[NCH - 1]
            if not do_qkv:
                prev[(b, "trig_v")] = prog.ctr.get("v", 0)
                prev[(b, "vev_last")] = prog.ctr.get("v", 0)
                for sub in range(TB // 128):
                    prev[(b, f"vev{sub}")] = prog.ctr.get("v", 0)
                continue
            for (nm, pcols, cs, sn, bcol) in (
                    ("wqT", (0, TB), cosq_s, sinq_s, AX_BQ),
                    ("wkT", (TB, 2 * TB), cosk_s, sink_s, AX_BK)):
                for ch in range(NCH):
                    w = [("v", y2_t[ch])]
                    if ch == 0:
                        w += qk_w_wait
                        if b > 0:
                            w.append(("v", prev[(b - 1, "trig_v")]))
                    last = ch == NCH - 1
                    tq = op("tensor", lambda e, ch=ch, nm=nm, pcols=pcols,
                            last=last: e.matmul(
                        out=qk_ps[:, pcols[0]:pcols[1]], lhsT=wq_s[nm][ch][:, :],
                        rhs=E["yb"][ch][:, :], start=(ch == 0), stop=last),
                        tick="t" if last else None, wait=w)
                _trig_emit(op, E, qk_ps[:, pcols[0]:pcols[1]], bcol,
                           cs[:, c0:c0 + TB], sn[:, c0:c0 + TB],
                           [("t", tq), ("v", E["t_smalls_v"])])
            prev[(b, "trig_v")] = prog.ctr.get("v", 0)
            if not do_v:
                prev[(b, "vev_last")] = prog.ctr.get("v", 0)
                for sub in range(TB // 128):
                    prev[(b, f"vev{sub}")] = prog.ctr.get("v", 0)
                continue
            for sub in range(TB // 128):
                pc = (sub % 2) * 512 + (sub // 2) * 128
                tk0 = c0 + sub * 128
                for ch in range(NCH):
                    w = None
                    if ch == 0:
                        w = []
                        if sub == 0 and b > 0:
                            w.append(("v", prev[(b - 1, "vev_last")]))
                        w = w or None
                    last = ch == NCH - 1
                    tvm = op("tensor", lambda e, ch=ch, pc=pc, sub=sub,
                             last=last: e.matmul(
                        out=v_ps[:, pc:pc + 128],
                        lhsT=E["yb"][ch][:, sub * 128:(sub + 1) * 128],
                        rhs=wq_s["wvT"][ch][:, :], start=(ch == 0), stop=last),
                        tick="t" if last else None, wait=w)
                t = op("vector", lambda e, pc=pc, tk0=tk0: e.tensor_add(
                    out=vperm_s[:, tk0:tk0 + 128], in0=v_ps[:, pc:pc + 128],
                    in1=bvbc[:, :]), tick="v", wait=[("t", tvm)])
                prev[(b, f"vev{sub}")] = t
            prev[(b, "vev_last")] = t

        t_end_v = prog.ctr.get("v", 0)
        t_end_s = prog.ctr.get("s", 0)
        for hb, sbl in ((cosq_h, cosq_s), (sinq_h, sinq_s), (cosk_h, cosk_s),
                        (sink_h, sink_s), (vperm_h, vperm_s)):
            dma("gpsimd", "stf", hb, sbl[:, :],
                wait=[("v", t_end_v), ("s", t_end_s)])
        dma("gpsimd", "stf", smalls_h, E["smalls"][:, 0:NSMALL])

        def _fin(e):
            for cn in ("st0", "st1", "stf"):
                e.wait_ge(sems[cn], prog.ctr.get(cn, 0))
        prog.emit("gpsimd", _fin)

        _replay(nc, es, prog)
    return nc


def build_launch_b():
    nc = bass.Bass("TRN2", target_bir_lowering=False,
                   detect_race_conditions=False)
    x1_h = nc.dram_tensor("x1", [DM, TC], F32, kind="ExternalInput").ap()
    cosq_h = nc.dram_tensor("cosq", [H, TC], BF16, kind="ExternalInput").ap()
    sinq_h = nc.dram_tensor("sinq", [H, TC], BF16, kind="ExternalInput").ap()
    coskF_h = nc.dram_tensor("coskF", [H, SEQ], BF16, kind="ExternalInput").ap()
    sinkF_h = nc.dram_tensor("sinkF", [H, SEQ], BF16, kind="ExternalInput").ap()
    vpermF_h = nc.dram_tensor("vpermF", [128, SEQ], BF16, kind="ExternalInput").ap()
    w_h = {nm: nc.dram_tensor(nm, [DM, DM], F32, kind="ExternalInput").ap()
           for nm in ("wfT", "wiT", "wcT", "woT", "uoT")}
    waoT_h = nc.dram_tensor("waoT", [H, DM], F32, kind="ExternalInput").ap()
    auxf_h = nc.dram_tensor("auxf", [128, NAUXF], F32, kind="ExternalInput").ap()
    auxb_h = nc.dram_tensor("auxb", [128, 2], BF16, kind="ExternalInput").ap()
    onesrow_h = nc.dram_tensor("onesrow", [1, 128], BF16,
                               kind="ExternalInput").ap()
    ones512_h = nc.dram_tensor("ones512", [1, 512], BF16,
                               kind="ExternalInput").ap()
    boa_h = nc.dram_tensor("boaRow", [1, DM], F32, kind="ExternalInput").ap()
    smalls_h = nc.dram_tensor("smalls", [128, NSMALL], F32, kind="ExternalInput").ap()
    out_h = nc.dram_tensor("out", [DM, TC], F32, kind="ExternalOutput").ap()

    es = contextlib.ExitStack()
    with es:
        E, sb = _common_sbuf(nc, es)
        ps_ = lambda n, sh: es.enter_context(nc.psum_tensor(n, sh, F32))
        E["smalls"] = sb("smalls_s", [128, NSMALL], F32)
        waoT = sb("wao_s", [H, DM], BF16)
        cosq = sb("cosq_sb", [H, TC], BF16)
        sinq = sb("sinq_sb", [H, TC], BF16)
        kbuf = [dict(ck=sb(f"ck{i}", [H, KG], BF16), sk=sb(f"sk{i}", [H, KG], BF16),
                     vp=sb(f"vp{i}", [128, KG], BF16)) for i in range(2)]
        x1s = [[sb(f"x1_0_{c}", [128, SW], F32) for c in range(NCH)]]
        ptiles = [sb(f"ptile{i}", [128, SW], BF16) for i in range(3)]
        pairs = [sb(f"pair{i}", [128, SW], BF16) for i in range(2)]
        quad = sb("quad", [128, SW], BF16)
        boa = sb("boa_s", [1, DM], F32)
        ones512_b = sb("ones512_b", [1, 512], BF16)
        boa_hi = sb("boa_hi", [1, DM], BF16)
        boa_lo = sb("boa_lo", [1, DM], BF16)
        dhi_b = pairs[0][0:1, :]
        dlo_b = pairs[1][0:1, :]
        out_sb = sb("out_sb", [H, SW], F32)
        rproj = sb("rproj", [H, SW], BF16)
        drow = sb("drow", [1, SW], F32)

        sA = ps_("sA", [128, SW])
        sB = ps_("sB", [128, SW])
        out_ps = ps_("out_ps", [128, SW])
        den_ps = ps_("den_ps", [128, SW])
        E["stats_ps"] = sA
        E["bc_ps"] = sB
        E["gsets"] = ((sA[:, 0:TB], sA[:, TB:2 * TB], sB[:, 0:TB]),
                      (sB[:, TB:2 * TB], out_ps[:, 0:TB], out_ps[:, TB:2 * TB]))
        E["hsets"] = (den_ps[:, 0:TB], den_ps[:, TB:2 * TB])

        sems = {k: es.enter_context(nc.semaphore(f"sem_{k}"))
                for k in ("lda", "lx0", "lk0", "lk1", "lw1", "lw2", "lw3",
                          "st0", "st1", "v", "s", "t")}
        prog = Prog()
        dma, op = _mk(prog, sems)

        dma("sync", "lda", E["auxf"][:, :], auxf_h)
        dma("sync", "lda", E["onesb"][:, :], auxb_h)
        dma("sync", "lda", E["onesrow_b"][:, :], onesrow_h)
        dma("sync", "lda", E["smalls"][:, :], smalls_h)
        dma("sync", "lda", cosq[:, :], cosq_h)
        dma("sync", "lda", sinq[:, :], sinq_h)
        dma("sync", "lda", ones512_b[:, :], ones512_h)
        dma("sync", "lda", boa[:, :], boa_h)
        AUX = [("lda", 128)]
        x1ld = {}
        for ch in range(NCH):
            x1ld[(0, ch)] = dma("sync", "lx0", x1s[0][ch][:, :],
                                x1_h[ch * 128:(ch + 1) * 128, 0:SW])
        for nm in ("wfT", "wiT", "wcT"):
            dma("gpsimd", "lw1", E[nm][:, :].rearrange("p (c i) -> p c i", c=NCH),
                w_h[nm].rearrange("(c p) i -> p c i", p=128))
        for nm in ("woT", "uoT"):
            dma("gpsimd", "lw2", E[nm][:, :].rearrange("p (c i) -> p c i", c=NCH),
                w_h[nm].rearrange("(c p) i -> p c i", p=128))
        dma("gpsimd", "lw3", waoT[:, :], waoT_h)
        E["h_weight_wait"] = [("lw2", 2 * 16)]
        gate_w_wait = [("lw1", 3 * 16)] + AUX

        op("vector", lambda e: e.tensor_copy(
            out=boa_hi[0:1, :], in_=boa[0:1, :]), wait=AUX)
        op("vector", lambda e: e.tensor_copy(
            out=quad[:, 0:56], in_=E["auxf"][:, 0:56]))
        op("vector", lambda e: e.tensor_copy(
            out=quad[:, 0:56], in_=E["auxf"][:, 0:56]))
        op("vector", lambda e: e.tensor_copy(
            out=drow[0:1, :], in_=boa_hi[0:1, :]))
        op("vector", lambda e: e.tensor_copy(
            out=quad[:, 0:56], in_=E["auxf"][:, 0:56]))
        t_bo = op("vector", lambda e: e.tensor_sub(
            out=boa_lo[0:1, :], in0=boa[0:1, :], in1=drow[0:1, :]), tick="v")
        bo_wait = [[("v", t_bo)]]

        NG = NTILE // 4
        kc = {}
        prev = {}
        E["trig_th"] = None
        for sw in range(NSW):
            cur = x1s[0]
            q0 = sw * SW
            if sw > 0:
                for ch in range(NCH):
                    w = None
                    if ch == 0:
                        w = [(f"st{(sw - 1) % 2}", prev[(sw - 1, "ost")])]
                    x1ld[(sw, ch)] = dma("sync", "lx0", cur[ch][:, :],
                                         x1_h[ch * 128:(ch + 1) * 128, q0:q0 + SW],
                                         wait=w)
            xp_t = {}
            for ch in range(NCH):
                xp_t[ch] = prog.ctr.get("v", 0)
            t_x1rdy = [("lx0", x1ld[(sw, NCH - 1)])] + AUX
            sweep_bar = [("v", prog.ctr.get("v", 0)), ("s", prog.ctr.get("s", 0))]
            exp_t = {}
            den_t = {}
            tile_meta = {}

            def emit_kload(g, wait):
                buf = kbuf[g % 2]
                gg = sw * NG + g
                k0 = g * KG
                lks = f"lk{gg % 2}"
                dma("sync", lks, buf["ck"][:, :], coskF_h[:, k0:k0 + KG], wait=wait)
                dma("sync", lks, buf["sk"][:, :], sinkF_h[:, k0:k0 + KG])
                lvp = dma("sync", lks, buf["vp"][:, :], vpermF_h[:, k0:k0 + KG])
                for ti in range(4):
                    tile_meta[g * 4 + ti] = (g * 4 + ti, buf, lks, lvp, ti)

            emit_kload(0, None if sw == 0 else [("t", kc[sw * NG - 2])])
            emit_kload(1, None if sw == 0 else [("t", kc[sw * NG - 1])])

            def emit_S(tile, buf, lks, lvp):
                Sp = sA if tile % 2 == 0 else sB
                c0, c1 = (tile % 4) * 128, (tile % 4 + 1) * 128
                w = [(lks, lvp)]
                if tile == 0:
                    w.append(("lw3", 16))
                    w += sweep_bar
                if tile >= 2:
                    w.append(("s", exp_t[tile - 2]))
                tS = None
                for half in range(2):
                    hc0 = half * 512
                    op("tensor", lambda e, buf=buf, c0=c0, c1=c1, hc0=hc0,
                       Sp=Sp: e.matmul(
                           out=Sp[:, hc0:hc0 + 512], lhsT=buf["ck"][:, c0:c1],
                           rhs=cosq[:, q0 + hc0:q0 + hc0 + 512],
                           start=True, stop=False),
                       wait=w if half == 0 else None)
                    last = half == 1
                    tS = op("tensor", lambda e, buf=buf, c0=c0, c1=c1, hc0=hc0,
                            Sp=Sp: e.matmul(
                                out=Sp[:, hc0:hc0 + 512], lhsT=buf["sk"][:, c0:c1],
                                rhs=sinq[:, q0 + hc0:q0 + hc0 + 512],
                                start=False, stop=True),
                            tick="t" if last else None)
                return tS, Sp

            pair_t = {}
            quad_t = {}

            def emit_exp(tile, tS, Sp):
                w = [("t", tS)]
                if tile >= 3:
                    w.append(("t", den_t[tile - 3]))
                    w.append(("v", pair_t[tile - 3]))
                pt = ptiles[tile % 3]
                exp_t[tile] = op("scalar", lambda e, Sp=Sp, pt=pt: e.activation(
                    out=pt[:, :], in_=Sp[:, :], func=AF.Exp,
                    bias=E["auxf"][:, AX_ZERO:AX_ZERO + 1],
                    scale=E["auxf"][:, AX_ISQ:AX_ISQ + 1]), tick="s", wait=w)

            def emit_pv(tile, buf):
                c0, c1 = (tile % 4) * 128, (tile % 4 + 1) * 128
                pt = ptiles[tile % 3]
                tden = None
                for half in range(2):
                    hc0 = half * 512
                    w2 = [("s", exp_t[tile])] if half == 0 else None
                    last = half == 1
                    tden = op("tensor", lambda e, buf=buf, c0=c0, c1=c1, hc0=hc0,
                              tile=tile, pt=pt: e.matmul(
                                  out=out_ps[:, hc0:hc0 + 512],
                                  lhsT=buf["vp"][:, c0:c1],
                                  rhs=pt[:, hc0:hc0 + 512], start=(tile == 0),
                                  stop=(tile == NTILE - 1)),
                              tick="t" if last else None, wait=w2)
                if tile % 2 == 1:
                    pair_t[tile] = pair_t[tile - 1] = op(
                        "vector", lambda e, tile=tile: e.tensor_add(
                            out=pairs[(tile // 2) % 2][:, :],
                            in0=ptiles[(tile - 1) % 3][:, :],
                            in1=ptiles[tile % 3][:, :]),
                        tick="v", wait=[("s", exp_t[tile])])
                else:
                    pair_t.setdefault(tile, prog.ctr.get("v", 0))
                if tile % 4 == 3:
                    tq4 = op("vector", lambda e: e.tensor_add(
                        out=quad[:, :], in0=pairs[0][:, :], in1=pairs[1][:, :]),
                        tick="v",
                        wait=[("t", quad_t[tile - 4])] if tile >= 7 else None)
                    for half in range(2):
                        hc0 = half * 512
                        last = half == 1
                        tden = op("tensor", lambda e, hc0=hc0, tile=tile: e.matmul(
                            out=den_ps[0:1, hc0:hc0 + 512],
                            lhsT=E["onesb"][:, 0:1], rhs=quad[:, hc0:hc0 + 512],
                            start=(tile == 3), stop=(tile == NTILE - 1)),
                            tick="t" if last else None,
                            wait=[("v", tq4)] if half == 0 else None)
                        quad_t[tile] = tden
                den_t[tile] = tden
                return tden

            pend = {}
            tile, buf, lks, lvp, _ = tile_meta[0]
            pend[0] = emit_S(tile, buf, lks, lvp)

            def do_pv(t):
                tden = emit_pv(t, tile_meta[t][1])
                if tile_meta[t][4] == 3:
                    g = tile_meta[t][0] // 4
                    kc[sw * NG + g] = tden
                    if g + 2 < NG:
                        emit_kload(g + 2, [("t", kc[sw * NG + g])])
                return tden

            for t in range(NTILE):
                emit_exp(t, *pend.pop(t))
                if t + 1 < NTILE:
                    tile, buf, lks, lvp, _ = tile_meta[t + 1]
                    pend[t + 1] = emit_S(tile, buf, lks, lvp)
                if t >= 1:
                    tden = do_pv(t - 1)
            tden = do_pv(NTILE - 1)
            t_att = tden
            op("vector", lambda e: e.tensor_copy(
                out=E["srow"][0:1, 0:SW], in_=den_ps[0:1, :]),
               wait=[("t", t_att)])
            op("vector", lambda e: e.tensor_copy(
                out=out_sb[0:1, :], in_=E["srow"][0:1, 0:SW]))
            op("vector", lambda e: e.tensor_copy(
                out=rproj[0:1, :], in_=E["srow"][0:1, 0:SW]))
            op("vector", lambda e: e.reciprocal(
                out=drow[0:1, :], in_=E["srow"][0:1, 0:SW]))
            op("vector", lambda e: e.tensor_copy(
                out=out_sb[0:1, :], in_=E["srow"][0:1, 0:SW]))
            op("vector", lambda e: e.tensor_copy(
                out=rproj[0:1, :], in_=E["srow"][0:1, 0:SW]))
            op("vector", lambda e: e.tensor_copy(
                out=dhi_b[0:1, :], in_=drow[0:1, :]))
            tdr = op("vector", lambda e: e.tensor_copy(out=out_sb[:, :],
                                                       in_=out_ps[:, :]), tick="v")
            op("vector", lambda e: e.tensor_copy(
                out=E["srow"][0:1, 0:SW], in_=dhi_b[0:1, :]))
            op("vector", lambda e: e.tensor_copy(
                out=quad[:, 0:56], in_=E["auxf"][:, 0:56]))
            op("vector", lambda e: e.tensor_copy(
                out=quad[:, 0:56], in_=E["auxf"][:, 0:56]))
            tden2 = op("vector", lambda e: e.tensor_sub(
                out=dlo_b[0:1, :], in0=drow[0:1, :],
                in1=E["srow"][0:1, 0:SW]), tick="v")
            for half in range(2):
                hc0 = half * 512
                op("tensor", lambda e, hc0=hc0: e.matmul(
                    out=sA[:, hc0:hc0 + 512], lhsT=E["onesrow_b"][0:1, :],
                    rhs=dhi_b[0:1, hc0:hc0 + 512], start=True, stop=False),
                    wait=[("v", tden2)] if half == 0 else None)
                tb = op("tensor", lambda e, hc0=hc0: e.matmul(
                    out=sA[:, hc0:hc0 + 512], lhsT=E["onesrow_b"][0:1, :],
                    rhs=dlo_b[0:1, hc0:hc0 + 512], start=False, stop=True),
                    tick="t" if half == 1 else None)
            trp = op("vector", lambda e: e.tensor_mul(
                out=rproj[:, :], in0=out_sb[:, :], in1=sA[:, :]), tick="v",
                wait=[("t", tb)])
            pr_t = {}
            for ch in range(NCH):
                pb = sB if ch % 2 == 0 else den_ps
                if ch >= 2:
                    op("tensor", lambda e: e.nop(), wait=[("v", pr_t[ch - 2])])
                for half in range(2):
                    hc0 = half * 512
                    w3 = [("v", trp)] if (ch == 0 and half == 0) else None
                    op("tensor", lambda e, ch=ch, hc0=hc0, pb=pb: e.matmul(
                        out=pb[:, hc0:hc0 + 512],
                        lhsT=waoT[:, ch * 128:(ch + 1) * 128],
                        rhs=rproj[:, hc0:hc0 + 512], start=True, stop=False),
                        wait=w3)
                    last = half == 1
                    op("tensor", lambda e, ch=ch, hc0=hc0, pb=pb: e.matmul(
                        out=pb[:, hc0:hc0 + 512],
                        lhsT=boa_hi[0:1, ch * 128:(ch + 1) * 128],
                        rhs=ones512_b[0:1, :], start=False, stop=False),
                        wait=bo_wait.pop() if bo_wait else None)
                    tpm = op("tensor", lambda e, ch=ch, hc0=hc0, pb=pb,
                             last=last: e.matmul(
                        out=pb[:, hc0:hc0 + 512],
                        lhsT=boa_lo[0:1, ch * 128:(ch + 1) * 128],
                        rhs=ones512_b[0:1, :], start=False, stop=True),
                        tick="t" if last else None)
                pr_t[ch] = op("vector", lambda e, ch=ch, cur=cur, pb=pb: e.tensor_add(
                    out=cur[ch][:, :], in0=cur[ch][:, :], in1=pb[:, :]), tick="v",
                    wait=[("t", tpm)] + t_x1rdy)
            stats_prev = None
            for bb in range(SW // TB):
                xsrc = [cur[ch][:, bb * TB:(bb + 1) * TB] for ch in range(NCH)]
                wfst = [("v", pr_t[NCH - 1]), ("t", prog.ctr.get("t", 0))]
                if stats_prev is not None:
                    wfst.append(("t", stats_prev))
                y_t, stats_prev = _ln_emit(op, E, xsrc, wfst)
                x_t = _ffn_emit(op, E, y_t, xsrc,
                                gate_w_wait if (sw == 0 and bb == 0) else None)
                for ch in range(NCH):
                    v = dma("gpsimd", f"st{sw % 2}",
                            out_h[ch * 128:(ch + 1) * 128,
                                  q0 + bb * TB:q0 + (bb + 1) * TB],
                            xsrc[ch], wait=[("v", x_t[ch])])
            prev[(sw, "ost")] = v

        def _fin(e):
            for cn in ("st0", "st1"):
                e.wait_ge(sems[cn], prog.ctr.get(cn, 0))
        prog.emit("gpsimd", _fin)
        _replay(nc, es, prog)
    return nc


def _prep_host(inputs):
    g = lambda k: np.ascontiguousarray(np.asarray(inputs[k], dtype=np.float32))
    auxf = np.zeros((128, NAUXF), np.float32)
    auxf[:, AX_WAV] = g("L0_attn_wavelengths")
    auxf[:, AX_PHASE] = g("L0_attn_phase_bias")
    auxf[:, AX_BQ] = g("L0_attn_bq")
    auxf[:, AX_BK] = g("L0_attn_bk")
    auxf[:, AX_BF:AX_BF + 8] = g("L0_ffn_bf").reshape(8, 128).T
    auxf[:, AX_BI:AX_BI + 8] = g("L0_ffn_bi").reshape(8, 128).T
    auxf[:, AX_BC:AX_BC + 8] = g("L0_ffn_bc").reshape(8, 128).T
    auxf[:, AX_BO_F:AX_BO_F + 8] = g("L0_ffn_bo").reshape(8, 128).T
    auxf[:, AX_INC:AX_INC + 8] = g("L0_ffn_in_c").reshape(8, 128).T
    auxf[:, AX_BO_A:AX_BO_A + 8] = g("L0_attn_bo").reshape(8, 128).T
    auxf[:, AX_ISQ] = 1.0 / math.sqrt(H)
    auxf[:, AX_EPS] = 1e-5
    auxf[:, AX_ZERO] = 0.0
    auxb = np.zeros((128, 2), BF)
    auxb[:, 0] = 1.0
    onesrow = np.ones((1, 128), BF)
    bvbc = np.ascontiguousarray(np.broadcast_to(g("L0_attn_bv")[None, :], (128, 128)))
    T = lambda a: np.ascontiguousarray(a.T)
    com = {"wfT": T(g("L0_ffn_wf")), "wiT": T(g("L0_ffn_wi")),
           "wcT": T(g("L0_ffn_wc")), "woT": T(g("L0_ffn_wo")),
           "uoT": T(g("L0_ffn_uo")), "auxf": auxf, "auxb": auxb,
           "onesrow": onesrow}
    a_only = {"ufT": T(g("L0_ffn_uf")), "uiT": T(g("L0_ffn_ui")),
              "ucT": T(g("L0_ffn_uc")),
              "wqT": T(g("L0_attn_Wq")), "wkT": T(g("L0_attn_Wk")),
              "wvT": T(g("L0_attn_Wv")), "bvbc": bvbc}
    b_only = {"waoT": T(g("L0_attn_Wo")),
              "ones512": np.ones((1, 512), BF),
              "boaRow": np.ascontiguousarray(g("L0_attn_bo")[None, :])}
    xT = np.ascontiguousarray(g("L0_attn_x").reshape(SEQ, DM).T)
    return com, a_only, b_only, xT


_NC_CACHE = {}


def kernel(**inputs):
    com, a_only, b_only, xT = _prep_host(inputs)
    if "a" not in _NC_CACHE:
        _NC_CACHE["a"] = build_launch_a()
        _NC_CACHE["b"] = build_launch_b()
    nc_a, nc_b = _NC_CACHE["a"], _NC_CACHE["b"]

    in_maps_a = []
    for c in range(NCORES):
        m = dict(com)
        m.update(a_only)
        m["x"] = np.ascontiguousarray(xT[:, c * TC:(c + 1) * TC])
        in_maps_a.append(m)
    res_a = run_bass_kernel_spmd(nc_a, in_maps_a, core_ids=list(range(NCORES)))

    coskF = np.concatenate([res_a.results[c]["cosk"] for c in range(NCORES)], axis=1)
    sinkF = np.concatenate([res_a.results[c]["sink"] for c in range(NCORES)], axis=1)
    vpermF = np.concatenate([res_a.results[c]["vperm"] for c in range(NCORES)], axis=1)

    in_maps_b = []
    for c in range(NCORES):
        m = dict(com)
        m.update(b_only)
        m["x1"] = res_a.results[c]["x1"]
        m["cosq"] = res_a.results[c]["cosq"]
        m["sinq"] = res_a.results[c]["sinq"]
        m["coskF"] = coskF
        m["sinkF"] = sinkF
        m["vpermF"] = vpermF
        m["smalls"] = res_a.results[c]["smalls"]
        in_maps_b.append(m)
    res_b = run_bass_kernel_spmd(nc_b, in_maps_b, core_ids=list(range(NCORES)))

    out = np.concatenate([res_b.results[c]["out"] for c in range(NCORES)], axis=1)
    return np.ascontiguousarray(out.T).reshape(1, SEQ, DM)


# version 09
# speedup vs baseline: 1.0848x; 1.0050x over previous
import contextlib
import math

import numpy as np
import ml_dtypes

import concourse.bass as bass
import concourse.mybir as mybir
from concourse.bass_utils import run_bass_kernel_spmd

F32 = mybir.dt.float32
_DBG_VV = False
BF16 = mybir.dt.bfloat16
I32 = mybir.dt.int32
BF = ml_dtypes.bfloat16
AF = mybir.ActivationFunctionType
ALU = mybir.AluOpType

NCORES = 8
DM = 1024
H = 128
NCH = DM // 128
TC = 2048
TB = 512
NB = TC // TB
SEQ = NCORES * TC
NTILE = SEQ // 128
SW = 1024
NSW = TC // SW
KG = 512

_TPI = 2 * math.pi
TWO_PI_HI = float(np.float32(_TPI))
TWO_PI_MID = float(np.float32(_TPI - TWO_PI_HI))
TWO_PI_LO = float(_TPI - TWO_PI_HI - TWO_PI_MID)
PI = float(np.float32(math.pi))
HALF_PI = float(np.float32(math.pi / 2))

AX_WAV, AX_PHASE, AX_BQ, AX_BK = 0, 1, 2, 3
AX_BF, AX_BI, AX_BC, AX_BO_F, AX_INC, AX_BO_A = 4, 12, 20, 28, 36, 44
AX_ISQ, AX_EPS, AX_ZERO = 52, 53, 54
NAUXF = 56

SM_GBF, SM_GBI, SM_GBC, SM_INVW = 0, 8, 16, 24
NSMALL = 25
SM_QB, SM_KB = 25, 26


class Prog:

    def __init__(self):
        self.p = {k: [] for k in ("sync", "gpsimd", "vector", "scalar", "tensor")}
        self.ctr = {}

    def emit(self, eng, fn):
        self.p[eng].append(fn)


def _mk(prog, sems):
    def w_emit(e, wait):
        if wait:
            for (cn, wv) in wait:
                e.wait_ge(sems[cn], wv)

    def dma(queue, cn, out_ap, in_ap, wait=None, qn=None):
        prog.ctr[cn] = prog.ctr.get(cn, 0) + 16
        val = prog.ctr[cn]

        def f(e, out_ap=out_ap, in_ap=in_ap, wait=wait):
            w_emit(e, wait)
            e.dma_start(out=out_ap, in_=in_ap).then_inc(sems[cn], 16)

        prog.emit(queue, f)
        return val

    def op(eng, fn, tick=None, wait=None):
        val = None
        if tick is not None:
            prog.ctr[tick] = prog.ctr.get(tick, 0) + 1
            val = prog.ctr[tick]

        def f(e, fn=fn, wait=wait, tick=tick):
            w_emit(e, wait)
            ins = fn(e)
            if tick is not None:
                ins.then_inc(sems[tick], 1)

        prog.emit(eng, f)
        return val

    return dma, op


def _ln_stats(op, E, xsrc, wait_first_cast):
    xb, x2b = E["xb"], E["x2b"]
    stats = E["stats_ps"]
    onesb = E["onesb"]
    cast_t = []
    for ch in range(NCH):
        w = wait_first_cast if ch == 0 else None
        op("vector", lambda e, ch=ch: e.tensor_copy(out=xb[ch][:, :], in_=xsrc[ch]),
           wait=w)
        t = op("vector", lambda e, ch=ch: e.tensor_mul(
            out=x2b[ch][:, :], in0=xb[ch][:, :], in1=xb[ch][:, :]), tick="v")
        cast_t.append(t)
    for ch in range(NCH):
        op("tensor", lambda e, ch=ch: e.matmul(
            out=stats[0:1, 0:TB], lhsT=onesb[:, 0:1], rhs=xb[ch][:, :],
            start=(ch == 0), stop=(ch == NCH - 1)),
           wait=[("v", cast_t[ch])])
        last = (ch == NCH - 1)
        t_stats = op("tensor", lambda e, ch=ch, last=last: e.matmul(
            out=stats[0:1, TB:2 * TB], lhsT=onesb[:, 0:1], rhs=x2b[ch][:, :],
            start=(ch == 0), stop=last),
            tick="t" if last else None)
    return t_stats


def _ln_hoist_mk(op, E, xsrc):
    xb = E["xb"]
    stats = E["stats_ps"]
    onesb = E["onesb"]
    x2buf = (E["ftile"], E["itile"], E["gtile"], E["gtmp"],
             E["m_bc"], E["z_bc"])
    st = {"sq": {}}

    def do_ch(ch, buf, wait=None):
        op("vector", lambda e, ch=ch: e.tensor_copy(
            out=xb[ch][:, :], in_=xsrc[ch]))
        st["sq"][ch] = op("vector", lambda e, ch=ch, buf=buf: e.tensor_mul(
            out=buf[:, :], in0=xb[ch][:, :], in1=xb[ch][:, :]), tick="v",
            wait=wait)

    def hook(ob):
        if 1 <= ob <= 6:
            do_ch(ob - 1, x2buf[ob - 1])

    def mm_pair(ch, buf, w, tick_x2=False):
        op("tensor", lambda e, ch=ch: e.matmul(
            out=stats[0:1, 0:TB], lhsT=onesb[:, 0:1], rhs=xb[ch][:, :],
            start=(ch == 0), stop=(ch == NCH - 1)), wait=w)
        last = (ch == NCH - 1)
        return op("tensor", lambda e, ch=ch, buf=buf, last=last: e.matmul(
            out=stats[0:1, TB:2 * TB], lhsT=onesb[:, 0:1],
            rhs=buf[:, :], start=(ch == 0), stop=last),
            tick="t" if (last or tick_x2) else None)

    def finalize():
        mmt = {}
        for ch in range(6):
            mmt[ch] = mm_pair(ch, x2buf[ch], [("v", st["sq"][ch])],
                              tick_x2=(ch < 2))
        do_ch(6, x2buf[0], wait=[("t", mmt[0])])
        do_ch(7, x2buf[1], wait=[("t", mmt[1])])
        t_stats = None
        for ch in (6, 7):
            t_stats = mm_pair(ch, x2buf[ch - 6], [("v", st["sq"][ch])])
        return t_stats

    return hook, finalize


def _ln_finish(op, E, t_stats):
    yb = E["yb"]
    xb = E["xb"]
    stats, bc = E["stats_ps"], E["bc_ps"]
    srow, mrow, zrow = E["srow"], E["mrow"], E["zrow"]
    m_bc, z_bc = E["m_bc"], E["z_bc"]
    auxf = E["auxf"]
    op("vector", lambda e: e.tensor_scalar(
        out=mrow[0:1, :], in0=stats[0:1, 0:TB], scalar1=1.0 / DM, scalar2=None,
        op0=ALU.mult), wait=[("t", t_stats)])
    op("vector", lambda e: e.tensor_scalar(
        out=srow[0:1, 0:TB], in0=stats[0:1, TB:2 * TB], scalar1=1.0 / DM,
        scalar2=None, op0=ALU.mult))
    op("vector", lambda e: e.tensor_mul(
        out=srow[0:1, TB:2 * TB], in0=mrow[0:1, :], in1=mrow[0:1, :]))
    tv = op("vector", lambda e: e.tensor_sub(
        out=srow[0:1, 0:TB], in0=srow[0:1, 0:TB], in1=srow[0:1, TB:2 * TB]),
        tick="v")
    ts0 = op("scalar", lambda e: e.activation(
        out=srow[0:1, TB:2 * TB], in_=srow[0:1, 0:TB], func=AF.Sqrt,
        bias=auxf[0:1, AX_EPS:AX_EPS + 1]), tick="s", wait=[("v", tv)])
    op("vector", lambda e: e.tensor_scalar(
        out=srow[0:1, 2 * TB:3 * TB], in0=srow[0:1, 0:TB], scalar1=1e-5,
        scalar2=None, op0=ALU.add), wait=[("s", ts0)])
    op("vector", lambda e: e.reciprocal(out=zrow[0:1, :], in_=srow[0:1, TB:2 * TB]))
    op("vector", lambda e: e.tensor_copy(out=m_bc[0:1, :], in_=mrow[0:1, :]))
    op("vector", lambda e: e.tensor_copy(out=z_bc[0:1, :], in_=mrow[0:1, :]))
    op("vector", lambda e: e.tensor_mul(
        out=srow[0:1, 0:TB], in0=zrow[0:1, :], in1=zrow[0:1, :]))
    op("vector", lambda e: e.tensor_mul(
        out=srow[0:1, 0:TB], in0=srow[0:1, 0:TB], in1=srow[0:1, 2 * TB:3 * TB]))
    op("vector", lambda e: e.tensor_scalar(
        out=srow[0:1, 0:TB], in0=srow[0:1, 0:TB], scalar1=-0.5, scalar2=1.5,
        op0=ALU.mult, op1=ALU.add))
    tz = op("vector", lambda e: e.tensor_mul(
        out=zrow[0:1, :], in0=zrow[0:1, :], in1=srow[0:1, 0:TB]), tick="v")
    op("vector", lambda e: e.tensor_copy(
        out=E["mrow_b"][0:1, :], in_=mrow[0:1, :]))
    tzb = op("vector", lambda e: e.tensor_copy(
        out=E["zrow_b"][0:1, :], in_=zrow[0:1, :]), tick="v")
    op("tensor", lambda e: e.matmul(
        out=bc[:, 0:TB], lhsT=E["onesrow_b"][0:1, :], rhs=E["mrow_b"][0:1, :],
        start=True, stop=True), wait=[("v", tzb)])
    tb = op("tensor", lambda e: e.matmul(
        out=bc[:, TB:2 * TB], lhsT=E["onesrow_b"][0:1, :],
        rhs=E["zrow_b"][0:1, :], start=True, stop=True), tick="t")
    op("vector", lambda e: e.tensor_copy(out=m_bc[:, :], in_=bc[:, 0:TB]),
       wait=[("t", tb)])
    op("vector", lambda e: e.tensor_copy(out=z_bc[:, :], in_=bc[:, TB:2 * TB]))
    y_t = []
    for ch in range(NCH):
        op("vector", lambda e, ch=ch: e.tensor_sub(
            out=yb[ch][:, :], in0=xb[ch][:, :], in1=m_bc[:, :]))
        t = op("vector", lambda e, ch=ch: e.tensor_mul(
            out=yb[ch][:, :], in0=yb[ch][:, :], in1=z_bc[:, :]), tick="v")
        y_t.append(t)
    return y_t


def _ln_emit(op, E, xsrc, wait_first_cast):
    t_stats = _ln_stats(op, E, xsrc, wait_first_cast)
    y_t = _ln_finish(op, E, t_stats)
    return y_t, t_stats


def _ffn_emit(op, E, y_ticks, xdst, extra_gate_wait, ln_hook=None):
    yb, cb = E["yb"], E["cb"]
    ftile, itile, gtile, gtmp = E["ftile"], E["itile"], E["gtile"], E["gtmp"]
    th32 = E["th32"]
    wf, wi, wc, wo, uo = E["wfT"], E["wiT"], E["wcT"], E["woT"], E["uoT"]
    auxf, smalls = E["auxf"], E["smalls"]
    gsets, hsets = E["gsets"], E["hsets"]

    act_t = {}
    c_t = {}
    for ob in range(8):
        pf, pi_, pg = gsets[ob % 2]
        if ob >= 2:
            op("tensor", lambda e: e.nop(), wait=[("s", act_t[ob - 2])])
        for ch in range(NCH):
            w = [("v", y_ticks[ch])]
            if ob == 0 and ch == 0 and extra_gate_wait:
                w = w + extra_gate_wait
            op("tensor", lambda e, ch=ch, ob=ob, pf=pf: e.matmul(
                out=pf, lhsT=wf[:, ch * DM + ob * 128:ch * DM + (ob + 1) * 128],
                rhs=yb[ch][:, :], start=(ch == 0), stop=(ch == NCH - 1)),
               wait=w if ob == 0 else (w if ch == 0 and ob == 0 else None))
        for ch in range(NCH):
            op("tensor", lambda e, ch=ch, ob=ob, pi_=pi_: e.matmul(
                out=pi_, lhsT=wi[:, ch * DM + ob * 128:ch * DM + (ob + 1) * 128],
                rhs=yb[ch][:, :], start=(ch == 0), stop=(ch == NCH - 1)))
        for ch in range(NCH):
            last = ch == NCH - 1
            tmm = op("tensor", lambda e, ch=ch, ob=ob, pg=pg, last=last: e.matmul(
                out=pg, lhsT=wc[:, ch * DM + ob * 128:ch * DM + (ob + 1) * 128],
                rhs=yb[ch][:, :], start=(ch == 0), stop=last),
                tick="t" if last else None)
        aw = [("t", tmm)]
        if ob >= 1:
            aw.append(("v", c_t[ob - 1]))
        if ob == 0 and E.get("act_extra_wait"):
            aw = aw + E.pop("act_extra_wait")
        op("scalar", lambda e, ob=ob, pf=pf: e.activation(
            out=ftile[:, :], in_=pf, func=AF.Sigmoid,
            bias=smalls[:, SM_GBF + ob:SM_GBF + ob + 1]), wait=aw)
        op("scalar", lambda e, ob=ob, pi_=pi_: e.activation(
            out=itile[:, :], in_=pi_, func=AF.Sigmoid,
            bias=smalls[:, SM_GBI + ob:SM_GBI + ob + 1]))
        ta = op("scalar", lambda e, ob=ob, pg=pg: e.activation(
            out=gtile[:, :], in_=pg, func=AF.Tanh,
            bias=smalls[:, SM_GBC + ob:SM_GBC + ob + 1]), tick="s")
        act_t[ob] = ta
        op("vector", lambda e, ob=ob: e.tensor_scalar(
            out=cb[ob][:, :], in0=ftile[:, :],
            scalar1=auxf[:, AX_INC + ob:AX_INC + ob + 1], scalar2=None,
            op0=ALU.mult), wait=[("s", ta)])
        op("vector", lambda e: e.tensor_mul(
            out=gtmp[:, :], in0=itile[:, :], in1=gtile[:, :]))
        c_t[ob] = op("vector", lambda e, ob=ob: e.tensor_add(
            out=cb[ob][:, :], in0=cb[ob][:, :], in1=gtmp[:, :]), tick="v")
        if ob == 1 and E.get("ffn_mid_hook"):
            E.pop("ffn_mid_hook")()

    h_t = {}
    x_t = {}
    for ob in range(8):
        ph = hsets[ob % 2]
        if ob >= 2:
            op("tensor", lambda e: e.nop(), wait=[("s", h_t[ob - 2])])
        for ch in range(NCH):
            w = None
            if ob == 0 and ch == 0:
                w = [("v", c_t[7])] + (E.get("h_weight_wait") or [])
            op("tensor", lambda e, ch=ch, ob=ob, ph=ph: e.matmul(
                out=ph, lhsT=wo[:, ch * DM + ob * 128:ch * DM + (ob + 1) * 128],
                rhs=yb[ch][:, :], start=(ch == 0), stop=False), wait=w)
        for ch in range(NCH):
            last = ch == NCH - 1
            tmm = op("tensor", lambda e, ch=ch, ob=ob, ph=ph, last=last: e.matmul(
                out=ph, lhsT=uo[:, ch * DM + ob * 128:ch * DM + (ob + 1) * 128],
                rhs=cb[ch][:, :], start=False, stop=last),
                tick="t" if last else None)
        hw = [("t", tmm)]
        if ob >= 2:
            hw.append(("v", x_t[ob - 2]))
        tth = op("scalar", lambda e, ob=ob, ph=ph: e.activation(
            out=th32[ob % 2][:, :], in_=ph, func=AF.Tanh,
            bias=auxf[:, AX_BO_F + ob:AX_BO_F + ob + 1]), tick="s", wait=hw)
        h_t[ob] = tth
        x_t[ob] = op("vector", lambda e, ob=ob: e.scalar_tensor_tensor(
            out=xdst[ob], in0=th32[ob % 2][:, :], scalar=0.5, in1=xdst[ob],
            op0=ALU.mult, op1=ALU.add), tick="v", wait=[("s", tth)])
        if ln_hook is not None:
            ln_hook(ob)
    return x_t


def _trig_emit(op, E, src_ps_ap, bias_col, cos_out, sin_out, wait_src):
    th, red, kf, ki = E["trig_th"], E["trig_red"], E["trig_kf"], E["trig_ki"]
    smalls, auxf = E["smalls"], E["auxf"]
    w0 = list(wait_src)
    if E.get("trig_last_s"):
        w0.append(("s", E["trig_last_s"]))
    op("vector", lambda e: e.tensor_copy(out=th[:, :], in_=src_ps_ap), wait=w0)
    op("vector", lambda e: e.tensor_scalar(
        out=red[:, :], in0=th[:, :], scalar1=auxf[:, bias_col:bias_col + 1],
        scalar2=None, op0=ALU.add))
    op("vector", lambda e: e.tensor_scalar(
        out=th[:, :], in0=red[:, :], scalar1=smalls[:, SM_INVW:SM_INVW + 1],
        scalar2=auxf[:, AX_PHASE:AX_PHASE + 1], op0=ALU.mult, op1=ALU.add))
    op("vector", lambda e: e.tensor_scalar(
        out=red[:, :], in0=th[:, :], scalar1=-1.0 / _TPI, scalar2=None,
        op0=ALU.mult))
    op("vector", lambda e: e.tensor_copy(out=ki[:, :], in_=red[:, :]))
    op("vector", lambda e: e.tensor_copy(out=kf[:, :], in_=ki[:, :]))
    op("vector", lambda e: e.scalar_tensor_tensor(
        out=red[:, :], in0=kf[:, :], scalar=TWO_PI_HI, in1=th[:, :],
        op0=ALU.mult, op1=ALU.add))
    op("vector", lambda e: e.scalar_tensor_tensor(
        out=red[:, :], in0=kf[:, :], scalar=TWO_PI_MID, in1=red[:, :],
        op0=ALU.mult, op1=ALU.add))
    op("vector", lambda e: e.scalar_tensor_tensor(
        out=red[:, :], in0=kf[:, :], scalar=TWO_PI_LO, in1=red[:, :],
        op0=ALU.mult, op1=ALU.add))
    tsin = op("vector", lambda e: e.tensor_scalar(
        out=th[:, :], in0=red[:, :], scalar1=PI, scalar2=-PI,
        op0=ALU.min, op1=ALU.max), tick="v")
    tss = op("scalar", lambda e: e.activation(
        out=sin_out, in_=th[:, :], func=AF.Sin,
        bias=E["auxf"][:, AX_ZERO:AX_ZERO + 1]), tick="s", wait=[("v", tsin)])
    op("vector", lambda e: e.tensor_scalar(
        out=kf[:, :], in0=th[:, :], scalar1=HALF_PI, scalar2=None,
        op0=ALU.is_gt), wait=[("s", tss)])
    op("vector", lambda e: e.scalar_tensor_tensor(
        out=red[:, :], in0=kf[:, :], scalar=-TWO_PI_HI, in1=th[:, :],
        op0=ALU.mult, op1=ALU.add))
    op("vector", lambda e: e.tensor_scalar(
        out=kf[:, :], in0=red[:, :], scalar1=HALF_PI, scalar2=PI,
        op0=ALU.add, op1=ALU.min))
    tcos = op("vector", lambda e: e.tensor_scalar(
        out=red[:, :], in0=kf[:, :], scalar1=-PI, scalar2=None, op0=ALU.max),
        tick="v")
    tcs = op("scalar", lambda e: e.activation(
        out=cos_out, in_=red[:, :], func=AF.Sin,
        bias=E["auxf"][:, AX_ZERO:AX_ZERO + 1]), tick="s", wait=[("v", tcos)])
    E["trig_last_s"] = tcs
    return tcs


def _replay(nc, es, prog):
    block = es.enter_context(nc.Block())

    @block.sync
    def _(e):
        for f in prog.p["sync"]:
            f(e)

    @block.gpsimd
    def _(e):
        for f in prog.p["gpsimd"]:
            f(e)

    @block.vector
    def _(e):
        for f in prog.p["vector"]:
            f(e)

    @block.scalar
    def _(e):
        for f in prog.p["scalar"]:
            f(e)

    @block.tensor
    def _(e):
        for f in prog.p["tensor"]:
            f(e)


def _common_sbuf(nc, es):
    sb = lambda n, sh, dt: es.enter_context(nc.sbuf_tensor(n, sh, dt))
    E = {}
    for nm in ("wfT", "wiT", "wcT", "woT", "uoT"):
        E[nm] = sb(nm + "_s", [128, NCH * DM], BF16)
    E["xb"] = [sb(f"xb{c}", [128, TB], BF16) for c in range(NCH)]
    E["yb"] = [sb(f"y{c}", [128, TB], BF16) for c in range(NCH)]
    E["cb"] = [sb(f"c{c}", [128, TB], BF16) for c in range(NCH)]
    E["x2b"] = E["cb"]
    E["ftile"] = sb("ftile", [128, TB], BF16)
    E["itile"] = sb("itile", [128, TB], BF16)
    E["gtile"] = sb("gtile", [128, TB], BF16)
    E["gtmp"] = sb("gtmp", [128, TB], BF16)
    E["th32"] = [sb(f"th32_{i}", [128, TB], F32) for i in range(2)]
    E["m_bc"] = sb("m_bc", [128, TB], BF16)
    E["z_bc"] = sb("z_bc", [128, TB], BF16)
    E["srow"] = sb("srow", [1, 3 * TB], F32)
    E["mrow"] = sb("mrow", [1, TB], F32)
    E["zrow"] = sb("zrow", [1, TB], F32)
    E["auxf"] = sb("auxf_s", [128, NAUXF], F32)
    E["onesb"] = sb("auxb_s", [128, 2], BF16)
    E["onesrow_b"] = sb("onesrow_b", [1, 128], BF16)
    E["mrow_b"] = sb("mrow_b", [1, TB], BF16)
    E["zrow_b"] = sb("zrow_b", [1, TB], BF16)
    return E, sb


def build_launch_a(n_blocks=NB, do_ffn=True, do_ln2=True, do_qkv=True, do_v=True):
    nc = bass.Bass("TRN2", target_bir_lowering=False,
                   detect_race_conditions=False)
    x_h = nc.dram_tensor("x", [DM, TC], F32, kind="ExternalInput").ap()
    w_h = {nm: nc.dram_tensor(nm, [DM, DM], F32, kind="ExternalInput").ap()
           for nm in ("wfT", "wiT", "wcT", "woT", "uoT", "ufT", "uiT", "ucT")}
    wq_h = {nm: nc.dram_tensor(nm, [DM, H], F32, kind="ExternalInput").ap()
            for nm in ("wqT", "wkT", "wvT")}
    auxf_h = nc.dram_tensor("auxf", [128, NAUXF], F32, kind="ExternalInput").ap()
    auxb_h = nc.dram_tensor("auxb", [128, 2], BF16, kind="ExternalInput").ap()
    onesrow_h = nc.dram_tensor("onesrow", [1, 128], BF16,
                               kind="ExternalInput").ap()
    bvbc_h = nc.dram_tensor("bvbc", [128, 128], F32, kind="ExternalInput").ap()
    x1_h = nc.dram_tensor("x1", [DM, TC], F32, kind="ExternalOutput").ap()
    cosq_h = nc.dram_tensor("cosq", [H, TC], BF16, kind="ExternalOutput").ap()
    sinq_h = nc.dram_tensor("sinq", [H, TC], BF16, kind="ExternalOutput").ap()
    cosk_h = nc.dram_tensor("cosk", [H, TC], BF16, kind="ExternalOutput").ap()
    sink_h = nc.dram_tensor("sink", [H, TC], BF16, kind="ExternalOutput").ap()
    vperm_h = nc.dram_tensor("vperm", [128, TC], BF16, kind="ExternalOutput").ap()
    smalls_h = nc.dram_tensor("smalls", [128, NSMALL], F32, kind="ExternalOutput").ap()
    if _DBG_VV:
        vvdump_h = nc.dram_tensor("vvdump", [128, 544], F32,
                                  kind="ExternalOutput").ap()

    es = contextlib.ExitStack()
    with es:
        E, sb = _common_sbuf(nc, es)
        ps_ = lambda n, sh: es.enter_context(nc.psum_tensor(n, sh, F32))
        E["smalls"] = sb("smalls_s", [128, 28], F32)
        wq_s = {nm: [sb(f"{nm}{c}", [128, H], BF16) for c in range(NCH)]
                for nm in ("wqT", "wkT", "wvT")}
        u_stage = [sb(f"ust{i}", [128, DM], F32) for i in range(2)]
        xblk = [[sb(f"x{i}_{c}", [128, TB], F32) for c in range(NCH)]
                for i in range(2)]
        bvbc = sb("bvbc_s", [128, 128], F32)

        E["trig_th"] = sb("trig_th", [128, TB], F32)
        E["trig_red"] = sb("trig_red", [128, TB], F32)
        E["trig_kf"] = sb("trig_kf", [128, TB], F32)
        E["trig_ki"] = sb("trig_ki", [128, TB], I32)
        cosq_s = sb("cosq_sb", [H, TC], BF16)
        sinq_s = sb("sinq_sb", [H, TC], BF16)
        cosk_s = sb("cosk_sb", [H, TC], BF16)
        sink_s = sb("sink_sb", [H, TC], BF16)
        vperm_s = sb("vperm_sb", [128, TC], BF16)

        stats_ps = ps_("stats", [128, 2 * TB])
        bc_ps = ps_("bcst", [128, 2 * TB])
        gA0 = ps_("gA0", [128, TB])
        gB0 = ps_("gB0", [128, TB])
        vv_ps = ps_("vv", [128, TB])
        E["stats_ps"] = stats_ps
        E["bc_ps"] = bc_ps
        E["gsets"] = ((gA0[:, :], stats_ps[:, 0:TB], stats_ps[:, TB:2 * TB]),
                      (gB0[:, :], bc_ps[:, 0:TB], bc_ps[:, TB:2 * TB]))
        E["hsets"] = (gA0[:, :], gB0[:, :])
        qk_ps = bc_ps
        v_ps = stats_ps

        sems = {k: es.enter_context(nc.semaphore(f"sem_{k}"))
                for k in ("lda", "lx0", "lx1", "lu0", "lu1", "lw1", "lw2",
                          "lw3", "st0", "st1", "stf", "v", "s", "t")}
        prog = Prog()
        dma, op = _mk(prog, sems)

        dma("sync", "lda", E["auxf"][:, :], auxf_h)
        dma("sync", "lda", E["onesb"][:, :], auxb_h)
        dma("sync", "lda", E["onesrow_b"][:, :], onesrow_h)
        dma("sync", "lda", bvbc[:, :], bvbc_h)
        AUX = [("lda", 64)]
        xld = {}
        for ch in range(NCH):
            xld[(0, ch)] = dma("sync", "lx0", xblk[0][ch][:, :],
                               x_h[ch * 128:(ch + 1) * 128, 0:TB])
        u_t = {}

        def emit_uchain():
            tick_last = {}
            for idx in range(24):
                mi, ic = divmod(idx, NCH)
                buf = u_stage[idx % 2]
                lus = f"lu{idx % 2}"
                w = [("t", tick_last[idx - 2])] if idx >= 2 else None
                lv = dma("sync", lus, buf[:, :],
                         w_h[("ufT", "uiT", "ucT")[mi]][ic * 128:(ic + 1) * 128, :],
                         wait=w)
                for oc in range(NCH):
                    last = oc == NCH - 1
                    col = ((mi * 8 + oc) * 8) + ic
                    tick_last[idx] = op(
                        "tensor", lambda e, buf=buf, oc=oc, ic=ic, col=col: e.matmul(
                            out=vv_ps[:, col:col + 1],
                            lhsT=buf[:, oc * 128:(oc + 1) * 128],
                            rhs=E["auxf"][:, AX_INC + ic:AX_INC + ic + 1],
                            start=True, stop=True, skip_group_check=True),
                        tick="t" if last else None,
                        wait=([(lus, lv)] + AUX) if oc == 0 else None)
            tt = E["trig_th"]
            op("vector", lambda e: e.tensor_copy(
                out=tt[:, 0:192], in_=vv_ps[:, 0:192]),
                wait=[("t", tick_last[23])])
            t192 = tt[:, 0:192].rearrange("p (c two) -> p c two", two=2)
            op("vector", lambda e: e.tensor_add(
                out=tt[:, 192:288], in0=t192[:, :, 0], in1=t192[:, :, 1]))
            t96 = tt[:, 192:288].rearrange("p (c two) -> p c two", two=2)
            op("vector", lambda e: e.tensor_copy(
                out=E["trig_red"][:, 64:120], in_=E["auxf"][:, 0:56]))
            op("vector", lambda e: e.tensor_add(
                out=tt[:, 288:336], in0=t96[:, :, 0], in1=t96[:, :, 1]))
            t48 = tt[:, 288:336].rearrange("p (c two) -> p c two", two=2)
            op("vector", lambda e: e.tensor_copy(
                out=E["trig_red"][:, 64:120], in_=E["auxf"][:, 0:56]))
            op("vector", lambda e: e.tensor_copy(
                out=E["trig_red"][:, 64:120], in_=E["auxf"][:, 0:56]))
            t_ev = op("vector", lambda e: e.tensor_add(
                out=E["smalls"][:, 0:24], in0=t48[:, :, 0], in1=t48[:, :, 1]),
                tick="v")
            op("vector", lambda e: e.tensor_copy(
                out=E["trig_red"][:, 64:120], in_=E["auxf"][:, 0:56]))
            op("vector", lambda e: e.tensor_copy(
                out=E["trig_red"][:, 64:120], in_=E["auxf"][:, 0:56]))
            for sm0, ax0 in ((SM_GBF, AX_BF), (SM_GBI, AX_BI), (SM_GBC, AX_BC)):
                op("vector", lambda e, sm0=sm0, ax0=ax0: e.tensor_add(
                    out=E["smalls"][:, sm0:sm0 + 8], in0=E["smalls"][:, sm0:sm0 + 8],
                    in1=E["auxf"][:, ax0:ax0 + 8]))
            op("vector", lambda e: e.tensor_scalar(
                out=E["trig_kf"][:, 0:8], in0=E["auxf"][:, AX_WAV:AX_WAV + 8],
                scalar1=1e-3, scalar2=1e-8, op0=ALU.max, op1=ALU.add))
            op("vector", lambda e: e.tensor_copy(
                out=E["trig_th"][:, 0:56], in_=E["auxf"][:, 0:56]))
            op("vector", lambda e: e.tensor_copy(
                out=E["trig_th"][:, 0:56], in_=E["auxf"][:, 0:56]))
            op("vector", lambda e: e.reciprocal(
                out=E["trig_red"][:, 0:8], in_=E["trig_kf"][:, 0:8]))
            op("vector", lambda e: e.tensor_copy(
                out=E["trig_th"][:, 0:56], in_=E["auxf"][:, 0:56]))
            op("vector", lambda e: e.tensor_copy(
                out=E["trig_th"][:, 0:56], in_=E["auxf"][:, 0:56]))
            t_smalls = op("vector", lambda e: e.tensor_copy(
                out=E["smalls"][:, SM_INVW:SM_INVW + 1], in_=E["trig_red"][:, 0:1]),
                tick="v")
            E["act_extra_wait"] = [("v", t_smalls)]
            E["t_smalls_v"] = t_smalls
            if _DBG_VV:
                dma("sync", "stf", vvdump_h[:, 192:528],
                    E["trig_th"][:, 0:336], wait=[("v", t_ev)])
                dma("sync", "stf", vvdump_h[:, 0:28],
                    E["smalls"][:, 0:28], wait=[("v", t_smalls)])
                dma("sync", "stf", vvdump_h[:, 28:84],
                    E["auxf"][:, 0:56], wait=[("v", t_smalls)])

        for nm in ("wfT", "wiT", "wcT"):
            dma("gpsimd", "lw1", E[nm][:, :].rearrange("p (c i) -> p c i", c=NCH),
                w_h[nm].rearrange("(c p) i -> p c i", p=128))
        for nm in ("woT", "uoT"):
            dma("gpsimd", "lw2", E[nm][:, :].rearrange("p (c i) -> p c i", c=NCH),
                w_h[nm].rearrange("(c p) i -> p c i", p=128))
        for nm in ("wqT", "wkT", "wvT"):
            for ch in range(NCH):
                dma("gpsimd", "lw3", wq_s[nm][ch][:, :],
                    wq_h[nm][ch * 128:(ch + 1) * 128, :])
        E["h_weight_wait"] = [("lw2", 2 * 16)]
        gate_w_wait = [("lw1", 3 * 16)]
        qk_w_wait = [("lw3", 24 * 16)]

        prev = {}
        for b in range(n_blocks):
            cur = xblk[b % 2]
            c0 = b * TB
            lxs = f"lx{b % 2}"
            sts = f"st{b % 2}"
            if b > 0:
                for ch in range(NCH):
                    w = None
                    if ch == 0 and b >= 2:
                        w = [("v", prev[(b - 2, "y2")]), (sts, prev[(b - 2, "xst")])]
                    xld[(b, ch)] = dma("sync", lxs, cur[ch][:, :],
                                       x_h[ch * 128:(ch + 1) * 128, c0:c0 + TB],
                                       wait=w)
            xsrc = [cur[ch][:, :] for ch in range(NCH)]
            wf = [(lxs, xld[(b, NCH - 1)])] + AUX
            if b > 0:
                wf.append(("t", prev[(b - 1, "stats2")]))
            y_t, _ = _ln_emit(op, E, xsrc, wf)
            if b == 0:
                emit_uchain()
            if not do_ffn:
                prev[(b, "stats2")] = prog.ctr.get("t", 0)
                prev[(b, "y2")] = prog.ctr.get("v", 0)
                prev[(b, "xst")] = prog.ctr.get(f"st{b % 2}", 0)
                prev[(b, "trig_v")] = prog.ctr.get("v", 0)
                prev[(b, "vev_last")] = prog.ctr.get("v", 0)
                for sub in range(TB // 128):
                    prev[(b, f"vev{sub}")] = prog.ctr.get("v", 0)
                continue
            hook2, fin2 = (_ln_hoist_mk(op, E, xsrc) if do_ln2
                           else (None, None))
            x_t = _ffn_emit(op, E, y_t, xsrc, gate_w_wait if b == 0 else None,
                            ln_hook=hook2)
            for ch in range(NCH):
                v = dma("gpsimd", sts,
                        x1_h[ch * 128:(ch + 1) * 128, c0:c0 + TB], cur[ch][:, :],
                        wait=[("v", x_t[ch])])
            prev[(b, "xst")] = v
            if not do_ln2:
                prev[(b, "stats2")] = prog.ctr.get("t", 0)
                prev[(b, "y2")] = prog.ctr.get("v", 0)
                prev[(b, "trig_v")] = prog.ctr.get("v", 0)
                prev[(b, "vev_last")] = prog.ctr.get("v", 0)
                for sub in range(TB // 128):
                    prev[(b, f"vev{sub}")] = prog.ctr.get("v", 0)
                continue
            t_stats2 = fin2()
            y2_t = _ln_finish(op, E, t_stats2)
            prev[(b, "stats2")] = t_stats2
            prev[(b, "y2")] = y2_t[NCH - 1]
            if not do_qkv:
                prev[(b, "trig_v")] = prog.ctr.get("v", 0)
                prev[(b, "vev_last")] = prog.ctr.get("v", 0)
                for sub in range(TB // 128):
                    prev[(b, f"vev{sub}")] = prog.ctr.get("v", 0)
                continue
            for (nm, pcols, cs, sn, bcol) in (
                    ("wqT", (0, TB), cosq_s, sinq_s, AX_BQ),
                    ("wkT", (TB, 2 * TB), cosk_s, sink_s, AX_BK)):
                for ch in range(NCH):
                    w = [("v", y2_t[ch])]
                    if ch == 0:
                        w += qk_w_wait
                        if b > 0:
                            w.append(("v", prev[(b - 1, "trig_v")]))
                    last = ch == NCH - 1
                    tq = op("tensor", lambda e, ch=ch, nm=nm, pcols=pcols,
                            last=last: e.matmul(
                        out=qk_ps[:, pcols[0]:pcols[1]], lhsT=wq_s[nm][ch][:, :],
                        rhs=E["yb"][ch][:, :], start=(ch == 0), stop=last),
                        tick="t" if last else None, wait=w)
                _trig_emit(op, E, qk_ps[:, pcols[0]:pcols[1]], bcol,
                           cs[:, c0:c0 + TB], sn[:, c0:c0 + TB],
                           [("t", tq), ("v", E["t_smalls_v"])])
            prev[(b, "trig_v")] = prog.ctr.get("v", 0)
            if not do_v:
                prev[(b, "vev_last")] = prog.ctr.get("v", 0)
                for sub in range(TB // 128):
                    prev[(b, f"vev{sub}")] = prog.ctr.get("v", 0)
                continue
            for sub in range(TB // 128):
                pc = (sub % 2) * 512 + (sub // 2) * 128
                tk0 = c0 + sub * 128
                for ch in range(NCH):
                    w = None
                    if ch == 0:
                        w = []
                        if sub == 0 and b > 0:
                            w.append(("v", prev[(b - 1, "vev_last")]))
                        w = w or None
                    last = ch == NCH - 1
                    tvm = op("tensor", lambda e, ch=ch, pc=pc, sub=sub,
                             last=last: e.matmul(
                        out=v_ps[:, pc:pc + 128],
                        lhsT=E["yb"][ch][:, sub * 128:(sub + 1) * 128],
                        rhs=wq_s["wvT"][ch][:, :], start=(ch == 0), stop=last),
                        tick="t" if last else None, wait=w)
                t = op("vector", lambda e, pc=pc, tk0=tk0: e.tensor_add(
                    out=vperm_s[:, tk0:tk0 + 128], in0=v_ps[:, pc:pc + 128],
                    in1=bvbc[:, :]), tick="v", wait=[("t", tvm)])
                prev[(b, f"vev{sub}")] = t
            prev[(b, "vev_last")] = t

        t_end_v = prog.ctr.get("v", 0)
        t_end_s = prog.ctr.get("s", 0)
        for hb, sbl in ((cosq_h, cosq_s), (sinq_h, sinq_s), (cosk_h, cosk_s),
                        (sink_h, sink_s), (vperm_h, vperm_s)):
            dma("gpsimd", "stf", hb, sbl[:, :],
                wait=[("v", t_end_v), ("s", t_end_s)])
        dma("gpsimd", "stf", smalls_h, E["smalls"][:, 0:NSMALL])

        def _fin(e):
            for cn in ("st0", "st1", "stf"):
                e.wait_ge(sems[cn], prog.ctr.get(cn, 0))
        prog.emit("gpsimd", _fin)

        _replay(nc, es, prog)
    return nc


def build_launch_b():
    nc = bass.Bass("TRN2", target_bir_lowering=False,
                   detect_race_conditions=False)
    x1_h = nc.dram_tensor("x1", [DM, TC], F32, kind="ExternalInput").ap()
    cosq_h = nc.dram_tensor("cosq", [H, TC], BF16, kind="ExternalInput").ap()
    sinq_h = nc.dram_tensor("sinq", [H, TC], BF16, kind="ExternalInput").ap()
    coskF_h = nc.dram_tensor("coskF", [H, SEQ], BF16, kind="ExternalInput").ap()
    sinkF_h = nc.dram_tensor("sinkF", [H, SEQ], BF16, kind="ExternalInput").ap()
    vpermF_h = nc.dram_tensor("vpermF", [128, SEQ], BF16, kind="ExternalInput").ap()
    w_h = {nm: nc.dram_tensor(nm, [DM, DM], F32, kind="ExternalInput").ap()
           for nm in ("wfT", "wiT", "wcT", "woT", "uoT")}
    waoT_h = nc.dram_tensor("waoT", [H, DM], F32, kind="ExternalInput").ap()
    auxf_h = nc.dram_tensor("auxf", [128, NAUXF], F32, kind="ExternalInput").ap()
    auxb_h = nc.dram_tensor("auxb", [128, 2], BF16, kind="ExternalInput").ap()
    onesrow_h = nc.dram_tensor("onesrow", [1, 128], BF16,
                               kind="ExternalInput").ap()
    ones512_h = nc.dram_tensor("ones512", [1, 512], BF16,
                               kind="ExternalInput").ap()
    boa_h = nc.dram_tensor("boaRow", [1, DM], F32, kind="ExternalInput").ap()
    smalls_h = nc.dram_tensor("smalls", [128, NSMALL], F32, kind="ExternalInput").ap()
    out_h = nc.dram_tensor("out", [DM, TC], F32, kind="ExternalOutput").ap()

    es = contextlib.ExitStack()
    with es:
        E, sb = _common_sbuf(nc, es)
        ps_ = lambda n, sh: es.enter_context(nc.psum_tensor(n, sh, F32))
        E["smalls"] = sb("smalls_s", [128, NSMALL], F32)
        waoT = sb("wao_s", [H, DM], BF16)
        cosq = sb("cosq_sb", [H, TC], BF16)
        sinq = sb("sinq_sb", [H, TC], BF16)
        kbuf = [dict(ck=sb(f"ck{i}", [H, KG], BF16), sk=sb(f"sk{i}", [H, KG], BF16),
                     vp=sb(f"vp{i}", [128, KG], BF16)) for i in range(2)]
        x1s = [[sb(f"x1_0_{c}", [128, SW], F32) for c in range(NCH)]]
        ptiles = [sb(f"ptile{i}", [128, SW], BF16) for i in range(3)]
        pairs = [sb(f"pair{i}", [128, SW], BF16) for i in range(2)]
        quad = sb("quad", [128, SW], BF16)
        boa = sb("boa_s", [1, DM], F32)
        ones512_b = sb("ones512_b", [1, 512], BF16)
        boa_hi = sb("boa_hi", [1, DM], BF16)
        boa_lo = sb("boa_lo", [1, DM], BF16)
        dhi_b = pairs[0][0:1, :]
        dlo_b = pairs[1][0:1, :]
        out_sb = sb("out_sb", [H, SW], F32)
        rproj = sb("rproj", [H, SW], BF16)
        drow = sb("drow", [1, SW], F32)

        sA = ps_("sA", [128, SW])
        sB = ps_("sB", [128, SW])
        out_ps = ps_("out_ps", [128, SW])
        den_ps = ps_("den_ps", [128, SW])
        E["stats_ps"] = sA
        E["bc_ps"] = sB
        E["gsets"] = ((sA[:, 0:TB], sA[:, TB:2 * TB], sB[:, 0:TB]),
                      (sB[:, TB:2 * TB], out_ps[:, 0:TB], out_ps[:, TB:2 * TB]))
        E["hsets"] = (den_ps[:, 0:TB], den_ps[:, TB:2 * TB])

        sems = {k: es.enter_context(nc.semaphore(f"sem_{k}"))
                for k in ("lda", "lx0", "lk0", "lk1", "lw1", "lw2", "lw3",
                          "st0", "st1", "v", "s", "t")}
        prog = Prog()
        dma, op = _mk(prog, sems)

        dma("sync", "lda", E["auxf"][:, :], auxf_h)
        dma("sync", "lda", E["onesb"][:, :], auxb_h)
        dma("sync", "lda", E["onesrow_b"][:, :], onesrow_h)
        dma("sync", "lda", E["smalls"][:, :], smalls_h)
        dma("sync", "lda", cosq[:, :], cosq_h)
        dma("sync", "lda", sinq[:, :], sinq_h)
        dma("sync", "lda", ones512_b[:, :], ones512_h)
        dma("sync", "lda", boa[:, :], boa_h)
        AUX = [("lda", 128)]
        x1ld = {}
        for ch in range(NCH):
            x1ld[(0, ch)] = dma("sync", "lx0", x1s[0][ch][:, :],
                                x1_h[ch * 128:(ch + 1) * 128, 0:SW])
        for nm in ("wfT", "wiT", "wcT"):
            dma("gpsimd", "lw1", E[nm][:, :].rearrange("p (c i) -> p c i", c=NCH),
                w_h[nm].rearrange("(c p) i -> p c i", p=128))
        for nm in ("woT", "uoT"):
            dma("gpsimd", "lw2", E[nm][:, :].rearrange("p (c i) -> p c i", c=NCH),
                w_h[nm].rearrange("(c p) i -> p c i", p=128))
        dma("gpsimd", "lw3", waoT[:, :], waoT_h)
        E["h_weight_wait"] = [("lw2", 2 * 16)]
        gate_w_wait = [("lw1", 3 * 16)] + AUX

        op("vector", lambda e: e.tensor_copy(
            out=boa_hi[0:1, :], in_=boa[0:1, :]), wait=AUX)
        op("vector", lambda e: e.tensor_copy(
            out=quad[:, 0:56], in_=E["auxf"][:, 0:56]))
        op("vector", lambda e: e.tensor_copy(
            out=quad[:, 0:56], in_=E["auxf"][:, 0:56]))
        op("vector", lambda e: e.tensor_copy(
            out=drow[0:1, :], in_=boa_hi[0:1, :]))
        op("vector", lambda e: e.tensor_copy(
            out=quad[:, 0:56], in_=E["auxf"][:, 0:56]))
        t_bo = op("vector", lambda e: e.tensor_sub(
            out=boa_lo[0:1, :], in0=boa[0:1, :], in1=drow[0:1, :]), tick="v")
        bo_wait = [[("v", t_bo)]]

        NG = NTILE // 4
        kc = {}
        prev = {}
        E["trig_th"] = None
        for sw in range(NSW):
            cur = x1s[0]
            q0 = sw * SW
            if sw > 0:
                for ch in range(NCH):
                    w = None
                    if ch == 0:
                        w = [(f"st{(sw - 1) % 2}", prev[(sw - 1, "ost")])]
                    x1ld[(sw, ch)] = dma("sync", "lx0", cur[ch][:, :],
                                         x1_h[ch * 128:(ch + 1) * 128, q0:q0 + SW],
                                         wait=w)
            xp_t = {}
            for ch in range(NCH):
                xp_t[ch] = prog.ctr.get("v", 0)
            t_x1rdy = [("lx0", x1ld[(sw, NCH - 1)])] + AUX
            sweep_bar = [("v", prog.ctr.get("v", 0)), ("s", prog.ctr.get("s", 0))]
            exp_t = {}
            den_t = {}
            tile_meta = {}

            def emit_kload(g, wait):
                buf = kbuf[g % 2]
                gg = sw * NG + g
                k0 = g * KG
                lks = f"lk{gg % 2}"
                dma("sync", lks, buf["ck"][:, :], coskF_h[:, k0:k0 + KG], wait=wait)
                dma("sync", lks, buf["sk"][:, :], sinkF_h[:, k0:k0 + KG])
                lvp = dma("sync", lks, buf["vp"][:, :], vpermF_h[:, k0:k0 + KG])
                for ti in range(4):
                    tile_meta[g * 4 + ti] = (g * 4 + ti, buf, lks, lvp, ti)

            emit_kload(0, None if sw == 0 else [("t", kc[sw * NG - 2])])
            emit_kload(1, None if sw == 0 else [("t", kc[sw * NG - 1])])

            def emit_S(tile, buf, lks, lvp):
                Sp = sA if tile % 2 == 0 else sB
                c0, c1 = (tile % 4) * 128, (tile % 4 + 1) * 128
                w = [(lks, lvp)]
                if tile == 0:
                    w.append(("lw3", 16))
                    w += sweep_bar
                if tile >= 2:
                    w.append(("s", exp_t[tile - 2]))
                tS = None
                for half in range(2):
                    hc0 = half * 512
                    op("tensor", lambda e, buf=buf, c0=c0, c1=c1, hc0=hc0,
                       Sp=Sp: e.matmul(
                           out=Sp[:, hc0:hc0 + 512], lhsT=buf["ck"][:, c0:c1],
                           rhs=cosq[:, q0 + hc0:q0 + hc0 + 512],
                           start=True, stop=False),
                       wait=w if half == 0 else None)
                    last = half == 1
                    tS = op("tensor", lambda e, buf=buf, c0=c0, c1=c1, hc0=hc0,
                            Sp=Sp: e.matmul(
                                out=Sp[:, hc0:hc0 + 512], lhsT=buf["sk"][:, c0:c1],
                                rhs=sinq[:, q0 + hc0:q0 + hc0 + 512],
                                start=False, stop=True),
                            tick="t" if last else None)
                return tS, Sp

            pair_t = {}
            quad_t = {}

            def emit_exp(tile, tS, Sp):
                w = [("t", tS)]
                if tile >= 3:
                    w.append(("t", den_t[tile - 3]))
                    w.append(("v", pair_t[tile - 3]))
                pt = ptiles[tile % 3]
                exp_t[tile] = op("scalar", lambda e, Sp=Sp, pt=pt: e.activation(
                    out=pt[:, :], in_=Sp[:, :], func=AF.Exp,
                    bias=E["auxf"][:, AX_ZERO:AX_ZERO + 1],
                    scale=E["auxf"][:, AX_ISQ:AX_ISQ + 1]), tick="s", wait=w)

            def emit_pv(tile, buf):
                c0, c1 = (tile % 4) * 128, (tile % 4 + 1) * 128
                pt = ptiles[tile % 3]
                tden = None
                for half in range(2):
                    hc0 = half * 512
                    w2 = [("s", exp_t[tile])] if half == 0 else None
                    last = half == 1
                    tden = op("tensor", lambda e, buf=buf, c0=c0, c1=c1, hc0=hc0,
                              tile=tile, pt=pt: e.matmul(
                                  out=out_ps[:, hc0:hc0 + 512],
                                  lhsT=buf["vp"][:, c0:c1],
                                  rhs=pt[:, hc0:hc0 + 512], start=(tile == 0),
                                  stop=(tile == NTILE - 1)),
                              tick="t" if last else None, wait=w2)
                if tile % 2 == 1:
                    pair_t[tile] = pair_t[tile - 1] = op(
                        "vector", lambda e, tile=tile: e.tensor_add(
                            out=pairs[(tile // 2) % 2][:, :],
                            in0=ptiles[(tile - 1) % 3][:, :],
                            in1=ptiles[tile % 3][:, :]),
                        tick="v", wait=[("s", exp_t[tile])])
                else:
                    pair_t.setdefault(tile, prog.ctr.get("v", 0))
                if tile % 4 == 3:
                    tq4 = op("vector", lambda e: e.tensor_add(
                        out=quad[:, :], in0=pairs[0][:, :], in1=pairs[1][:, :]),
                        tick="v",
                        wait=[("t", quad_t[tile - 4])] if tile >= 7 else None)
                    for half in range(2):
                        hc0 = half * 512
                        last = half == 1
                        tden = op("tensor", lambda e, hc0=hc0, tile=tile: e.matmul(
                            out=den_ps[0:1, hc0:hc0 + 512],
                            lhsT=E["onesb"][:, 0:1], rhs=quad[:, hc0:hc0 + 512],
                            start=(tile == 3), stop=(tile == NTILE - 1)),
                            tick="t" if last else None,
                            wait=[("v", tq4)] if half == 0 else None)
                        quad_t[tile] = tden
                den_t[tile] = tden
                return tden

            pend = {}
            tile, buf, lks, lvp, _ = tile_meta[0]
            pend[0] = emit_S(tile, buf, lks, lvp)

            def do_pv(t):
                tden = emit_pv(t, tile_meta[t][1])
                if tile_meta[t][4] == 3:
                    g = tile_meta[t][0] // 4
                    kc[sw * NG + g] = tden
                    if g + 2 < NG:
                        emit_kload(g + 2, [("t", kc[sw * NG + g])])
                return tden

            for t in range(NTILE):
                emit_exp(t, *pend.pop(t))
                if t + 1 < NTILE:
                    tile, buf, lks, lvp, _ = tile_meta[t + 1]
                    pend[t + 1] = emit_S(tile, buf, lks, lvp)
                if t >= 1:
                    tden = do_pv(t - 1)
            tden = do_pv(NTILE - 1)
            t_att = tden
            op("vector", lambda e: e.tensor_copy(
                out=E["srow"][0:1, 0:SW], in_=den_ps[0:1, :]),
               wait=[("t", t_att)])
            op("vector", lambda e: e.tensor_copy(
                out=out_sb[0:1, :], in_=E["srow"][0:1, 0:SW]))
            op("vector", lambda e: e.tensor_copy(
                out=rproj[0:1, :], in_=E["srow"][0:1, 0:SW]))
            op("vector", lambda e: e.reciprocal(
                out=drow[0:1, :], in_=E["srow"][0:1, 0:SW]))
            op("vector", lambda e: e.tensor_copy(
                out=out_sb[0:1, :], in_=E["srow"][0:1, 0:SW]))
            op("vector", lambda e: e.tensor_copy(
                out=rproj[0:1, :], in_=E["srow"][0:1, 0:SW]))
            op("vector", lambda e: e.tensor_copy(
                out=dhi_b[0:1, :], in_=drow[0:1, :]))
            tdr = op("vector", lambda e: e.tensor_copy(out=out_sb[:, :],
                                                       in_=out_ps[:, :]), tick="v")
            op("vector", lambda e: e.tensor_copy(
                out=E["srow"][0:1, 0:SW], in_=dhi_b[0:1, :]))
            op("vector", lambda e: e.tensor_copy(
                out=quad[:, 0:56], in_=E["auxf"][:, 0:56]))
            op("vector", lambda e: e.tensor_copy(
                out=quad[:, 0:56], in_=E["auxf"][:, 0:56]))
            tden2 = op("vector", lambda e: e.tensor_sub(
                out=dlo_b[0:1, :], in0=drow[0:1, :],
                in1=E["srow"][0:1, 0:SW]), tick="v")
            for half in range(2):
                hc0 = half * 512
                op("tensor", lambda e, hc0=hc0: e.matmul(
                    out=sA[:, hc0:hc0 + 512], lhsT=E["onesrow_b"][0:1, :],
                    rhs=dhi_b[0:1, hc0:hc0 + 512], start=True, stop=False),
                    wait=[("v", tden2)] if half == 0 else None)
                tb = op("tensor", lambda e, hc0=hc0: e.matmul(
                    out=sA[:, hc0:hc0 + 512], lhsT=E["onesrow_b"][0:1, :],
                    rhs=dlo_b[0:1, hc0:hc0 + 512], start=False, stop=True),
                    tick="t" if half == 1 else None)
            trp = op("vector", lambda e: e.tensor_mul(
                out=rproj[:, :], in0=out_sb[:, :], in1=sA[:, :]), tick="v",
                wait=[("t", tb)])
            pr_t = {}
            for ch in range(NCH):
                pb = sB if ch % 2 == 0 else den_ps
                if ch >= 2:
                    op("tensor", lambda e: e.nop(), wait=[("v", pr_t[ch - 2])])
                for half in range(2):
                    hc0 = half * 512
                    w3 = [("v", trp)] if (ch == 0 and half == 0) else None
                    op("tensor", lambda e, ch=ch, hc0=hc0, pb=pb: e.matmul(
                        out=pb[:, hc0:hc0 + 512],
                        lhsT=waoT[:, ch * 128:(ch + 1) * 128],
                        rhs=rproj[:, hc0:hc0 + 512], start=True, stop=False),
                        wait=w3)
                    last = half == 1
                    op("tensor", lambda e, ch=ch, hc0=hc0, pb=pb: e.matmul(
                        out=pb[:, hc0:hc0 + 512],
                        lhsT=boa_hi[0:1, ch * 128:(ch + 1) * 128],
                        rhs=ones512_b[0:1, :], start=False, stop=False),
                        wait=bo_wait.pop() if bo_wait else None)
                    tpm = op("tensor", lambda e, ch=ch, hc0=hc0, pb=pb,
                             last=last: e.matmul(
                        out=pb[:, hc0:hc0 + 512],
                        lhsT=boa_lo[0:1, ch * 128:(ch + 1) * 128],
                        rhs=ones512_b[0:1, :], start=False, stop=True),
                        tick="t" if last else None)
                pr_t[ch] = op("vector", lambda e, ch=ch, cur=cur, pb=pb: e.tensor_add(
                    out=cur[ch][:, :], in0=cur[ch][:, :], in1=pb[:, :]), tick="v",
                    wait=[("t", tpm)] + t_x1rdy)
            stats_prev = None
            hoist = {}
            for bb in range(SW // TB):
                xsrc = [cur[ch][:, bb * TB:(bb + 1) * TB] for ch in range(NCH)]
                if bb == 0:
                    wfst = [("v", pr_t[NCH - 1]), ("t", prog.ctr.get("t", 0))]
                    if stats_prev is not None:
                        wfst.append(("t", stats_prev))
                    y_t, stats_prev = _ln_emit(op, E, xsrc, wfst)
                    xsrc1 = [cur[ch][:, TB:2 * TB] for ch in range(NCH)]
                    hoist["hook"], hoist["fin"] = _ln_hoist_mk(op, E, xsrc1)
                else:
                    stats_prev = hoist["fin"]()
                    y_t = _ln_finish(op, E, stats_prev)
                x_t = _ffn_emit(op, E, y_t, xsrc,
                                gate_w_wait if (sw == 0 and bb == 0) else None,
                                ln_hook=hoist["hook"] if bb == 0 else None)
                for ch in range(NCH):
                    v = dma("gpsimd", f"st{sw % 2}",
                            out_h[ch * 128:(ch + 1) * 128,
                                  q0 + bb * TB:q0 + (bb + 1) * TB],
                            xsrc[ch], wait=[("v", x_t[ch])])
            prev[(sw, "ost")] = v

        def _fin(e):
            for cn in ("st0", "st1"):
                e.wait_ge(sems[cn], prog.ctr.get(cn, 0))
        prog.emit("gpsimd", _fin)
        _replay(nc, es, prog)
    return nc


def _prep_host(inputs):
    g = lambda k: np.ascontiguousarray(np.asarray(inputs[k], dtype=np.float32))
    auxf = np.zeros((128, NAUXF), np.float32)
    auxf[:, AX_WAV] = g("L0_attn_wavelengths")
    auxf[:, AX_PHASE] = g("L0_attn_phase_bias")
    auxf[:, AX_BQ] = g("L0_attn_bq")
    auxf[:, AX_BK] = g("L0_attn_bk")
    auxf[:, AX_BF:AX_BF + 8] = g("L0_ffn_bf").reshape(8, 128).T
    auxf[:, AX_BI:AX_BI + 8] = g("L0_ffn_bi").reshape(8, 128).T
    auxf[:, AX_BC:AX_BC + 8] = g("L0_ffn_bc").reshape(8, 128).T
    auxf[:, AX_BO_F:AX_BO_F + 8] = g("L0_ffn_bo").reshape(8, 128).T
    auxf[:, AX_INC:AX_INC + 8] = g("L0_ffn_in_c").reshape(8, 128).T
    auxf[:, AX_BO_A:AX_BO_A + 8] = g("L0_attn_bo").reshape(8, 128).T
    auxf[:, AX_ISQ] = 1.0 / math.sqrt(H)
    auxf[:, AX_EPS] = 1e-5
    auxf[:, AX_ZERO] = 0.0
    auxb = np.zeros((128, 2), BF)
    auxb[:, 0] = 1.0
    onesrow = np.ones((1, 128), BF)
    bvbc = np.ascontiguousarray(np.broadcast_to(g("L0_attn_bv")[None, :], (128, 128)))
    T = lambda a: np.ascontiguousarray(a.T)
    com = {"wfT": T(g("L0_ffn_wf")), "wiT": T(g("L0_ffn_wi")),
           "wcT": T(g("L0_ffn_wc")), "woT": T(g("L0_ffn_wo")),
           "uoT": T(g("L0_ffn_uo")), "auxf": auxf, "auxb": auxb,
           "onesrow": onesrow}
    a_only = {"ufT": T(g("L0_ffn_uf")), "uiT": T(g("L0_ffn_ui")),
              "ucT": T(g("L0_ffn_uc")),
              "wqT": T(g("L0_attn_Wq")), "wkT": T(g("L0_attn_Wk")),
              "wvT": T(g("L0_attn_Wv")), "bvbc": bvbc}
    b_only = {"waoT": T(g("L0_attn_Wo")),
              "ones512": np.ones((1, 512), BF),
              "boaRow": np.ascontiguousarray(g("L0_attn_bo")[None, :])}
    xT = np.ascontiguousarray(g("L0_attn_x").reshape(SEQ, DM).T)
    return com, a_only, b_only, xT


_NC_CACHE = {}


def kernel(**inputs):
    com, a_only, b_only, xT = _prep_host(inputs)
    if "a" not in _NC_CACHE:
        _NC_CACHE["a"] = build_launch_a()
        _NC_CACHE["b"] = build_launch_b()
    nc_a, nc_b = _NC_CACHE["a"], _NC_CACHE["b"]

    in_maps_a = []
    for c in range(NCORES):
        m = dict(com)
        m.update(a_only)
        m["x"] = np.ascontiguousarray(xT[:, c * TC:(c + 1) * TC])
        in_maps_a.append(m)
    res_a = run_bass_kernel_spmd(nc_a, in_maps_a, core_ids=list(range(NCORES)))

    coskF = np.concatenate([res_a.results[c]["cosk"] for c in range(NCORES)], axis=1)
    sinkF = np.concatenate([res_a.results[c]["sink"] for c in range(NCORES)], axis=1)
    vpermF = np.concatenate([res_a.results[c]["vperm"] for c in range(NCORES)], axis=1)

    in_maps_b = []
    for c in range(NCORES):
        m = dict(com)
        m.update(b_only)
        m["x1"] = res_a.results[c]["x1"]
        m["cosq"] = res_a.results[c]["cosq"]
        m["sinq"] = res_a.results[c]["sinq"]
        m["coskF"] = coskF
        m["sinkF"] = sinkF
        m["vpermF"] = vpermF
        m["smalls"] = res_a.results[c]["smalls"]
        in_maps_b.append(m)
    res_b = run_bass_kernel_spmd(nc_b, in_maps_b, core_ids=list(range(NCORES)))

    out = np.concatenate([res_b.results[c]["out"] for c in range(NCORES)], axis=1)
    return np.ascontiguousarray(out.T).reshape(1, SEQ, DM)


# version 10 (final)
# speedup vs baseline: 1.1139x; 1.0268x over previous
import contextlib
import math

import numpy as np
import ml_dtypes

import concourse.bass as bass
import concourse.mybir as mybir
from concourse.bass_utils import run_bass_kernel_spmd

F32 = mybir.dt.float32
_DBG_VV = False
BF16 = mybir.dt.bfloat16
I32 = mybir.dt.int32
BF = ml_dtypes.bfloat16
AF = mybir.ActivationFunctionType
ALU = mybir.AluOpType

NCORES = 8
DM = 1024
H = 128
NCH = DM // 128
TC = 2048
TB = 512
NB = TC // TB
SEQ = NCORES * TC
NTILE = SEQ // 128
SW = 1024
NSW = TC // SW
KG = 512

_TPI = 2 * math.pi
TWO_PI_HI = float(np.float32(_TPI))
TWO_PI_MID = float(np.float32(_TPI - TWO_PI_HI))
TWO_PI_LO = float(_TPI - TWO_PI_HI - TWO_PI_MID)
PI = float(np.float32(math.pi))
HALF_PI = float(np.float32(math.pi / 2))

AX_WAV, AX_PHASE, AX_BQ, AX_BK = 0, 1, 2, 3
AX_BF, AX_BI, AX_BC, AX_BO_F, AX_INC, AX_BO_A = 4, 12, 20, 28, 36, 44
AX_ISQ, AX_EPS, AX_ZERO = 52, 53, 54
NAUXF = 56

SM_GBF, SM_GBI, SM_GBC, SM_INVW = 0, 8, 16, 24
NSMALL = 25
SM_QB, SM_KB = 25, 26


class Prog:

    def __init__(self):
        self.p = {k: [] for k in ("sync", "gpsimd", "vector", "scalar", "tensor")}
        self.ctr = {}

    def emit(self, eng, fn):
        self.p[eng].append(fn)


def _mk(prog, sems):
    def w_emit(e, wait):
        if wait:
            for (cn, wv) in wait:
                e.wait_ge(sems[cn], wv)

    def dma(queue, cn, out_ap, in_ap, wait=None, qn=None):
        prog.ctr[cn] = prog.ctr.get(cn, 0) + 16
        val = prog.ctr[cn]

        def f(e, out_ap=out_ap, in_ap=in_ap, wait=wait):
            w_emit(e, wait)
            e.dma_start(out=out_ap, in_=in_ap).then_inc(sems[cn], 16)

        prog.emit(queue, f)
        return val

    def op(eng, fn, tick=None, wait=None):
        val = None
        if tick is not None:
            prog.ctr[tick] = prog.ctr.get(tick, 0) + 1
            val = prog.ctr[tick]

        def f(e, fn=fn, wait=wait, tick=tick):
            w_emit(e, wait)
            ins = fn(e)
            if tick is not None:
                ins.then_inc(sems[tick], 1)

        prog.emit(eng, f)
        return val

    return dma, op


def _ln_stats(op, E, xsrc, wait_first_cast):
    xb, x2b = E["xb"], E["x2b"]
    stats = E["stats_ps"]
    onesb = E["onesb"]
    cast_t = []
    for ch in range(NCH):
        w = wait_first_cast if ch == 0 else None
        op("vector", lambda e, ch=ch: e.tensor_copy(out=xb[ch][:, :], in_=xsrc[ch]),
           wait=w)
        t = op("vector", lambda e, ch=ch: e.tensor_mul(
            out=x2b[ch][:, :], in0=xb[ch][:, :], in1=xb[ch][:, :]), tick="v")
        cast_t.append(t)
    for ch in range(NCH):
        op("tensor", lambda e, ch=ch: e.matmul(
            out=stats[0:1, 0:TB], lhsT=onesb[:, 0:1], rhs=xb[ch][:, :],
            start=(ch == 0), stop=(ch == NCH - 1)),
           wait=[("v", cast_t[ch])])
        last = (ch == NCH - 1)
        t_stats = op("tensor", lambda e, ch=ch, last=last: e.matmul(
            out=stats[0:1, TB:2 * TB], lhsT=onesb[:, 0:1], rhs=x2b[ch][:, :],
            start=(ch == 0), stop=last),
            tick="t" if last else None)
    return t_stats


def _ln_hoist_mk(op, E, xsrc):
    xb = E["xb"]
    stats = E["stats_ps"]
    onesb = E["onesb"]
    x2buf = (E["ftile"], E["itile"], E["gtile"], E["gtmp"],
             E["m_bc"], E["z_bc"])
    st = {"sq": {}}

    def do_ch(ch, buf, wait=None):
        op("vector", lambda e, ch=ch: e.tensor_copy(
            out=xb[ch][:, :], in_=xsrc[ch]))
        st["sq"][ch] = op("vector", lambda e, ch=ch, buf=buf: e.tensor_mul(
            out=buf[:, :], in0=xb[ch][:, :], in1=xb[ch][:, :]), tick="v",
            wait=wait)

    def hook(ob):
        if 1 <= ob <= 6:
            do_ch(ob - 1, x2buf[ob - 1])

    def mm_pair(ch, buf, w, tick_x2=False):
        op("tensor", lambda e, ch=ch: e.matmul(
            out=stats[0:1, 0:TB], lhsT=onesb[:, 0:1], rhs=xb[ch][:, :],
            start=(ch == 0), stop=(ch == NCH - 1)), wait=w)
        last = (ch == NCH - 1)
        return op("tensor", lambda e, ch=ch, buf=buf, last=last: e.matmul(
            out=stats[0:1, TB:2 * TB], lhsT=onesb[:, 0:1],
            rhs=buf[:, :], start=(ch == 0), stop=last),
            tick="t" if (last or tick_x2) else None)

    def finalize():
        mmt = {}
        for ch in range(6):
            mmt[ch] = mm_pair(ch, x2buf[ch], [("v", st["sq"][ch])],
                              tick_x2=(ch < 2))
        do_ch(6, x2buf[0], wait=[("t", mmt[0])])
        do_ch(7, x2buf[1], wait=[("t", mmt[1])])
        t_stats = None
        for ch in (6, 7):
            t_stats = mm_pair(ch, x2buf[ch - 6], [("v", st["sq"][ch])])
        return t_stats

    return hook, finalize


def _ln_finish(op, E, t_stats):
    yb = E["yb"]
    xb = E["xb"]
    stats, bc = E["stats_ps"], E["bc_ps"]
    srow, mrow, zrow = E["srow"], E["mrow"], E["zrow"]
    m_bc, z_bc = E["m_bc"], E["z_bc"]
    auxf = E["auxf"]
    op("vector", lambda e: e.tensor_scalar(
        out=mrow[0:1, :], in0=stats[0:1, 0:TB], scalar1=1.0 / DM, scalar2=None,
        op0=ALU.mult), wait=[("t", t_stats)])
    op("vector", lambda e: e.tensor_scalar(
        out=srow[0:1, 0:TB], in0=stats[0:1, TB:2 * TB], scalar1=1.0 / DM,
        scalar2=None, op0=ALU.mult))
    op("vector", lambda e: e.tensor_mul(
        out=srow[0:1, TB:2 * TB], in0=mrow[0:1, :], in1=mrow[0:1, :]))
    tv = op("vector", lambda e: e.tensor_sub(
        out=srow[0:1, 0:TB], in0=srow[0:1, 0:TB], in1=srow[0:1, TB:2 * TB]),
        tick="v")
    ts0 = op("scalar", lambda e: e.activation(
        out=srow[0:1, TB:2 * TB], in_=srow[0:1, 0:TB], func=AF.Sqrt,
        bias=auxf[0:1, AX_EPS:AX_EPS + 1]), tick="s", wait=[("v", tv)])
    op("vector", lambda e: e.tensor_scalar(
        out=srow[0:1, 2 * TB:3 * TB], in0=srow[0:1, 0:TB], scalar1=1e-5,
        scalar2=None, op0=ALU.add), wait=[("s", ts0)])
    op("vector", lambda e: e.reciprocal(out=zrow[0:1, :], in_=srow[0:1, TB:2 * TB]))
    op("vector", lambda e: e.tensor_copy(out=m_bc[0:1, :], in_=mrow[0:1, :]))
    op("vector", lambda e: e.tensor_copy(out=z_bc[0:1, :], in_=mrow[0:1, :]))
    op("vector", lambda e: e.tensor_mul(
        out=srow[0:1, 0:TB], in0=zrow[0:1, :], in1=zrow[0:1, :]))
    op("vector", lambda e: e.tensor_mul(
        out=srow[0:1, 0:TB], in0=srow[0:1, 0:TB], in1=srow[0:1, 2 * TB:3 * TB]))
    op("vector", lambda e: e.tensor_scalar(
        out=srow[0:1, 0:TB], in0=srow[0:1, 0:TB], scalar1=-0.5, scalar2=1.5,
        op0=ALU.mult, op1=ALU.add))
    tz = op("vector", lambda e: e.tensor_mul(
        out=zrow[0:1, :], in0=zrow[0:1, :], in1=srow[0:1, 0:TB]), tick="v")
    op("vector", lambda e: e.tensor_copy(
        out=E["mrow_b"][0:1, :], in_=mrow[0:1, :]))
    tzb = op("vector", lambda e: e.tensor_copy(
        out=E["zrow_b"][0:1, :], in_=zrow[0:1, :]), tick="v")
    op("tensor", lambda e: e.matmul(
        out=bc[:, 0:TB], lhsT=E["onesrow_b"][0:1, :], rhs=E["mrow_b"][0:1, :],
        start=True, stop=True), wait=[("v", tzb)])
    tb = op("tensor", lambda e: e.matmul(
        out=bc[:, TB:2 * TB], lhsT=E["onesrow_b"][0:1, :],
        rhs=E["zrow_b"][0:1, :], start=True, stop=True), tick="t")
    op("vector", lambda e: e.tensor_copy(out=m_bc[:, :], in_=bc[:, 0:TB]),
       wait=[("t", tb)])
    op("vector", lambda e: e.tensor_copy(out=z_bc[:, :], in_=bc[:, TB:2 * TB]))
    y_t = []
    for ch in range(NCH):
        op("vector", lambda e, ch=ch: e.tensor_sub(
            out=yb[ch][:, :], in0=xb[ch][:, :], in1=m_bc[:, :]))
        t = op("vector", lambda e, ch=ch: e.tensor_mul(
            out=yb[ch][:, :], in0=yb[ch][:, :], in1=z_bc[:, :]), tick="v")
        y_t.append(t)
    return y_t


def _ln_emit(op, E, xsrc, wait_first_cast):
    t_stats = _ln_stats(op, E, xsrc, wait_first_cast)
    y_t = _ln_finish(op, E, t_stats)
    return y_t, t_stats


def _ffn_emit(op, E, y_ticks, xdst, extra_gate_wait, ln_hook=None):
    yb, cb = E["yb"], E["cb"]
    ftile, itile, gtile, gtmp = E["ftile"], E["itile"], E["gtile"], E["gtmp"]
    th32 = E["th32"]
    wf, wi, wc, wo, uo = E["wfT"], E["wiT"], E["wcT"], E["woT"], E["uoT"]
    auxf, smalls = E["auxf"], E["smalls"]
    gsets, hsets = E["gsets"], E["hsets"]

    act_t = {}
    c_t = {}
    for ob in range(8):
        pf, pi_, pg = gsets[ob % 2]
        if ob >= 2:
            op("tensor", lambda e: e.nop(), wait=[("s", act_t[ob - 2])])
        for ch in range(NCH):
            w = [("v", y_ticks[ch])]
            if ob == 0 and ch == 0 and extra_gate_wait:
                w = w + extra_gate_wait
            op("tensor", lambda e, ch=ch, ob=ob, pf=pf: e.matmul(
                out=pf, lhsT=wf[:, ch * DM + ob * 128:ch * DM + (ob + 1) * 128],
                rhs=yb[ch][:, :], start=(ch == 0), stop=(ch == NCH - 1)),
               wait=w if ob == 0 else (w if ch == 0 and ob == 0 else None))
        for ch in range(NCH):
            op("tensor", lambda e, ch=ch, ob=ob, pi_=pi_: e.matmul(
                out=pi_, lhsT=wi[:, ch * DM + ob * 128:ch * DM + (ob + 1) * 128],
                rhs=yb[ch][:, :], start=(ch == 0), stop=(ch == NCH - 1)))
        for ch in range(NCH):
            last = ch == NCH - 1
            tmm = op("tensor", lambda e, ch=ch, ob=ob, pg=pg, last=last: e.matmul(
                out=pg, lhsT=wc[:, ch * DM + ob * 128:ch * DM + (ob + 1) * 128],
                rhs=yb[ch][:, :], start=(ch == 0), stop=last),
                tick="t" if last else None)
        aw = [("t", tmm)]
        if ob >= 1:
            aw.append(("v", c_t[ob - 1]))
        if ob == 0 and E.get("act_extra_wait"):
            aw = aw + E.pop("act_extra_wait")
        op("scalar", lambda e, ob=ob, pf=pf: e.activation(
            out=ftile[:, :], in_=pf, func=AF.Sigmoid,
            bias=smalls[:, SM_GBF + ob:SM_GBF + ob + 1]), wait=aw)
        op("scalar", lambda e, ob=ob, pi_=pi_: e.activation(
            out=itile[:, :], in_=pi_, func=AF.Sigmoid,
            bias=smalls[:, SM_GBI + ob:SM_GBI + ob + 1]))
        ta = op("scalar", lambda e, ob=ob, pg=pg: e.activation(
            out=gtile[:, :], in_=pg, func=AF.Tanh,
            bias=smalls[:, SM_GBC + ob:SM_GBC + ob + 1]), tick="s")
        act_t[ob] = ta
        op("vector", lambda e, ob=ob: e.tensor_scalar(
            out=cb[ob][:, :], in0=ftile[:, :],
            scalar1=auxf[:, AX_INC + ob:AX_INC + ob + 1], scalar2=None,
            op0=ALU.mult), wait=[("s", ta)])
        op("vector", lambda e: e.tensor_mul(
            out=gtmp[:, :], in0=itile[:, :], in1=gtile[:, :]))
        c_t[ob] = op("vector", lambda e, ob=ob: e.tensor_add(
            out=cb[ob][:, :], in0=cb[ob][:, :], in1=gtmp[:, :]), tick="v")
        if ob == 1 and E.get("ffn_mid_hook"):
            E.pop("ffn_mid_hook")()

    h_t = {}
    x_t = {}
    for ob in range(8):
        ph = hsets[ob % 2]
        if ob >= 2:
            op("tensor", lambda e: e.nop(), wait=[("s", h_t[ob - 2])])
        for ch in range(NCH):
            w = None
            if ob == 0 and ch == 0:
                w = [("v", c_t[7])] + (E.get("h_weight_wait") or [])
            op("tensor", lambda e, ch=ch, ob=ob, ph=ph: e.matmul(
                out=ph, lhsT=wo[:, ch * DM + ob * 128:ch * DM + (ob + 1) * 128],
                rhs=yb[ch][:, :], start=(ch == 0), stop=False), wait=w)
        for ch in range(NCH):
            last = ch == NCH - 1
            tmm = op("tensor", lambda e, ch=ch, ob=ob, ph=ph, last=last: e.matmul(
                out=ph, lhsT=uo[:, ch * DM + ob * 128:ch * DM + (ob + 1) * 128],
                rhs=cb[ch][:, :], start=False, stop=last),
                tick="t" if last else None)
        hw = [("t", tmm)]
        if ob >= 2:
            hw.append(("v", x_t[ob - 2]))
        tth = op("scalar", lambda e, ob=ob, ph=ph: e.activation(
            out=th32[ob % 2][:, :], in_=ph, func=AF.Tanh,
            bias=auxf[:, AX_BO_F + ob:AX_BO_F + ob + 1]), tick="s", wait=hw)
        h_t[ob] = tth
        x_t[ob] = op("vector", lambda e, ob=ob: e.scalar_tensor_tensor(
            out=xdst[ob], in0=th32[ob % 2][:, :], scalar=0.5, in1=xdst[ob],
            op0=ALU.mult, op1=ALU.add), tick="v", wait=[("s", tth)])
        if ln_hook is not None:
            ln_hook(ob)
    return x_t


def _trig_emit(op, E, src_ps_ap, bias_col, cos_out, sin_out, wait_src):
    th, red, kf, ki = E["trig_th"], E["trig_red"], E["trig_kf"], E["trig_ki"]
    smalls, auxf = E["smalls"], E["auxf"]
    w0 = list(wait_src)
    if E.get("trig_last_s"):
        w0.append(("s", E["trig_last_s"]))
    op("vector", lambda e: e.tensor_copy(out=th[:, :], in_=src_ps_ap), wait=w0)
    op("vector", lambda e: e.tensor_scalar(
        out=red[:, :], in0=th[:, :], scalar1=auxf[:, bias_col:bias_col + 1],
        scalar2=None, op0=ALU.add))
    op("vector", lambda e: e.tensor_scalar(
        out=th[:, :], in0=red[:, :], scalar1=smalls[:, SM_INVW:SM_INVW + 1],
        scalar2=auxf[:, AX_PHASE:AX_PHASE + 1], op0=ALU.mult, op1=ALU.add))
    op("vector", lambda e: e.tensor_scalar(
        out=red[:, :], in0=th[:, :], scalar1=-1.0 / _TPI, scalar2=None,
        op0=ALU.mult))
    op("vector", lambda e: e.tensor_copy(out=ki[:, :], in_=red[:, :]))
    op("vector", lambda e: e.tensor_copy(out=kf[:, :], in_=ki[:, :]))
    op("vector", lambda e: e.scalar_tensor_tensor(
        out=red[:, :], in0=kf[:, :], scalar=TWO_PI_HI, in1=th[:, :],
        op0=ALU.mult, op1=ALU.add))
    op("vector", lambda e: e.scalar_tensor_tensor(
        out=red[:, :], in0=kf[:, :], scalar=TWO_PI_MID, in1=red[:, :],
        op0=ALU.mult, op1=ALU.add))
    op("vector", lambda e: e.scalar_tensor_tensor(
        out=red[:, :], in0=kf[:, :], scalar=TWO_PI_LO, in1=red[:, :],
        op0=ALU.mult, op1=ALU.add))
    tsin = op("vector", lambda e: e.tensor_scalar(
        out=th[:, :], in0=red[:, :], scalar1=PI, scalar2=-PI,
        op0=ALU.min, op1=ALU.max), tick="v")
    tss = op("scalar", lambda e: e.activation(
        out=sin_out, in_=th[:, :], func=AF.Sin,
        bias=E["auxf"][:, AX_ZERO:AX_ZERO + 1]), tick="s", wait=[("v", tsin)])
    op("vector", lambda e: e.tensor_scalar(
        out=kf[:, :], in0=th[:, :], scalar1=HALF_PI, scalar2=None,
        op0=ALU.is_gt), wait=[("s", tss)])
    op("vector", lambda e: e.scalar_tensor_tensor(
        out=red[:, :], in0=kf[:, :], scalar=-TWO_PI_HI, in1=th[:, :],
        op0=ALU.mult, op1=ALU.add))
    op("vector", lambda e: e.tensor_scalar(
        out=kf[:, :], in0=red[:, :], scalar1=HALF_PI, scalar2=PI,
        op0=ALU.add, op1=ALU.min))
    tcos = op("vector", lambda e: e.tensor_scalar(
        out=red[:, :], in0=kf[:, :], scalar1=-PI, scalar2=None, op0=ALU.max),
        tick="v")
    tcs = op("scalar", lambda e: e.activation(
        out=cos_out, in_=red[:, :], func=AF.Sin,
        bias=E["auxf"][:, AX_ZERO:AX_ZERO + 1]), tick="s", wait=[("v", tcos)])
    E["trig_last_s"] = tcs
    return tcs


def _replay(nc, es, prog):
    block = es.enter_context(nc.Block())

    @block.sync
    def _(e):
        for f in prog.p["sync"]:
            f(e)

    @block.gpsimd
    def _(e):
        for f in prog.p["gpsimd"]:
            f(e)

    @block.vector
    def _(e):
        for f in prog.p["vector"]:
            f(e)

    @block.scalar
    def _(e):
        for f in prog.p["scalar"]:
            f(e)

    @block.tensor
    def _(e):
        for f in prog.p["tensor"]:
            f(e)


def _common_sbuf(nc, es):
    sb = lambda n, sh, dt: es.enter_context(nc.sbuf_tensor(n, sh, dt))
    E = {}
    for nm in ("wfT", "wiT", "wcT", "woT", "uoT"):
        E[nm] = sb(nm + "_s", [128, NCH * DM], BF16)
    E["xb"] = [sb(f"xb{c}", [128, TB], BF16) for c in range(NCH)]
    E["yb"] = [sb(f"y{c}", [128, TB], BF16) for c in range(NCH)]
    E["cb"] = [sb(f"c{c}", [128, TB], BF16) for c in range(NCH)]
    E["x2b"] = E["cb"]
    E["ftile"] = sb("ftile", [128, TB], BF16)
    E["itile"] = sb("itile", [128, TB], BF16)
    E["gtile"] = sb("gtile", [128, TB], BF16)
    E["gtmp"] = sb("gtmp", [128, TB], BF16)
    E["th32"] = [sb(f"th32_{i}", [128, TB], F32) for i in range(2)]
    E["m_bc"] = sb("m_bc", [128, TB], BF16)
    E["z_bc"] = sb("z_bc", [128, TB], BF16)
    E["srow"] = sb("srow", [1, 3 * TB], F32)
    E["mrow"] = sb("mrow", [1, TB], F32)
    E["zrow"] = sb("zrow", [1, TB], F32)
    E["auxf"] = sb("auxf_s", [128, NAUXF], F32)
    E["onesb"] = sb("auxb_s", [128, 2], BF16)
    E["onesrow_b"] = sb("onesrow_b", [1, 128], BF16)
    E["mrow_b"] = sb("mrow_b", [1, TB], BF16)
    E["zrow_b"] = sb("zrow_b", [1, TB], BF16)
    return E, sb


def build_launch_a(n_blocks=NB, do_ffn=True, do_ln2=True, do_qkv=True, do_v=True):
    nc = bass.Bass("TRN2", target_bir_lowering=False,
                   detect_race_conditions=False)
    x_h = nc.dram_tensor("x", [DM, TC], F32, kind="ExternalInput").ap()
    w_h = {nm: nc.dram_tensor(nm, [DM, DM], F32, kind="ExternalInput").ap()
           for nm in ("wfT", "wiT", "wcT", "woT", "uoT", "ufT", "uiT", "ucT")}
    wq_h = {nm: nc.dram_tensor(nm, [DM, H], F32, kind="ExternalInput").ap()
            for nm in ("wqT", "wkT", "wvT")}
    auxf_h = nc.dram_tensor("auxf", [128, NAUXF], F32, kind="ExternalInput").ap()
    auxb_h = nc.dram_tensor("auxb", [128, 2], BF16, kind="ExternalInput").ap()
    onesrow_h = nc.dram_tensor("onesrow", [1, 128], BF16,
                               kind="ExternalInput").ap()
    bvbc_h = nc.dram_tensor("bvbc", [128, 128], F32, kind="ExternalInput").ap()
    x1_h = nc.dram_tensor("x1", [DM, TC], F32, kind="ExternalOutput").ap()
    cosq_h = nc.dram_tensor("cosq", [H, TC], BF16, kind="ExternalOutput").ap()
    sinq_h = nc.dram_tensor("sinq", [H, TC], BF16, kind="ExternalOutput").ap()
    cosk_h = nc.dram_tensor("cosk", [H, TC], BF16, kind="ExternalOutput").ap()
    sink_h = nc.dram_tensor("sink", [H, TC], BF16, kind="ExternalOutput").ap()
    vperm_h = nc.dram_tensor("vperm", [128, TC], BF16, kind="ExternalOutput").ap()
    smalls_h = nc.dram_tensor("smalls", [128, NSMALL], F32, kind="ExternalOutput").ap()
    if _DBG_VV:
        vvdump_h = nc.dram_tensor("vvdump", [128, 544], F32,
                                  kind="ExternalOutput").ap()

    es = contextlib.ExitStack()
    with es:
        E, sb = _common_sbuf(nc, es)
        ps_ = lambda n, sh: es.enter_context(nc.psum_tensor(n, sh, F32))
        E["smalls"] = sb("smalls_s", [128, 28], F32)
        wq_s = {nm: [sb(f"{nm}{c}", [128, H], BF16) for c in range(NCH)]
                for nm in ("wqT", "wkT", "wvT")}
        u_stage = [sb(f"ust{i}", [128, DM], F32) for i in range(2)]
        xblk = [[sb(f"x{i}_{c}", [128, TB], F32) for c in range(NCH)]
                for i in range(2)]
        bvbc = sb("bvbc_s", [128, 128], F32)

        E["trig_th"] = sb("trig_th", [128, TB], F32)
        E["trig_red"] = sb("trig_red", [128, TB], F32)
        E["trig_kf"] = sb("trig_kf", [128, TB], F32)
        E["trig_ki"] = sb("trig_ki", [128, TB], I32)
        cosq_s = sb("cosq_sb", [H, TC], BF16)
        sinq_s = sb("sinq_sb", [H, TC], BF16)
        cosk_s = sb("cosk_sb", [H, TC], BF16)
        sink_s = sb("sink_sb", [H, TC], BF16)
        vperm_s = sb("vperm_sb", [128, TC], BF16)

        stats_ps = ps_("stats", [128, 2 * TB])
        bc_ps = ps_("bcst", [128, 2 * TB])
        gA0 = ps_("gA0", [128, TB])
        gB0 = ps_("gB0", [128, TB])
        vv_ps = ps_("vv", [128, TB])
        E["stats_ps"] = stats_ps
        E["bc_ps"] = bc_ps
        E["gsets"] = ((gA0[:, :], stats_ps[:, 0:TB], stats_ps[:, TB:2 * TB]),
                      (gB0[:, :], bc_ps[:, 0:TB], bc_ps[:, TB:2 * TB]))
        E["hsets"] = (gA0[:, :], gB0[:, :])
        qk_ps = bc_ps
        v_ps = stats_ps

        sems = {k: es.enter_context(nc.semaphore(f"sem_{k}"))
                for k in ("lda", "lx0", "lx1", "lu0", "lu1", "lw1", "lw2",
                          "lw3", "st0", "st1", "stf", "v", "s", "t")}
        prog = Prog()
        dma, op = _mk(prog, sems)

        dma("sync", "lda", E["auxf"][:, :], auxf_h)
        dma("sync", "lda", E["onesb"][:, :], auxb_h)
        dma("sync", "lda", E["onesrow_b"][:, :], onesrow_h)
        dma("sync", "lda", bvbc[:, :], bvbc_h)
        AUX = [("lda", 64)]
        xld = {}
        for ch in range(NCH):
            xld[(0, ch)] = dma("sync", "lx0", xblk[0][ch][:, :],
                               x_h[ch * 128:(ch + 1) * 128, 0:TB])
        u_t = {}

        def emit_uchain():
            tick_last = {}
            for idx in range(24):
                mi, ic = divmod(idx, NCH)
                buf = u_stage[idx % 2]
                lus = f"lu{idx % 2}"
                w = [("t", tick_last[idx - 2])] if idx >= 2 else None
                lv = dma("sync", lus, buf[:, :],
                         w_h[("ufT", "uiT", "ucT")[mi]][ic * 128:(ic + 1) * 128, :],
                         wait=w)
                for oc in range(NCH):
                    last = oc == NCH - 1
                    col = ((mi * 8 + oc) * 8) + ic
                    tick_last[idx] = op(
                        "tensor", lambda e, buf=buf, oc=oc, ic=ic, col=col: e.matmul(
                            out=vv_ps[:, col:col + 1],
                            lhsT=buf[:, oc * 128:(oc + 1) * 128],
                            rhs=E["auxf"][:, AX_INC + ic:AX_INC + ic + 1],
                            start=True, stop=True, skip_group_check=True),
                        tick="t" if last else None,
                        wait=([(lus, lv)] + AUX) if oc == 0 else None)
            tt = E["trig_th"]
            op("vector", lambda e: e.tensor_copy(
                out=tt[:, 0:192], in_=vv_ps[:, 0:192]),
                wait=[("t", tick_last[23])])
            t192 = tt[:, 0:192].rearrange("p (c two) -> p c two", two=2)
            op("vector", lambda e: e.tensor_add(
                out=tt[:, 192:288], in0=t192[:, :, 0], in1=t192[:, :, 1]))
            t96 = tt[:, 192:288].rearrange("p (c two) -> p c two", two=2)
            op("vector", lambda e: e.tensor_copy(
                out=E["trig_red"][:, 64:120], in_=E["auxf"][:, 0:56]))
            op("vector", lambda e: e.tensor_add(
                out=tt[:, 288:336], in0=t96[:, :, 0], in1=t96[:, :, 1]))
            t48 = tt[:, 288:336].rearrange("p (c two) -> p c two", two=2)
            op("vector", lambda e: e.tensor_copy(
                out=E["trig_red"][:, 64:120], in_=E["auxf"][:, 0:56]))
            op("vector", lambda e: e.tensor_copy(
                out=E["trig_red"][:, 64:120], in_=E["auxf"][:, 0:56]))
            t_ev = op("vector", lambda e: e.tensor_add(
                out=E["smalls"][:, 0:24], in0=t48[:, :, 0], in1=t48[:, :, 1]),
                tick="v")
            op("vector", lambda e: e.tensor_copy(
                out=E["trig_red"][:, 64:120], in_=E["auxf"][:, 0:56]))
            op("vector", lambda e: e.tensor_copy(
                out=E["trig_red"][:, 64:120], in_=E["auxf"][:, 0:56]))
            for sm0, ax0 in ((SM_GBF, AX_BF), (SM_GBI, AX_BI), (SM_GBC, AX_BC)):
                op("vector", lambda e, sm0=sm0, ax0=ax0: e.tensor_add(
                    out=E["smalls"][:, sm0:sm0 + 8], in0=E["smalls"][:, sm0:sm0 + 8],
                    in1=E["auxf"][:, ax0:ax0 + 8]))
            op("vector", lambda e: e.tensor_scalar(
                out=E["trig_kf"][:, 0:8], in0=E["auxf"][:, AX_WAV:AX_WAV + 8],
                scalar1=1e-3, scalar2=1e-8, op0=ALU.max, op1=ALU.add))
            op("vector", lambda e: e.tensor_copy(
                out=E["trig_th"][:, 0:56], in_=E["auxf"][:, 0:56]))
            op("vector", lambda e: e.tensor_copy(
                out=E["trig_th"][:, 0:56], in_=E["auxf"][:, 0:56]))
            op("vector", lambda e: e.reciprocal(
                out=E["trig_red"][:, 0:8], in_=E["trig_kf"][:, 0:8]))
            op("vector", lambda e: e.tensor_copy(
                out=E["trig_th"][:, 0:56], in_=E["auxf"][:, 0:56]))
            op("vector", lambda e: e.tensor_copy(
                out=E["trig_th"][:, 0:56], in_=E["auxf"][:, 0:56]))
            t_smalls = op("vector", lambda e: e.tensor_copy(
                out=E["smalls"][:, SM_INVW:SM_INVW + 1], in_=E["trig_red"][:, 0:1]),
                tick="v")
            E["act_extra_wait"] = [("v", t_smalls)]
            E["t_smalls_v"] = t_smalls
            if _DBG_VV:
                dma("sync", "stf", vvdump_h[:, 192:528],
                    E["trig_th"][:, 0:336], wait=[("v", t_ev)])
                dma("sync", "stf", vvdump_h[:, 0:28],
                    E["smalls"][:, 0:28], wait=[("v", t_smalls)])
                dma("sync", "stf", vvdump_h[:, 28:84],
                    E["auxf"][:, 0:56], wait=[("v", t_smalls)])

        for nm in ("wfT", "wiT", "wcT"):
            dma("gpsimd", "lw1", E[nm][:, :].rearrange("p (c i) -> p c i", c=NCH),
                w_h[nm].rearrange("(c p) i -> p c i", p=128))
        for nm in ("woT", "uoT"):
            dma("gpsimd", "lw2", E[nm][:, :].rearrange("p (c i) -> p c i", c=NCH),
                w_h[nm].rearrange("(c p) i -> p c i", p=128))
        for nm in ("wqT", "wkT", "wvT"):
            for ch in range(NCH):
                dma("gpsimd", "lw3", wq_s[nm][ch][:, :],
                    wq_h[nm][ch * 128:(ch + 1) * 128, :])
        E["h_weight_wait"] = [("lw2", 2 * 16)]
        gate_w_wait = [("lw1", 3 * 16)]
        qk_w_wait = [("lw3", 24 * 16)]

        prev = {}
        for b in range(n_blocks):
            cur = xblk[b % 2]
            c0 = b * TB
            lxs = f"lx{b % 2}"
            sts = f"st{b % 2}"
            if b > 0:
                for ch in range(NCH):
                    w = None
                    if ch == 0 and b >= 2:
                        w = [("v", prev[(b - 2, "y2")]), (sts, prev[(b - 2, "xst")])]
                    xld[(b, ch)] = dma("sync", lxs, cur[ch][:, :],
                                       x_h[ch * 128:(ch + 1) * 128, c0:c0 + TB],
                                       wait=w)
            xsrc = [cur[ch][:, :] for ch in range(NCH)]
            wf = [(lxs, xld[(b, NCH - 1)])] + AUX
            if b > 0:
                wf.append(("t", prev[(b - 1, "stats2")]))
            y_t, _ = _ln_emit(op, E, xsrc, wf)
            if b == 0:
                emit_uchain()
            if not do_ffn:
                prev[(b, "stats2")] = prog.ctr.get("t", 0)
                prev[(b, "y2")] = prog.ctr.get("v", 0)
                prev[(b, "xst")] = prog.ctr.get(f"st{b % 2}", 0)
                prev[(b, "trig_v")] = prog.ctr.get("v", 0)
                prev[(b, "vev_last")] = prog.ctr.get("v", 0)
                for sub in range(TB // 128):
                    prev[(b, f"vev{sub}")] = prog.ctr.get("v", 0)
                continue
            hook2, fin2 = (_ln_hoist_mk(op, E, xsrc) if do_ln2
                           else (None, None))
            x_t = _ffn_emit(op, E, y_t, xsrc, gate_w_wait if b == 0 else None,
                            ln_hook=hook2)
            for ch in range(NCH):
                v = dma("gpsimd", sts,
                        x1_h[ch * 128:(ch + 1) * 128, c0:c0 + TB], cur[ch][:, :],
                        wait=[("v", x_t[ch])])
            prev[(b, "xst")] = v
            if not do_ln2:
                prev[(b, "stats2")] = prog.ctr.get("t", 0)
                prev[(b, "y2")] = prog.ctr.get("v", 0)
                prev[(b, "trig_v")] = prog.ctr.get("v", 0)
                prev[(b, "vev_last")] = prog.ctr.get("v", 0)
                for sub in range(TB // 128):
                    prev[(b, f"vev{sub}")] = prog.ctr.get("v", 0)
                continue
            t_stats2 = fin2()
            y2_t = _ln_finish(op, E, t_stats2)
            prev[(b, "stats2")] = t_stats2
            prev[(b, "y2")] = y2_t[NCH - 1]
            if not do_qkv:
                prev[(b, "trig_v")] = prog.ctr.get("v", 0)
                prev[(b, "vev_last")] = prog.ctr.get("v", 0)
                for sub in range(TB // 128):
                    prev[(b, f"vev{sub}")] = prog.ctr.get("v", 0)
                continue
            for (nm, pcols, cs, sn, bcol) in (
                    ("wqT", (0, TB), cosq_s, sinq_s, AX_BQ),
                    ("wkT", (TB, 2 * TB), cosk_s, sink_s, AX_BK)):
                for ch in range(NCH):
                    w = [("v", y2_t[ch])]
                    if ch == 0:
                        w += qk_w_wait
                        if b > 0:
                            w.append(("v", prev[(b - 1, "trig_v")]))
                    last = ch == NCH - 1
                    tq = op("tensor", lambda e, ch=ch, nm=nm, pcols=pcols,
                            last=last: e.matmul(
                        out=qk_ps[:, pcols[0]:pcols[1]], lhsT=wq_s[nm][ch][:, :],
                        rhs=E["yb"][ch][:, :], start=(ch == 0), stop=last),
                        tick="t" if last else None, wait=w)
                _trig_emit(op, E, qk_ps[:, pcols[0]:pcols[1]], bcol,
                           cs[:, c0:c0 + TB], sn[:, c0:c0 + TB],
                           [("t", tq), ("v", E["t_smalls_v"])])
            prev[(b, "trig_v")] = prog.ctr.get("v", 0)
            if not do_v:
                prev[(b, "vev_last")] = prog.ctr.get("v", 0)
                for sub in range(TB // 128):
                    prev[(b, f"vev{sub}")] = prog.ctr.get("v", 0)
                continue
            for sub in range(TB // 128):
                pc = (sub % 2) * 512 + (sub // 2) * 128
                tk0 = c0 + sub * 128
                for ch in range(NCH):
                    w = None
                    if ch == 0:
                        w = []
                        if sub == 0 and b > 0:
                            w.append(("v", prev[(b - 1, "vev_last")]))
                        w = w or None
                    last = ch == NCH - 1
                    tvm = op("tensor", lambda e, ch=ch, pc=pc, sub=sub,
                             last=last: e.matmul(
                        out=v_ps[:, pc:pc + 128],
                        lhsT=E["yb"][ch][:, sub * 128:(sub + 1) * 128],
                        rhs=wq_s["wvT"][ch][:, :], start=(ch == 0), stop=last),
                        tick="t" if last else None, wait=w)
                t = op("vector", lambda e, pc=pc, tk0=tk0: e.tensor_add(
                    out=vperm_s[:, tk0:tk0 + 128], in0=v_ps[:, pc:pc + 128],
                    in1=bvbc[:, :]), tick="v", wait=[("t", tvm)])
                prev[(b, f"vev{sub}")] = t
            prev[(b, "vev_last")] = t

        t_end_v = prog.ctr.get("v", 0)
        t_end_s = prog.ctr.get("s", 0)
        for hb, sbl in ((cosq_h, cosq_s), (sinq_h, sinq_s), (cosk_h, cosk_s),
                        (sink_h, sink_s), (vperm_h, vperm_s)):
            dma("gpsimd", "stf", hb, sbl[:, :],
                wait=[("v", t_end_v), ("s", t_end_s)])
        dma("gpsimd", "stf", smalls_h, E["smalls"][:, 0:NSMALL])

        def _fin(e):
            for cn in ("st0", "st1", "stf"):
                e.wait_ge(sems[cn], prog.ctr.get(cn, 0))
        prog.emit("gpsimd", _fin)

        _replay(nc, es, prog)
    return nc


def build_launch_b():
    nc = bass.Bass("TRN2", target_bir_lowering=False,
                   detect_race_conditions=False)
    x1_h = nc.dram_tensor("x1", [DM, TC], F32, kind="ExternalInput").ap()
    cosq_h = nc.dram_tensor("cosq", [H, TC], BF16, kind="ExternalInput").ap()
    sinq_h = nc.dram_tensor("sinq", [H, TC], BF16, kind="ExternalInput").ap()
    coskF_h = nc.dram_tensor("coskF", [H, SEQ], BF16, kind="ExternalInput").ap()
    sinkF_h = nc.dram_tensor("sinkF", [H, SEQ], BF16, kind="ExternalInput").ap()
    vpermF_h = nc.dram_tensor("vpermF", [128, SEQ], BF16, kind="ExternalInput").ap()
    w_h = {nm: nc.dram_tensor(nm, [DM, DM], F32, kind="ExternalInput").ap()
           for nm in ("wfT", "wiT", "wcT", "woT", "uoT")}
    waoT_h = nc.dram_tensor("waoT", [H, DM], F32, kind="ExternalInput").ap()
    auxf_h = nc.dram_tensor("auxf", [128, NAUXF], F32, kind="ExternalInput").ap()
    auxb_h = nc.dram_tensor("auxb", [128, 2], BF16, kind="ExternalInput").ap()
    onesrow_h = nc.dram_tensor("onesrow", [1, 128], BF16,
                               kind="ExternalInput").ap()
    ones512_h = nc.dram_tensor("ones512", [1, 512], BF16,
                               kind="ExternalInput").ap()
    boa_h = nc.dram_tensor("boaRow", [1, DM], F32, kind="ExternalInput").ap()
    smalls_h = nc.dram_tensor("smalls", [128, NSMALL], F32, kind="ExternalInput").ap()
    out_h = nc.dram_tensor("out", [DM, TC], F32, kind="ExternalOutput").ap()

    es = contextlib.ExitStack()
    with es:
        E, sb = _common_sbuf(nc, es)
        ps_ = lambda n, sh: es.enter_context(nc.psum_tensor(n, sh, F32))
        E["smalls"] = sb("smalls_s", [128, NSMALL], F32)
        waoT = sb("wao_s", [H, DM], BF16)
        cosq = sb("cosq_sb", [H, TC], BF16)
        sinq = sb("sinq_sb", [H, TC], BF16)
        kbuf = [dict(ck=sb(f"ck{i}", [H, KG], BF16), sk=sb(f"sk{i}", [H, KG], BF16),
                     vp=sb(f"vp{i}", [128, KG], BF16)) for i in range(2)]
        x1s = [[sb(f"x1_0_{c}", [128, SW], F32) for c in range(NCH)]]
        ptiles = [sb(f"ptile{i}", [128, SW], BF16) for i in range(3)]
        pairs = [sb(f"pair{i}", [128, SW], BF16) for i in range(2)]
        quad = sb("quad", [128, SW], BF16)
        boa = sb("boa_s", [1, DM], F32)
        ones512_b = sb("ones512_b", [1, 512], BF16)
        boa_hi = sb("boa_hi", [1, DM], BF16)
        boa_lo = sb("boa_lo", [1, DM], BF16)
        dhi_b = pairs[0][0:1, :]
        dlo_b = pairs[1][0:1, :]
        out_sb = sb("out_sb", [H, SW], F32)
        rproj = sb("rproj", [H, SW], BF16)
        drow = sb("drow", [1, SW], F32)

        sA = ps_("sA", [128, SW])
        sB = ps_("sB", [128, SW])
        out_ps = ps_("out_ps", [128, SW])
        den_ps = ps_("den_ps", [128, SW])
        E["stats_ps"] = sA
        E["bc_ps"] = sB
        E["gsets"] = ((sA[:, 0:TB], sA[:, TB:2 * TB], sB[:, 0:TB]),
                      (sB[:, TB:2 * TB], out_ps[:, 0:TB], out_ps[:, TB:2 * TB]))
        E["hsets"] = (den_ps[:, 0:TB], den_ps[:, TB:2 * TB])

        sems = {k: es.enter_context(nc.semaphore(f"sem_{k}"))
                for k in ["lda", "lx0", "lk0", "lk1", "lw1", "lw2", "lw3",
                          "v", "s", "t"] + [f"sc{c}" for c in range(NCH)]}
        prog = Prog()
        dma, op = _mk(prog, sems)

        dma("sync", "lda", E["auxf"][:, :], auxf_h)
        dma("sync", "lda", E["onesb"][:, :], auxb_h)
        dma("sync", "lda", E["onesrow_b"][:, :], onesrow_h)
        dma("sync", "lda", E["smalls"][:, :], smalls_h)
        dma("sync", "lda", cosq[:, :], cosq_h)
        dma("sync", "lda", sinq[:, :], sinq_h)
        dma("sync", "lda", ones512_b[:, :], ones512_h)
        dma("sync", "lda", boa[:, :], boa_h)
        AUX = [("lda", 128)]
        x1ld = {}
        for ch in range(NCH):
            x1ld[(0, ch)] = dma("sync", "lx0", x1s[0][ch][:, :],
                                x1_h[ch * 128:(ch + 1) * 128, 0:SW])
        for nm in ("wfT", "wiT", "wcT"):
            dma("gpsimd", "lw1", E[nm][:, :].rearrange("p (c i) -> p c i", c=NCH),
                w_h[nm].rearrange("(c p) i -> p c i", p=128))
        for nm in ("woT", "uoT"):
            dma("gpsimd", "lw2", E[nm][:, :].rearrange("p (c i) -> p c i", c=NCH),
                w_h[nm].rearrange("(c p) i -> p c i", p=128))
        dma("gpsimd", "lw3", waoT[:, :], waoT_h)
        E["h_weight_wait"] = [("lw2", 2 * 16)]
        gate_w_wait = [("lw1", 3 * 16)] + AUX

        op("vector", lambda e: e.tensor_copy(
            out=boa_hi[0:1, :], in_=boa[0:1, :]), wait=AUX)
        op("vector", lambda e: e.tensor_copy(
            out=quad[:, 0:56], in_=E["auxf"][:, 0:56]))
        op("vector", lambda e: e.tensor_copy(
            out=quad[:, 0:56], in_=E["auxf"][:, 0:56]))
        op("vector", lambda e: e.tensor_copy(
            out=drow[0:1, :], in_=boa_hi[0:1, :]))
        op("vector", lambda e: e.tensor_copy(
            out=quad[:, 0:56], in_=E["auxf"][:, 0:56]))
        t_bo = op("vector", lambda e: e.tensor_sub(
            out=boa_lo[0:1, :], in0=boa[0:1, :], in1=drow[0:1, :]), tick="v")
        bo_wait = [[("v", t_bo)]]

        NG = NTILE // 4
        kc = {}
        prev = {}
        E["trig_th"] = None
        for sw in range(NSW):
            cur = x1s[0]
            q0 = sw * SW
            if sw > 0:
                for ch in range(NCH):
                    w = [(f"sc{ch}", 2 * 16 * sw)]
                    x1ld[(sw, ch)] = dma("sync", "lx0", cur[ch][:, :],
                                         x1_h[ch * 128:(ch + 1) * 128, q0:q0 + SW],
                                         wait=w)
            xp_t = {}
            for ch in range(NCH):
                xp_t[ch] = prog.ctr.get("v", 0)
            t_x1rdy = [("lx0", x1ld[(sw, NCH - 1)])] + AUX
            sweep_bar = [("v", prog.ctr.get("v", 0)), ("s", prog.ctr.get("s", 0))]
            exp_t = {}
            den_t = {}
            tile_meta = {}

            def emit_kload(g, wait):
                buf = kbuf[g % 2]
                gg = sw * NG + g
                k0 = g * KG
                lks = f"lk{gg % 2}"
                dma("sync", lks, buf["ck"][:, :], coskF_h[:, k0:k0 + KG], wait=wait)
                dma("sync", lks, buf["sk"][:, :], sinkF_h[:, k0:k0 + KG])
                lvp = dma("sync", lks, buf["vp"][:, :], vpermF_h[:, k0:k0 + KG])
                for ti in range(4):
                    tile_meta[g * 4 + ti] = (g * 4 + ti, buf, lks, lvp, ti)

            emit_kload(0, None if sw == 0 else [("t", kc[sw * NG - 2])])
            emit_kload(1, None if sw == 0 else [("t", kc[sw * NG - 1])])

            def emit_S(tile, buf, lks, lvp):
                Sp = sA if tile % 2 == 0 else sB
                c0, c1 = (tile % 4) * 128, (tile % 4 + 1) * 128
                w = [(lks, lvp)]
                if tile == 0:
                    w.append(("lw3", 16))
                    w += sweep_bar
                if tile >= 2:
                    w.append(("s", exp_t[tile - 2]))
                tS = None
                for half in range(2):
                    hc0 = half * 512
                    op("tensor", lambda e, buf=buf, c0=c0, c1=c1, hc0=hc0,
                       Sp=Sp: e.matmul(
                           out=Sp[:, hc0:hc0 + 512], lhsT=buf["ck"][:, c0:c1],
                           rhs=cosq[:, q0 + hc0:q0 + hc0 + 512],
                           start=True, stop=False),
                       wait=w if half == 0 else None)
                    last = half == 1
                    tS = op("tensor", lambda e, buf=buf, c0=c0, c1=c1, hc0=hc0,
                            Sp=Sp: e.matmul(
                                out=Sp[:, hc0:hc0 + 512], lhsT=buf["sk"][:, c0:c1],
                                rhs=sinq[:, q0 + hc0:q0 + hc0 + 512],
                                start=False, stop=True),
                            tick="t" if last else None)
                return tS, Sp

            pair_t = {}
            quad_t = {}

            def emit_exp(tile, tS, Sp):
                w = [("t", tS)]
                if tile >= 3:
                    w.append(("t", den_t[tile - 3]))
                    w.append(("v", pair_t[tile - 3]))
                pt = ptiles[tile % 3]
                exp_t[tile] = op("scalar", lambda e, Sp=Sp, pt=pt: e.activation(
                    out=pt[:, :], in_=Sp[:, :], func=AF.Exp,
                    bias=E["auxf"][:, AX_ZERO:AX_ZERO + 1],
                    scale=E["auxf"][:, AX_ISQ:AX_ISQ + 1]), tick="s", wait=w)

            def emit_pv(tile, buf):
                c0, c1 = (tile % 4) * 128, (tile % 4 + 1) * 128
                pt = ptiles[tile % 3]
                tden = None
                for half in range(2):
                    hc0 = half * 512
                    w2 = [("s", exp_t[tile])] if half == 0 else None
                    last = half == 1
                    tden = op("tensor", lambda e, buf=buf, c0=c0, c1=c1, hc0=hc0,
                              tile=tile, pt=pt: e.matmul(
                                  out=out_ps[:, hc0:hc0 + 512],
                                  lhsT=buf["vp"][:, c0:c1],
                                  rhs=pt[:, hc0:hc0 + 512], start=(tile == 0),
                                  stop=(tile == NTILE - 1)),
                              tick="t" if last else None, wait=w2)
                if tile % 2 == 1:
                    pair_t[tile] = pair_t[tile - 1] = op(
                        "vector", lambda e, tile=tile: e.tensor_add(
                            out=pairs[(tile // 2) % 2][:, :],
                            in0=ptiles[(tile - 1) % 3][:, :],
                            in1=ptiles[tile % 3][:, :]),
                        tick="v", wait=[("s", exp_t[tile])])
                else:
                    pair_t.setdefault(tile, prog.ctr.get("v", 0))
                if tile % 4 == 3:
                    tq4 = op("vector", lambda e: e.tensor_add(
                        out=quad[:, :], in0=pairs[0][:, :], in1=pairs[1][:, :]),
                        tick="v",
                        wait=[("t", quad_t[tile - 4])] if tile >= 7 else None)
                    for half in range(2):
                        hc0 = half * 512
                        last = half == 1
                        tden = op("tensor", lambda e, hc0=hc0, tile=tile: e.matmul(
                            out=den_ps[0:1, hc0:hc0 + 512],
                            lhsT=E["onesb"][:, 0:1], rhs=quad[:, hc0:hc0 + 512],
                            start=(tile == 3), stop=(tile == NTILE - 1)),
                            tick="t" if last else None,
                            wait=[("v", tq4)] if half == 0 else None)
                        quad_t[tile] = tden
                den_t[tile] = tden
                return tden

            pend = {}
            tile, buf, lks, lvp, _ = tile_meta[0]
            pend[0] = emit_S(tile, buf, lks, lvp)

            def do_pv(t):
                tden = emit_pv(t, tile_meta[t][1])
                if tile_meta[t][4] == 3:
                    g = tile_meta[t][0] // 4
                    kc[sw * NG + g] = tden
                    if g + 2 < NG:
                        emit_kload(g + 2, [("t", kc[sw * NG + g])])
                return tden

            for t in range(NTILE):
                emit_exp(t, *pend.pop(t))
                if t + 1 < NTILE:
                    tile, buf, lks, lvp, _ = tile_meta[t + 1]
                    pend[t + 1] = emit_S(tile, buf, lks, lvp)
                if t >= 1:
                    tden = do_pv(t - 1)
            tden = do_pv(NTILE - 1)
            t_att = tden
            op("vector", lambda e: e.tensor_copy(
                out=E["srow"][0:1, 0:SW], in_=den_ps[0:1, :]),
               wait=[("t", t_att)])
            op("vector", lambda e: e.tensor_copy(
                out=out_sb[0:1, :], in_=E["srow"][0:1, 0:SW]))
            op("vector", lambda e: e.tensor_copy(
                out=rproj[0:1, :], in_=E["srow"][0:1, 0:SW]))
            op("vector", lambda e: e.reciprocal(
                out=drow[0:1, :], in_=E["srow"][0:1, 0:SW]))
            op("vector", lambda e: e.tensor_copy(
                out=out_sb[0:1, :], in_=E["srow"][0:1, 0:SW]))
            op("vector", lambda e: e.tensor_copy(
                out=rproj[0:1, :], in_=E["srow"][0:1, 0:SW]))
            op("vector", lambda e: e.tensor_copy(
                out=dhi_b[0:1, :], in_=drow[0:1, :]))
            tdr = op("vector", lambda e: e.tensor_copy(out=out_sb[:, :],
                                                       in_=out_ps[:, :]), tick="v")
            op("vector", lambda e: e.tensor_copy(
                out=E["srow"][0:1, 0:SW], in_=dhi_b[0:1, :]))
            op("vector", lambda e: e.tensor_copy(
                out=quad[:, 0:56], in_=E["auxf"][:, 0:56]))
            op("vector", lambda e: e.tensor_copy(
                out=quad[:, 0:56], in_=E["auxf"][:, 0:56]))
            tden2 = op("vector", lambda e: e.tensor_sub(
                out=dlo_b[0:1, :], in0=drow[0:1, :],
                in1=E["srow"][0:1, 0:SW]), tick="v")
            for half in range(2):
                hc0 = half * 512
                op("tensor", lambda e, hc0=hc0: e.matmul(
                    out=sA[:, hc0:hc0 + 512], lhsT=E["onesrow_b"][0:1, :],
                    rhs=dhi_b[0:1, hc0:hc0 + 512], start=True, stop=False),
                    wait=[("v", tden2)] if half == 0 else None)
                tb = op("tensor", lambda e, hc0=hc0: e.matmul(
                    out=sA[:, hc0:hc0 + 512], lhsT=E["onesrow_b"][0:1, :],
                    rhs=dlo_b[0:1, hc0:hc0 + 512], start=False, stop=True),
                    tick="t" if half == 1 else None)
            trp = op("vector", lambda e: e.tensor_mul(
                out=rproj[:, :], in0=out_sb[:, :], in1=sA[:, :]), tick="v",
                wait=[("t", tb)])
            pr_t = {}
            for ch in range(NCH):
                pb = sB if ch % 2 == 0 else den_ps
                if ch >= 2:
                    op("tensor", lambda e: e.nop(), wait=[("v", pr_t[ch - 2])])
                for half in range(2):
                    hc0 = half * 512
                    w3 = [("v", trp)] if (ch == 0 and half == 0) else None
                    op("tensor", lambda e, ch=ch, hc0=hc0, pb=pb: e.matmul(
                        out=pb[:, hc0:hc0 + 512],
                        lhsT=waoT[:, ch * 128:(ch + 1) * 128],
                        rhs=rproj[:, hc0:hc0 + 512], start=True, stop=False),
                        wait=w3)
                    last = half == 1
                    op("tensor", lambda e, ch=ch, hc0=hc0, pb=pb: e.matmul(
                        out=pb[:, hc0:hc0 + 512],
                        lhsT=boa_hi[0:1, ch * 128:(ch + 1) * 128],
                        rhs=ones512_b[0:1, :], start=False, stop=False),
                        wait=bo_wait.pop() if bo_wait else None)
                    tpm = op("tensor", lambda e, ch=ch, hc0=hc0, pb=pb,
                             last=last: e.matmul(
                        out=pb[:, hc0:hc0 + 512],
                        lhsT=boa_lo[0:1, ch * 128:(ch + 1) * 128],
                        rhs=ones512_b[0:1, :], start=False, stop=True),
                        tick="t" if last else None)
                pr_t[ch] = op("vector", lambda e, ch=ch, cur=cur, pb=pb: e.tensor_add(
                    out=cur[ch][:, :], in0=cur[ch][:, :], in1=pb[:, :]), tick="v",
                    wait=[("t", tpm)] + t_x1rdy)
            stats_prev = None
            hoist = {}
            for bb in range(SW // TB):
                xsrc = [cur[ch][:, bb * TB:(bb + 1) * TB] for ch in range(NCH)]
                if bb == 0:
                    wfst = [("v", pr_t[NCH - 1]), ("t", prog.ctr.get("t", 0))]
                    if stats_prev is not None:
                        wfst.append(("t", stats_prev))
                    y_t, stats_prev = _ln_emit(op, E, xsrc, wfst)
                    xsrc1 = [cur[ch][:, TB:2 * TB] for ch in range(NCH)]
                    hoist["hook"], hoist["fin"] = _ln_hoist_mk(op, E, xsrc1)
                else:
                    stats_prev = hoist["fin"]()
                    y_t = _ln_finish(op, E, stats_prev)
                x_t = _ffn_emit(op, E, y_t, xsrc,
                                gate_w_wait if (sw == 0 and bb == 0) else None,
                                ln_hook=hoist["hook"] if bb == 0 else None)
                for ch in range(NCH):
                    dma("gpsimd", f"sc{ch}",
                        out_h[ch * 128:(ch + 1) * 128,
                              q0 + bb * TB:q0 + (bb + 1) * TB],
                        xsrc[ch], wait=[("v", x_t[ch])])

        def _fin(e):
            for cn in [f"sc{c}" for c in range(NCH)]:
                e.wait_ge(sems[cn], prog.ctr.get(cn, 0))
        prog.emit("gpsimd", _fin)
        _replay(nc, es, prog)
    return nc


def _prep_host(inputs):
    g = lambda k: np.ascontiguousarray(np.asarray(inputs[k], dtype=np.float32))
    auxf = np.zeros((128, NAUXF), np.float32)
    auxf[:, AX_WAV] = g("L0_attn_wavelengths")
    auxf[:, AX_PHASE] = g("L0_attn_phase_bias")
    auxf[:, AX_BQ] = g("L0_attn_bq")
    auxf[:, AX_BK] = g("L0_attn_bk")
    auxf[:, AX_BF:AX_BF + 8] = g("L0_ffn_bf").reshape(8, 128).T
    auxf[:, AX_BI:AX_BI + 8] = g("L0_ffn_bi").reshape(8, 128).T
    auxf[:, AX_BC:AX_BC + 8] = g("L0_ffn_bc").reshape(8, 128).T
    auxf[:, AX_BO_F:AX_BO_F + 8] = g("L0_ffn_bo").reshape(8, 128).T
    auxf[:, AX_INC:AX_INC + 8] = g("L0_ffn_in_c").reshape(8, 128).T
    auxf[:, AX_BO_A:AX_BO_A + 8] = g("L0_attn_bo").reshape(8, 128).T
    auxf[:, AX_ISQ] = 1.0 / math.sqrt(H)
    auxf[:, AX_EPS] = 1e-5
    auxf[:, AX_ZERO] = 0.0
    auxb = np.zeros((128, 2), BF)
    auxb[:, 0] = 1.0
    onesrow = np.ones((1, 128), BF)
    bvbc = np.ascontiguousarray(np.broadcast_to(g("L0_attn_bv")[None, :], (128, 128)))
    T = lambda a: np.ascontiguousarray(a.T)
    com = {"wfT": T(g("L0_ffn_wf")), "wiT": T(g("L0_ffn_wi")),
           "wcT": T(g("L0_ffn_wc")), "woT": T(g("L0_ffn_wo")),
           "uoT": T(g("L0_ffn_uo")), "auxf": auxf, "auxb": auxb,
           "onesrow": onesrow}
    a_only = {"ufT": T(g("L0_ffn_uf")), "uiT": T(g("L0_ffn_ui")),
              "ucT": T(g("L0_ffn_uc")),
              "wqT": T(g("L0_attn_Wq")), "wkT": T(g("L0_attn_Wk")),
              "wvT": T(g("L0_attn_Wv")), "bvbc": bvbc}
    b_only = {"waoT": T(g("L0_attn_Wo")),
              "ones512": np.ones((1, 512), BF),
              "boaRow": np.ascontiguousarray(g("L0_attn_bo")[None, :])}
    xT = np.ascontiguousarray(g("L0_attn_x").reshape(SEQ, DM).T)
    return com, a_only, b_only, xT


_NC_CACHE = {}


def kernel(**inputs):
    com, a_only, b_only, xT = _prep_host(inputs)
    if "a" not in _NC_CACHE:
        _NC_CACHE["a"] = build_launch_a()
        _NC_CACHE["b"] = build_launch_b()
    nc_a, nc_b = _NC_CACHE["a"], _NC_CACHE["b"]

    in_maps_a = []
    for c in range(NCORES):
        m = dict(com)
        m.update(a_only)
        m["x"] = np.ascontiguousarray(xT[:, c * TC:(c + 1) * TC])
        in_maps_a.append(m)
    res_a = run_bass_kernel_spmd(nc_a, in_maps_a, core_ids=list(range(NCORES)))

    coskF = np.concatenate([res_a.results[c]["cosk"] for c in range(NCORES)], axis=1)
    sinkF = np.concatenate([res_a.results[c]["sink"] for c in range(NCORES)], axis=1)
    vpermF = np.concatenate([res_a.results[c]["vperm"] for c in range(NCORES)], axis=1)

    in_maps_b = []
    for c in range(NCORES):
        m = dict(com)
        m.update(b_only)
        m["x1"] = res_a.results[c]["x1"]
        m["cosq"] = res_a.results[c]["cosq"]
        m["sinq"] = res_a.results[c]["sinq"]
        m["coskF"] = coskF
        m["sinkF"] = sinkF
        m["vpermF"] = vpermF
        m["smalls"] = res_a.results[c]["smalls"]
        in_maps_b.append(m)
    res_b = run_bass_kernel_spmd(nc_b, in_maps_b, core_ids=list(range(NCORES)))

    out = np.concatenate([res_b.results[c]["out"] for c in range(NCORES)], axis=1)
    return np.ascontiguousarray(out.T).reshape(1, SEQ, DM)
